# version 11 (final)
# baseline (speedup 1.0000x reference)
import contextlib
import numpy as np
import ml_dtypes
import concourse.bass as bass
import concourse.mybir as mybir
from concourse.bass_utils import run_bass_kernel_spmd

F32 = mybir.dt.float32
BF16 = mybir.dt.bfloat16
AF = mybir.ActivationFunctionType
ALU = mybir.AluOpType
AX = mybir.AxisListType

D = 1024
NH = 16
HD = 64
DR = 32
DKV = 256
EPS = 1e-5
BIG = 1.0e30

ENGINES = ("tensor", "vector", "scalar", "gpsimd", "sync")


class Task:
    __slots__ = ("eng", "tag", "deps", "emit", "dma", "idx")

    def __init__(self, eng, tag, deps, emit, dma):
        self.eng, self.tag, self.deps, self.emit, self.dma = eng, tag, deps, emit, dma
        self.idx = None


NDMASEM = 8


class Prog:

    def __init__(self, nc, es, name="P"):
        self.nc = nc
        self.name = name
        self.tasks = {}
        self.order = {e: [] for e in ENGINES}
        self.sems = {}
        self.cnt = {}
        for e in ENGINES:
            self.sems[e] = es.enter_context(nc.semaphore(f"{name}_{e}"))
            self.cnt[e] = 0
        for q in ("g", "s"):
            for i in range(NDMASEM):
                self.sems[f"{q}{i}"] = es.enter_context(nc.semaphore(f"{name}_{q}dma{i}"))
                self.cnt[f"{q}{i}"] = 0
        self.dma_i = {"g": 0, "s": 0}
        self.chsem = {}
        self.chcnt = {}
        for e in ENGINES:
            self.chsem[e] = es.enter_context(nc.semaphore(f"{name}_ch_{e}"))
            self.chcnt[e] = 0
        self.aliases = {}

    def alias(self, tag, deps):
        assert tag not in self.tasks and tag not in self.aliases, tag
        self.aliases[tag] = list(deps)
        return tag

    def _resolve(self, deps):
        out = []
        for d in deps:
            if d in self.aliases:
                out.extend(self._resolve(self.aliases[d]))
            else:
                out.append(d)
        return out

    def ch(self, e, eng, instr):
        self.chcnt[e] += 1
        instr.then_inc(self.chsem[e], 1)
        eng.wait_ge(self.chsem[e], self.chcnt[e])
        return instr

    def task(self, eng, tag, deps, emit, dma=False):
        assert tag not in self.tasks, tag
        t = Task(eng, tag, list(deps), emit, dma)
        if dma:
            q = "g" if eng == "gpsimd" else "s"
            k = f"{q}{self.dma_i[q] % NDMASEM}"
            self.dma_i[q] += 1
            self.cnt[k] += 16
            t.idx = (k, self.cnt[k])
        else:
            self.cnt[eng] += 1
            t.idx = (eng, self.cnt[eng])
        self.tasks[tag] = t
        self.order[eng].append(t)
        return tag

    def flush(self):
        nc = self.nc
        with nc.Block() as blk:
            for e in ENGINES:
                if not self.order[e]:
                    continue

                def body(eng, e=e):
                    hw = {k: 0 for k in self.sems}
                    for t in self.order[e]:
                        if t.dma:
                            sem_key, val = t.idx
                            if val > 16 and hw[sem_key] < val - 16:
                                eng.wait_ge(self.sems[sem_key], val - 16)
                                hw[sem_key] = val - 16
                        for dtag in self._resolve(t.deps):
                            dt_ = self.tasks[dtag]
                            sem_key, val = dt_.idx
                            if hw[sem_key] < val:
                                eng.wait_ge(self.sems[sem_key], val)
                                hw[sem_key] = val
                        instr = t.emit(eng)
                        assert instr is not None, f"task {t.tag} emit returned None"
                        sem_key, val = t.idx
                        instr.then_inc(self.sems[sem_key], 16 if t.dma else 1)

                getattr(blk, e)(body)
        self.order = {e: [] for e in ENGINES}

    def emit(self):
        self.flush()


def ap2(t):
    return t.ap() if hasattr(t, "ap") and callable(getattr(t, "ap", None)) else t



def dma_T(g, dst2d, src2d, o):
    return g.dma_start(dst2d.rearrange("p (dt o) -> p dt o", o=o),
                       src2d.rearrange("(dt p) o -> p dt o", p=128))

def ln_emit(v, x_ap, out_ap, wsq, c_mu, c_var, c_rstd, extra_in=None, extra_in2=None):
    raise NotImplementedError


def build_p2(nc, P, cfg, src, dst, w, pre="p2"):
    B, S = cfg["B"], cfg["S"]
    NT = B * S // 128
    tes = contextlib.ExitStack()
    a = lambda n, sh, dt: tes.enter_context(nc.sbuf_tensor(pre + n, sh, dt)).ap()
    psum = lambda n, sh, dt: tes.enter_context(nc.psum_tensor(pre + n, sh, dt)).ap()
    _task = P.task
    P_task = lambda eng, tag, deps, emit, dma=False: _task(
        eng, pre + tag, [pre + d for d in deps], emit, dma)

    baseWT = a("baseWT", [128, 8 * 1024], BF16)
    AallT = a("AallT", [128, 8 * 112], BF16)
    Ball = a("Ball", [112, 1024], BF16)
    routerWT = a("routerWT", [128, 8 * 6], BF16)
    basebrow = a("basebrow", [1, 1024], BF16)
    brrow = a("brrow", [1, 6], BF16)
    ones_bf = a("ones", [1, 128], BF16)
    ident = a("ident", [128, 128], BF16)
    epscol = a("epscol", [128, 1], F32)

    NB = 3
    xin = [a(f"x{i}", [128, 1024], F32) for i in range(NB)]
    xbf = [a(f"xbf{i}", [128, 1024], BF16) for i in range(NB)]
    xT = [a(f"xT{i}", [128, 1024], BF16) for i in range(NB)]
    hwbf = [a(f"hw{i}", [128, 112], BF16) for i in range(NB)]
    hwT = [a(f"hwT{i}", [112, 128], BF16) for i in range(NB)]
    wgt = [a(f"wgt{i}", [128, 6], F32) for i in range(NB)]
    esb = [a(f"esb{i}", [128, 6], F32) for i in range(NB)]
    xout = [a(f"xo{i}", [128, 1024], F32) for i in range(NB)]
    scr = [a(f"scr{i}", [128, 1024], F32) for i in range(NB)]
    st = [a(f"st{i}", [128, 8], F32) for i in range(NB)]

    ps_tr = [psum("pstr", [128, 1024], BF16)] * 2
    ps_l = [psum("psl", [128, 6], F32)] * 2
    ps_hw = [psum("pshw", [128, 112], F32)] * 2
    ps_hwT = [psum("pshwT", [112, 128], BF16)] * 2
    ps_o = [psum(f"pso{i}", [128, 1024], F32) for i in range(2)]

    src_t = src.rearrange("b (t p) d -> (b t) p d", p=128)
    dst_t = dst.rearrange("b (t p) d -> (b t) p d", p=128)

    P_task("gpsimd", "wl0", [], lambda g: dma_T(g, baseWT, w["baseWT"], 1024), dma=True)
    P_task("gpsimd", "wl1", [], lambda g: dma_T(g, AallT, w["AallT"], 112), dma=True)
    P_task("gpsimd", "wl2", [], lambda g: g.dma_start(Ball[:, :], w["Ball"][:, :]), dma=True)
    P_task("gpsimd", "wl3", [], lambda g: dma_T(g, routerWT, w["routerWT"], 6), dma=True)
    P_task("gpsimd", "wl4", [], lambda g: g.dma_start(basebrow[:, :], w["base_b"][None, :]), dma=True)
    P_task("gpsimd", "wl5", [], lambda g: g.dma_start(brrow[:, :], w["router_b"][None, :]), dma=True)
    P_task("gpsimd", "wl6", [], lambda g: g.dma_start(ones_bf[:, :], w["ones_row"][:, :]), dma=True)
    P_task("gpsimd", "wl7", [], lambda g: g.dma_start(ident[:, :], w["ident"][:, :]), dma=True)
    P_task("gpsimd", "wl8", [], lambda g: g.dma_start(epscol[:, :], w["epscol"][:, :]), dma=True)
    WL = [f"wl{i}" for i in range(9)]

    for t in range(NT):
        i = t % NB
        j = t % 2
        pv = lambda tag, tt=t: f"{tag}@{tt}"
        me = lambda tag, tt=t: f"{tag}@{tt}"

        deps = [pv("resid", t - NB)] if t >= NB else []
        P_task("sync", me("xin"), deps,
               lambda sy, i=i, t=t: sy.dma_start(xin[i][:, :], src_t[t, :, :]), dma=True)
        P_task("scalar", me("cast"), [me("xin")],
               lambda s, i=i: s.copy(xbf[i][:, :], xin[i][:, :]))
        deps = [me("cast")] + ([pv("xTev", t - 1)] if t >= 1 else [])
        def em_tr(t_, i=i, j=j):
            last = None
            for k in range(8):
                last = t_.transpose(ps_tr[j][:, 128 * k:128 * (k + 1)],
                                    xbf[i][:, 128 * k:128 * (k + 1)], ident[:, :])
            return last
        P_task("tensor", me("tr"), deps + [WL[7]], em_tr)
        P_task("scalar", me("xTev"), [me("tr")],
               lambda s, i=i, j=j: s.copy(xT[i][:, :], ps_tr[j][:, :]))
        deps = [me("xTev"), WL[3], WL[5], WL[6]] + ([pv("sexp", t - 1)] if t >= 1 else [])
        def em_rl(t_, i=i, j=j):
            for d in range(8):
                t_.matmul(ps_l[j][:, :], xT[i][:, 128 * d:128 * (d + 1)],
                          routerWT[:, 6 * d:6 * (d + 1)], start=(d == 0), stop=False)
            return t_.matmul(ps_l[j][:, :], ones_bf[:, :], brrow[:, :], start=False, stop=True)
        P_task("tensor", me("rl"), deps, em_rl)
        smax_deps = [me("rl")] + ([pv("xout", t - NB)] if t >= NB else [])
        P_task("vector", me("smax"), smax_deps,
               lambda v, i=i, j=j: v.reduce_max(st[i][:, 0:1], ps_l[j][:, :], axis=AX.X,
                                                negate=True))
        P_task("scalar", me("sexp"), [me("smax")],
               lambda s, i=i, j=j: s.activation(esb[i][:, :], ps_l[j][:, :], AF.Exp,
                                                bias=st[i][:, 0:1], scale=1.0))
        def em_wgt(v, i=i):
            C = lambda ins: P.ch("vector", v, ins)
            C(v.reduce_sum(st[i][:, 1:2], esb[i][:, :], axis=AX.X))
            C(v.reciprocal(st[i][:, 2:3], st[i][:, 1:2]))
            C(v.tensor_scalar(wgt[i][:, :], esb[i][:, :], st[i][:, 2:3], None, ALU.mult))
            C(v.reduce_max(st[i][:, 3:4], wgt[i][:, :], axis=AX.X))
            C(v.tensor_scalar(esb[i][:, :], wgt[i][:, :], st[i][:, 3:4], None, ALU.is_ge))
            C(v.tensor_scalar(esb[i][:, :], esb[i][:, :], BIG, None, ALU.mult))
            C(v.tensor_tensor(esb[i][:, :], wgt[i][:, :], esb[i][:, :], ALU.subtract))
            C(v.reduce_max(st[i][:, 4:5], esb[i][:, :], axis=AX.X))
            C(v.tensor_scalar(esb[i][:, :], wgt[i][:, :], st[i][:, 4:5], None, ALU.is_ge))
            C(v.tensor_tensor(wgt[i][:, :], wgt[i][:, :], esb[i][:, :], ALU.mult))
            C(v.reduce_sum(st[i][:, 5:6], wgt[i][:, :], axis=AX.X))
            C(v.tensor_scalar(st[i][:, 5:6], st[i][:, 5:6], 1e-6, None, ALU.add))
            C(v.reciprocal(st[i][:, 6:7], st[i][:, 5:6]))
            return v.tensor_scalar(wgt[i][:, :], wgt[i][:, :], st[i][:, 6:7], None, ALU.mult)
        wgt_deps = [me("sexp")] + ([pv("xout", t - NB)] if t >= NB else [])
        P_task("vector", me("wgt"), wgt_deps, em_wgt)
        deps = [me("xTev"), WL[0], WL[1], WL[4]]
        if t >= 1:
            deps += [pv("hwev", t - 1)]
        if t >= 2:
            deps += [pv("resid", t - 2)]
        def em_base(t_, i=i, j=j):
            for oc in range(2):
                for d in range(8):
                    t_.matmul(ps_o[j][:, 512 * oc:512 * (oc + 1)],
                              xT[i][:, 128 * d:128 * (d + 1)],
                              baseWT[:, 1024 * d + 512 * oc:1024 * d + 512 * (oc + 1)],
                              start=(d == 0), stop=False)
                t_.matmul(ps_o[j][:, 512 * oc:512 * (oc + 1)], ones_bf[:, :],
                          basebrow[:, 512 * oc:512 * (oc + 1)], start=False, stop=False)
            last = None
            for d in range(8):
                last = t_.matmul(ps_hw[j][:, :], xT[i][:, 128 * d:128 * (d + 1)],
                                 AallT[:, 112 * d:112 * (d + 1)], start=(d == 0), stop=(d == 7))
            return last
        P_task("tensor", me("base"), deps, em_base)
        def em_hwev(s, i=i, j=j):
            s.copy(hwbf[i][:, 0:16], ps_hw[j][:, 0:16])
            last = None
            for e in range(6):
                last = s.activation(hwbf[i][:, 16 * (e + 1):16 * (e + 2)],
                                    ps_hw[j][:, 16 * (e + 1):16 * (e + 2)],
                                    AF.Copy, scale=wgt[i][:, e:e + 1])
            return last
        P_task("scalar", me("hwev"), [me("base"), me("wgt")], em_hwev)
        deps = [me("hwev")] + ([pv("hwTev", t - 1)] if t >= 1 else [])
        P_task("tensor", me("hwtr"), deps,
               lambda t_, i=i, j=j: t_.transpose(ps_hwT[j][:, :], hwbf[i][:, :], ident[:, :]))
        P_task("scalar", me("hwTev"), [me("hwtr")],
               lambda s, i=i, j=j: s.copy(hwT[i][:, :], ps_hwT[j][:, :]))
        def em_ball(t_, i=i, j=j):
            last = None
            for oc in range(2):
                last = t_.matmul(ps_o[j][:, 512 * oc:512 * (oc + 1)], hwT[i][:, :],
                                 Ball[:, 512 * oc:512 * (oc + 1)], start=False, stop=True)
            return last
        P_task("tensor", me("ball"), [me("hwTev"), WL[2], me("base")], em_ball)
        def em_res(v, i=i, j=j):
            C = lambda ins: P.ch("vector", v, ins)
            C(v.tensor_tensor(scr[i][:, :], xin[i][:, :], ps_o[j][:, :], ALU.add))
            C(v.reduce_sum(st[i][:, 0:1], scr[i][:, :], axis=AX.X))
            C(v.tensor_scalar(st[i][:, 0:1], st[i][:, 0:1], 1.0 / 1024.0, None, ALU.mult))
            C(v.tensor_tensor(xout[i][:, :], scr[i][:, :], scr[i][:, :], ALU.mult))
            return v.reduce_sum(st[i][:, 1:2], xout[i][:, :], axis=AX.X)
        res_deps = [me("ball")] + ([pv("xst", t - NB)] if t >= NB else [])
        P_task("vector", me("resid"), res_deps, em_res)
        def em_rstd(s, i=i):
            s.mul(st[i][:, 2:3], st[i][:, 1:2], 1.0 / 1024.0)
            return s.activation(st[i][:, 3:4], st[i][:, 0:1], AF.Square)
        P_task("scalar", me("rstd0"), [me("resid")], em_rstd)
        def em_rstd1(v, i=i):
            return v.tensor_tensor(st[i][:, 4:5], st[i][:, 2:3], st[i][:, 3:4], ALU.subtract)
        P_task("vector", me("rstd1"), [me("rstd0")], em_rstd1)
        def em_rstd2(s, i=i):
            P.ch("scalar", s, s.activation(st[i][:, 5:6], st[i][:, 4:5], AF.Ln, bias=epscol[:, :]))
            return s.activation(st[i][:, 6:7], st[i][:, 5:6], AF.Exp, scale=-0.5)
        P_task("scalar", me("rstd2"), [me("rstd1"), WL[8]], em_rstd2)
        def em_xout(v, i=i):
            P.ch("vector", v, v.tensor_scalar(st[i][:, 7:8], st[i][:, 0:1], st[i][:, 6:7],
                                              -1.0, ALU.mult, ALU.mult))
            return v.tensor_scalar(xout[i][:, :], scr[i][:, :], st[i][:, 6:7], st[i][:, 7:8],
                                   ALU.mult, ALU.add)
        P_task("vector", me("xout"), [me("rstd2")], em_xout)
        P_task("sync", me("xst"), [me("xout")],
               lambda sy, i=i, t=t: sy.dma_start(dst_t[t, :, :], xout[i][:, :]), dma=True)

    P_task("sync", "fin", [f"xst@{t}" for t in range(max(0, NT - NB), NT)],
           lambda sy: sy.nop())
    P.flush()
    tes.close()



def build_p1(nc, P, cfg, src, dst, w, pre="p1"):
    B, S = cfg["B"], cfg["S"]
    NTB = S // 128
    NQ = S // 512
    NK = S // 128
    tes = contextlib.ExitStack()
    a = lambda n, sh, dt: tes.enter_context(nc.sbuf_tensor(pre + n, sh, dt)).ap()
    _task = P.task
    T = lambda eng, tag, deps, emit, dma=False: _task(
        eng, pre + tag, [pre + d for d in deps], emit, dma)

    ones_bf = a("ones", [1, 128], BF16)
    ones_f = a("onesf", [1, 128], F32)
    ident = a("ident", [128, 128], BF16)
    identf = a("identf", [128, 128], F32)
    epscol = a("epscol", [128, 1], F32)
    bqkvT = a("bqkvT", [128, 24], F32)
    bqkv_row = a("bqr", [1, 3072], BF16)
    bout_row = a("bor", [1, 1024], BF16)
    headsel = a("hsel", [128, 8 * 16], BF16)
    QT = a("QT", [128, 8 * S], BF16)
    KT = a("KT", [128, 8 * S], BF16)
    Vb = a("Vb", [128, NK * 1040], BF16)
    drow = a("drow", [1, S], F32)
    rrow = a("rrow", [1, S], F32)
    negCall = a("negCall", [128, 16], F32)
    negCrow = a("negCrow", [1, 16], F32)
    nqk = a("nqk", [16, 2], F32)
    ncmax = a("ncmax", [16, 8], F32)
    NB = 3

    Vb4 = Vb.rearrange("p (t g c) -> p t g c", g=16, c=65)

    src_t = src.rearrange("b (t p) d -> b t p d", p=128)
    dst_t = dst.rearrange("b (t p) d -> b t p d", p=128)

    T("gpsimd", "c0", [], lambda g: g.dma_start(ones_bf[:, :], w["ones_row"][:, :]), dma=True)
    T("gpsimd", "c1", [], lambda g: g.dma_start(ones_f[:, :], w["ones_row_f"][:, :]), dma=True)
    T("gpsimd", "c2", [], lambda g: g.dma_start(ident[:, :], w["ident"][:, :]), dma=True)
    T("gpsimd", "c3", [], lambda g: g.dma_start(identf[:, :], w["ident_f"][:, :]), dma=True)
    T("gpsimd", "c4", [], lambda g: g.dma_start(epscol[:, :], w["epscol"][:, :]), dma=True)
    T("gpsimd", "c7", [], lambda g: g.dma_start(bqkvT[:, :], w["bqkvT"][:, :]), dma=True)
    T("gpsimd", "c8", [], lambda g: g.dma_start(bqkv_row[:, :], w["b_qkv"][None, :]), dma=True)
    T("gpsimd", "c9", [], lambda g: g.dma_start(bout_row[:, :], w["b_out"][None, :]), dma=True)
    T("gpsimd", "c10", [], lambda g: g.dma_start(headsel[:, :], w["headsel"][:, :]), dma=True)

    for b in range(B):
        bb = f"b{b}_"
        bes = contextlib.ExitStack()
        ab = lambda n, sh, dt: bes.enter_context(nc.sbuf_tensor(pre + bb + n, sh, dt)).ap()

        sA = contextlib.ExitStack()
        aA = lambda n, sh, dt: sA.enter_context(nc.sbuf_tensor(pre + bb + n, sh, dt)).ap()
        xnT = ab("xnT", [128, 8 * S], BF16)
        gamma_bc = aA("gbc", [128, 1024], F32)
        beta_bc = aA("bbc", [128, 1024], F32)
        xin = [aA(f"x{i}", [128, 1024], F32) for i in range(NB)]
        xnb = [aA(f"xnb{i}", [128, 1024], BF16) for i in range(NB)]
        scr = [aA(f"scr{i}", [128, 1024], F32) for i in range(NB)]
        st = [aA(f"st{i}", [128, 8], F32) for i in range(NB)]
        ps_tr = sA.enter_context(nc.psum_tensor(pre + bb + "pstr", [128, 1024], BF16)).ap()
        ps_fil = sA.enter_context(nc.psum_tensor(pre + bb + "psfil", [128, 128], F32)).ap()
        T("gpsimd", bb + "gbc", [], lambda g: g.dma_start(gamma_bc[:, :], w["gamma_bc"][:, :]), dma=True)
        T("gpsimd", bb + "bbc", [], lambda g: g.dma_start(beta_bc[:, :], w["beta_bc"][:, :]), dma=True)
        T("gpsimd", bb + "vset", [],
          lambda g: g.memset(Vb4[:, :, :, 64:65], 1.0))
        xnT3 = xnT.rearrange("p (dt s) -> p dt s", s=S)
        def em_filA(t_):
            last = None
            for _ in range(24):
                last = t_.matmul(ps_fil[:, :], ident[:, :], ident[:, :],
                                 start=True, stop=True)
            return last
        T("tensor", bb + "fil@pre", ["c2"], em_filA)
        for t in range(NTB):
            i = t % NB
            M = lambda tag, tt=t: f"{bb}{tag}@{tt}"
            deps = [M("lnf", t - NB), M("xnev", t - NB)] if t >= NB else []
            T("sync", M("xin"), deps,
              lambda sy, i=i, t=t, b=b: sy.dma_start(xin[i][:, :], src_t[b, t, :, :]), dma=True)
            def em_stats(v, i=i):
                C = lambda ins: P.ch("vector", v, ins)
                C(v.tensor_tensor(scr[i][:, :], xin[i][:, :], xin[i][:, :], ALU.mult))
                C(v.reduce_sum(st[i][:, 1:2], scr[i][:, :], axis=AX.X))
                C(v.reduce_sum(st[i][:, 0:1], xin[i][:, :], axis=AX.X))
                C(v.tensor_scalar(st[i][:, 0:1], st[i][:, 0:1], 1.0 / 1024.0, None, ALU.mult))
                C(v.tensor_scalar(st[i][:, 2:3], st[i][:, 1:2], 1.0 / 1024.0, None, ALU.mult))
                C(v.tensor_scalar(st[i][:, 3:4], st[i][:, 0:1], st[i][:, 0:1], None, ALU.mult))
                return v.tensor_tensor(st[i][:, 4:5], st[i][:, 2:3], st[i][:, 3:4], ALU.subtract)
            T("vector", M("stats"), [M("xin")], em_stats)
            def em_rstd(s, i=i):
                P.ch("scalar", s, s.activation(st[i][:, 5:6], st[i][:, 4:5], AF.Ln, bias=epscol[:, :]))
                return s.activation(st[i][:, 6:7], st[i][:, 5:6], AF.Exp, scale=-0.5)
            T("scalar", M("rstd"), [M("stats"), "c4"], em_rstd)
            def em_lnf(v, i=i):
                C = lambda ins: P.ch("vector", v, ins)
                C(v.tensor_scalar(st[i][:, 7:8], st[i][:, 0:1], st[i][:, 6:7], -1.0,
                                  ALU.mult, ALU.mult))
                C(v.tensor_scalar(scr[i][:, :], xin[i][:, :], st[i][:, 6:7], st[i][:, 7:8],
                                  ALU.mult, ALU.add))
                C(v.tensor_tensor(scr[i][:, :], scr[i][:, :], gamma_bc[:, :], ALU.mult))
                return v.tensor_tensor(xnb[i][:, :], scr[i][:, :], beta_bc[:, :], ALU.add)
            T("vector", M("lnf"), [M("rstd"), bb + "gbc", bb + "bbc"], em_lnf)
            deps = [M("lnf"), "c2"] + ([M("xnev", t - 1)] if t >= 1 else [])
            def em_tr(t_, i=i):
                last = None
                for k in range(8):
                    last = t_.transpose(ps_tr[:, 128 * k:128 * (k + 1)],
                                        xnb[i][:, 128 * k:128 * (k + 1)], ident[:, :])
                return last
            T("tensor", M("tr"), deps, em_tr)
            if t < NTB - 1:
                def em_fil(t_):
                    last = None
                    for _ in range(16):
                        last = t_.matmul(ps_fil[:, :], ident[:, :], ident[:, :],
                                         start=True, stop=True)
                    return last
                T("tensor", M("fil"), ["c2"], em_fil)
            T("scalar", M("xnev"), [M("tr")],
              lambda s, t=t: s.copy(xnT3[:, :, 128 * t:128 * (t + 1)],
                                    ps_tr.rearrange("p (dt c) -> p dt c", c=128)))
        P.flush()
        sA.close()

        sBv = contextlib.ExitStack()
        WvT = sBv.enter_context(nc.sbuf_tensor(pre + bb + "WvT", [128, 8 * 1024], BF16)).ap()
        ps_v = [sBv.enter_context(nc.psum_tensor(pre + bb + f"psv{i}", [128, 1024], F32)).ap()
                for i in range(2)]
        lastA = f"{bb}xnev@{NTB-1}"
        T("gpsimd", bb + "wv", [], lambda g: g.dma_start(
            WvT.rearrange("p (dt o) -> p dt o", o=1024),
            w["WqkvT"][:, 2048:3072].rearrange("(dt p) o -> p dt o", p=128)), dma=True)
        for tt in range(NTB):
            M = lambda tag, ttt=tt: f"{bb}{tag}@{ttt}"
            deps = [lastA, bb + "wv", "c8"] + ([M("vev", tt - 2)] if tt >= 2 else [])
            def em_vmm(t_, tt=tt):
                last = None
                for oc in range(2):
                    for d in range(8):
                        t_.matmul(ps_v[tt % 2][:, 512 * oc:512 * (oc + 1)],
                                  xnT3[:, d, 128 * tt:128 * (tt + 1)],
                                  WvT[:, 1024 * d + 512 * oc:1024 * d + 512 * (oc + 1)],
                                  start=(d == 0), stop=False)
                    last = t_.matmul(ps_v[tt % 2][:, 512 * oc:512 * (oc + 1)], ones_bf[:, :],
                                     bqkv_row[:, 2048 + 512 * oc:2048 + 512 * (oc + 1)],
                                     start=False, stop=True)
                return last
            T("tensor", M("vmm"), deps, em_vmm)
            T("scalar", M("vev"), [M("vmm"), bb + "vset"],
              lambda s, tt=tt: s.copy(Vb4[:, tt, :, 0:64],
                                      ps_v[tt % 2].rearrange("p (g dd) -> p g dd", dd=64)))
        P.flush()
        sBv.close()

        sBq = contextlib.ExitStack()
        WqkT = sBq.enter_context(nc.sbuf_tensor(pre + bb + "WqkT", [128, 8 * 2048], BF16)).ap()
        NQH = NQ if NQ == 1 else NQ // 2
        ps_qk = [sBq.enter_context(nc.psum_tensor(pre + bb + f"psqk{i}", [128, 512 * NQH],
                                                  F32)).ap() for i in range(2)]
        T("gpsimd", bb + "wqk", [], lambda g: g.dma_start(
            WqkT.rearrange("p (dt o) -> p dt o", o=2048),
            w["WqkvT"][:, 0:2048].rearrange("(dt p) o -> p dt o", p=128)), dma=True)
        uq = 0
        for ot in range(16):
            tgt = QT if ot < 8 else KT
            for hf in range(2 if NQ >= 2 else 1):
                M = lambda tag, o=ot, f_=hf: f"{bb}{tag}@{o}_{f_}"
                pz = uq % 2
                deps = [lastA, bb + "wqk", "c7"]
                if uq >= 2:
                    po, pf = (uq - 2) // (2 if NQ >= 2 else 1), (uq - 2) % (2 if NQ >= 2 else 1)
                    deps.append(f"{bb}qkev@{po}_{pf}")
                def em_qkmm(t_, ot=ot, hf=hf, pz=pz):
                    last = None
                    for qc in range(hf * NQH, (hf + 1) * NQH):
                        for d in range(8):
                            last = t_.matmul(
                                ps_qk[pz][:, 512 * (qc - hf * NQH):512 * (qc - hf * NQH + 1)],
                                WqkT[:, 2048 * d + 128 * ot:2048 * d + 128 * (ot + 1)],
                                xnT3[:, d, 512 * qc:512 * (qc + 1)],
                                start=(d == 0), stop=(d == 7))
                    return last
                T("tensor", M("qkmm"), deps, em_qkmm)
                def em_qkev(s, ot=ot, tgt=tgt, hf=hf, pz=pz):
                    return s.activation(
                        tgt[:, S * (ot % 8) + 512 * NQH * hf:S * (ot % 8) + 512 * NQH * (hf + 1)],
                        ps_qk[pz][:, 0:512 * NQH],
                        AF.Identity, bias=bqkvT[:, ot:ot + 1], scale=1.0)
                T("scalar", M("qkev"), [M("qkmm")], em_qkev)
                uq += 1
        qkev_last = f"{bb}qkev@15_{1 if NQ >= 2 else 0}"
        P.flush()
        sBq.close()
        bes.close()

        sC = contextlib.ExitStack()
        sqb = sC.enter_context(nc.sbuf_tensor(pre + bb + "sqb", [128, 8 * 512], BF16)).ap()
        ps_n = sC.enter_context(nc.psum_tensor(pre + bb + "psn", [128, 512], F32)).ap()
        ps_t16 = sC.enter_context(nc.psum_tensor(pre + bb + "pst16", [1, 16], F32)).ap()
        NCH = S // 512
        for which, buf in (("q", QT), ("k", KT)):
            for ch in range(NCH):
                M = f"{bb}n{which}{ch}"
                def em_sq(v, buf=buf, ch=ch):
                    last = None
                    for o in range(8):
                        last = v.tensor_tensor(sqb[:, 512 * o:512 * (o + 1)],
                                               buf[:, S * o + 512 * ch:S * o + 512 * (ch + 1)],
                                               buf[:, S * o + 512 * ch:S * o + 512 * (ch + 1)],
                                               ALU.mult)
                    return last
                hfl = 1 if NQ >= 2 else 0
                dq = [f"{bb}qkev@{(o if which == 'q' else o + 8)}_{hfl}" for o in range(8)]
                dq += [f"{bb}qkev@{(o if which == 'q' else o + 8)}_0" for o in range(8)]
                if ch >= 1:
                    dq += [f"{bb}n{which}{ch-1}_red"]
                T("vector", M + "_sq", dq, em_sq)
                def em_nmm(t_, ch=ch):
                    last = None
                    for o in range(8):
                        last = t_.matmul(ps_n[0:16, :], headsel[:, 16 * o:16 * (o + 1)],
                                         sqb[:, 512 * o:512 * (o + 1)],
                                         start=(o == 0), stop=(o == 7))
                    return last
                T("tensor", M + "_mm", [M + "_sq", "c10"], em_nmm)
                cidx = ch if which == "q" else NCH + ch
                T("vector", M + "_red", [M + "_mm"],
                  lambda v, cidx=cidx: v.reduce_max(ncmax[:, cidx:cidx + 1], ps_n[0:16, :], axis=AX.X))
        def em_nfin(v):
            C = lambda ins: P.ch("vector", v, ins)
            C(v.reduce_max(nqk[:, 0:1], ncmax[:, 0:NCH], axis=AX.X))
            C(v.reduce_max(nqk[:, 1:2], ncmax[:, NCH:2 * NCH], axis=AX.X))
            return v.tensor_tensor(nqk[:, 0:1], nqk[:, 0:1], nqk[:, 1:2], ALU.mult)
        T("vector", bb + "nfin", [f"{bb}nq{NCH-1}_red", f"{bb}nk{NCH-1}_red"], em_nfin)
        T("scalar", bb + "nsqrt", [bb + "nfin"],
          lambda s: s.activation(nqk[:, 1:2], nqk[:, 0:1], AF.Sqrt, scale=1.0))
        T("vector", bb + "nscale", [bb + "nsqrt"],
          lambda v: v.tensor_scalar(nqk[:, 0:1], nqk[:, 1:2], -0.125, None, ALU.mult))
        T("tensor", bb + "ntr", [bb + "nscale", "c3"],
          lambda t_: t_.transpose(ps_t16[:, :], nqk[:, 0:1], identf[0:16, 0:16]))
        T("vector", bb + "nrow", [bb + "ntr"],
          lambda v: v.tensor_copy(negCrow[:, :], ps_t16[:, :]))
        T("tensor", bb + "nbc", [bb + "nrow", "c1"],
          lambda t_: t_.matmul(ps_n[:, 0:16], ones_f[:, :], negCrow[:, :],
                               start=True, stop=True))
        T("vector", bb + "nbcev", [bb + "nbc"],
          lambda v: v.tensor_copy(negCall[:, :], ps_n[:, 0:16]))
        P.flush()
        sC.close()

        sDE = contextlib.ExitStack()
        ctxT = sDE.enter_context(nc.sbuf_tensor(pre + bb + "ctxT", [128, 8 * S], BF16,
                                                side="right")).ap()
        sD = contextlib.ExitStack()
        Pb = [sD.enter_context(nc.sbuf_tensor(pre + bb + f"Pb{i}", [128, S], BF16)).ap()
              for i in range(2)]
        ps_S = sD.enter_context(nc.psum_tensor(pre + bb + "psS", [128, S], F32)).ap()
        ps_ctx = sD.enter_context(nc.psum_tensor(pre + bb + "psctx", [65, S], F32)).ap()
        NHF = 2 if NQ >= 2 else 1
        HCW = NQ // NHF

        def p1_tail_rbnorm(hprev):
            hbp = f"{bb}h{hprev}_"
            r0p = (hprev % 2) * 64
            cofp = (hprev // 2) * S
            def em_rb(t_):
                last = None
                for qc in range(NQ):
                    last = t_.matmul(ps_ctx[0:64, 512 * qc:512 * (qc + 1)], ones_f[:, 0:64],
                                     rrow[:, 512 * qc:512 * (qc + 1)], start=True, stop=True)
                return last
            T("tensor", hbp + "rb", [hbp + "den", hbp + "cev", "c1"], em_rb)
            T("vector", hbp + "norm", [hbp + "rb"],
              lambda v, r0p=r0p, cofp=cofp: v.tensor_tensor(
                  ctxT[r0p:r0p + 64, cofp:cofp + S],
                  ctxT[r0p:r0p + 64, cofp:cofp + S], ps_ctx[0:64, :], ALU.mult))

        for h in range(16):
            hb = f"{bb}h{h}_"
            r0 = (h % 2) * 64
            cof = (h // 2) * S
            units = [(kt, hf) for kt in range(NK) for hf in range(NHF)]

            def emit_smm(kt, hf):
                M = lambda tag: f"{hb}{tag}@{kt}_{hf}"
                deps = [bb + "nbcev"]
                if kt >= 1:
                    deps.append(f"{hb}exp@{kt-1}_{hf}")
                elif h >= 1:
                    deps.append(f"{bb}h{h-1}_exp@{NK-1}_{hf}")
                def em(t_, h=h, kt=kt, hf=hf, r0=r0, cof=cof):
                    last = None
                    for qc in range(hf * HCW, (hf + 1) * HCW):
                        last = t_.matmul(ps_S[:, 512 * qc:512 * (qc + 1)],
                                         KT[r0:r0 + 64, cof + 128 * kt:cof + 128 * (kt + 1)],
                                         QT[r0:r0 + 64, cof + 512 * qc:cof + 512 * (qc + 1)],
                                         start=True, stop=True)
                    return last
                T("tensor", M("smm"), deps, em)
                deps = [M("smm")]
                if kt >= 2:
                    deps.append(f"{hb}ctx@{kt-2}_{hf}")
                elif h >= 1:
                    deps.append(f"{bb}h{h-1}_ctx@{NK-2+kt}_{hf}")
                T("scalar", M("exp"), deps,
                  lambda s, kt=kt, hf=hf, h=h: s.activation(
                      Pb[kt % 2][:, 512 * HCW * hf:512 * HCW * (hf + 1)],
                      ps_S[:, 512 * HCW * hf:512 * HCW * (hf + 1)], AF.Exp,
                      bias=negCall[:, h:h + 1], scale=0.125))

            def emit_ctx(kt, hf):
                M = lambda tag: f"{hb}{tag}@{kt}_{hf}"
                deps = [M("exp")]
                if h >= 1 and kt == 0:
                    deps.append(f"{bb}h{h-1}_norm")
                def em(t_, h=h, kt=kt, hf=hf):
                    last = None
                    for qc in range(hf * HCW, (hf + 1) * HCW):
                        last = t_.matmul(ps_ctx[:, 512 * qc:512 * (qc + 1)],
                                         Vb[:, 1040 * kt + 65 * h:1040 * kt + 65 * (h + 1)],
                                         Pb[kt % 2][:, 512 * qc:512 * (qc + 1)],
                                         start=(kt == 0), stop=(kt == NK - 1))
                    return last
                T("tensor", M("ctx"), deps, em)

            for ui, (kt, hf) in enumerate(units):
                emit_smm(kt, hf)
                if ui == 0 and h >= 1:
                    p1_tail_rbnorm(h - 1)
                if ui >= 1:
                    emit_ctx(*units[ui - 1])
            emit_ctx(*units[-1])
            T("vector", hb + "cev", [f"{hb}ctx@{NK-1}_{NHF-1}"],
              lambda v, r0=r0, cof=cof: v.tensor_copy(ctxT[r0:r0 + 64, cof:cof + S],
                                                      ps_ctx[0:64, :]))
            def em_den(s):
                P.ch("scalar", s, s.activation(drow[:, :], ps_ctx[64:65, :], AF.Ln,
                                               bias=epscol[0:1, :]))
                return s.activation(rrow[:, :], drow[:, :], AF.Exp, scale=-1.0)
            T("scalar", hb + "den", [f"{hb}ctx@{NK-1}_{NHF-1}", "c4"], em_den)
            def em_fill(t_):
                last = None
                for _ in range(14):
                    last = t_.matmul(ps_S[:, 0:512], ident[:, :], Pb[0][:, 0:512],
                                     start=True, stop=True)
                return last
            T("tensor", hb + "fill", [f"{hb}ctx@{NK-1}_{NHF-1}"], em_fill)
        p1_tail_rbnorm(15)
        P.flush()
        sD.close()

        sE = contextlib.ExitStack()
        aE = lambda n, sh, dt: sE.enter_context(nc.sbuf_tensor(pre + bb + n, sh, dt)).ap()
        WoutT = aE("WoutT", [128, 8 * 1024], BF16)
        xin = [aE(f"ex{i}", [128, 1024], F32) for i in range(NB)]
        xo = [aE(f"exo{i}", [128, 1024], F32) for i in range(NB)]
        scr = [aE(f"escr{i}", [128, 1024], F32) for i in range(NB)]
        st = [aE(f"est{i}", [128, 8], F32) for i in range(NB)]
        ps_o = [sE.enter_context(nc.psum_tensor(pre + bb + f"pso{i}", [128, 1024], F32)).ap()
                for i in range(2)]
        T("gpsimd", bb + "wout", [], lambda g: dma_T(g, WoutT, w["WoutT"], 1024), dma=True)
        lastD = f"{bb}h15_norm"
        for t in range(NTB):
            i = t % NB
            j = t % 2
            M = lambda tag, tt=t: f"{bb}e_{tag}@{tt}"
            deps = [M("xst", t - NB)] if t >= NB else []
            T("sync", M("xin"), deps,
              lambda sy, i=i, t=t, b=b: sy.dma_start(xin[i][:, :], src_t[b, t, :, :]), dma=True)
            deps = [lastD, bb + "wout", "c9"] + ([M("res", t - 2)] if t >= 2 else [])
            def em_omm(t_, t=t, j=j):
                last = None
                for oc in range(2):
                    for d in range(8):
                        t_.matmul(ps_o[j][:, 512 * oc:512 * (oc + 1)],
                                  ctxT[:, S * d + 128 * t:S * d + 128 * (t + 1)],
                                  WoutT[:, 1024 * d + 512 * oc:1024 * d + 512 * (oc + 1)],
                                  start=(d == 0), stop=False)
                    last = t_.matmul(ps_o[j][:, 512 * oc:512 * (oc + 1)], ones_bf[:, :],
                                     bout_row[:, 512 * oc:512 * (oc + 1)], start=False, stop=True)
                return last
            T("tensor", M("omm"), deps, em_omm)
            def em_res(v, i=i, j=j):
                C = lambda ins: P.ch("vector", v, ins)
                C(v.tensor_scalar(scr[i][:, :], xin[i][:, :], 2.0, None, ALU.mult))
                C(v.tensor_tensor(scr[i][:, :], scr[i][:, :], ps_o[j][:, :], ALU.add))
                C(v.tensor_tensor(xo[i][:, :], scr[i][:, :], scr[i][:, :], ALU.mult))
                C(v.reduce_sum(st[i][:, 1:2], xo[i][:, :], axis=AX.X))
                C(v.reduce_sum(st[i][:, 0:1], scr[i][:, :], axis=AX.X))
                C(v.tensor_scalar(st[i][:, 0:1], st[i][:, 0:1], 1.0 / 1024.0, None, ALU.mult))
                C(v.tensor_scalar(st[i][:, 2:3], st[i][:, 1:2], 1.0 / 1024.0, None, ALU.mult))
                C(v.tensor_scalar(st[i][:, 3:4], st[i][:, 0:1], st[i][:, 0:1], None, ALU.mult))
                return v.tensor_tensor(st[i][:, 4:5], st[i][:, 2:3], st[i][:, 3:4], ALU.subtract)
            T("vector", M("res"), [M("omm"), M("xin")], em_res)
            def em_rstd(s, i=i):
                P.ch("scalar", s, s.activation(st[i][:, 5:6], st[i][:, 4:5], AF.Ln, bias=epscol[:, :]))
                return s.activation(st[i][:, 6:7], st[i][:, 5:6], AF.Exp, scale=-0.5)
            T("scalar", M("rstd"), [M("res"), "c4"], em_rstd)
            def em_lno(v, i=i):
                C = lambda ins: P.ch("vector", v, ins)
                C(v.tensor_scalar(st[i][:, 7:8], st[i][:, 0:1], st[i][:, 6:7], -1.0,
                                  ALU.mult, ALU.mult))
                return v.tensor_scalar(xo[i][:, :], scr[i][:, :], st[i][:, 6:7], st[i][:, 7:8],
                                       ALU.mult, ALU.add)
            T("vector", M("lno"), [M("rstd")], em_lno)
            T("sync", M("xst"), [M("lno")],
              lambda sy, i=i, t=t, b=b: sy.dma_start(dst_t[b, t, :, :], xo[i][:, :]), dma=True)
        T("sync", bb + "fin", [f"{bb}e_xst@{t}" for t in range(max(0, NTB - NB), NTB)],
          lambda sy: sy.nop())
        P.flush()
        sE.close()
        sDE.close()
    tes.close()


def host_prep_p1(inp):
    w = {}
    w["WqkvT"] = np.ascontiguousarray(inp["L0_attn_W_qkv"].T).astype(np.float32)
    w["WoutT"] = np.ascontiguousarray(inp["L0_attn_W_out"].T).astype(np.float32)
    w["b_qkv"] = inp["L0_attn_b_qkv"].astype(np.float32)
    w["b_out"] = inp["L0_attn_b_out"].astype(np.float32)
    w["bqkvT"] = np.ascontiguousarray(inp["L0_attn_b_qkv"].reshape(24, 128).T).astype(np.float32)
    w["gamma_bc"] = np.broadcast_to(inp["L0_attn_ln_gamma"], (128, 1024)).astype(np.float32).copy()
    w["beta_bc"] = np.broadcast_to(inp["L0_attn_ln_beta"], (128, 1024)).astype(np.float32).copy()
    hs = np.zeros((128, 8 * 16), np.float32)
    for o in range(8):
        hs[0:64, 16 * o + 2 * o] = 1.0
        hs[64:128, 16 * o + 2 * o + 1] = 1.0
    w["headsel"] = hs
    return w


def np_ref_p1(x, inp):
    xf = x.astype(np.float64)
    B, S, _ = xf.shape
    mu = xf.mean(-1, keepdims=True)
    var = xf.var(-1, keepdims=True)
    xn = (xf - mu) / np.sqrt(var + EPS) * inp["L0_attn_ln_gamma"] + inp["L0_attn_ln_beta"]
    qkv = xn @ inp["L0_attn_W_qkv"].T.astype(np.float64) + inp["L0_attn_b_qkv"]
    qkv = qkv.reshape(B, S, 3, 16, 64).transpose(2, 0, 3, 1, 4)
    q, k, v = qkv[0], qkv[1], qkv[2]
    sc = np.einsum("bhqd,bhkd->bhqk", q, k) / 8.0
    sc = sc - sc.max(-1, keepdims=True)
    e = np.exp(sc)
    p = e / e.sum(-1, keepdims=True)
    ctx = np.einsum("bhqk,bhkd->bhqd", p, v)
    ctx = ctx.transpose(0, 2, 1, 3).reshape(B, S, 1024)
    out = ctx @ inp["L0_attn_W_out"].T.astype(np.float64) + inp["L0_attn_b_out"]
    t = 2 * xf + out
    mu = t.mean(-1, keepdims=True)
    var = t.var(-1, keepdims=True)
    return ((t - mu) / np.sqrt(var + EPS)).astype(np.float32)



def build_p4(nc, P, cfg, src, dst, w, pre="p4"):
    B, S = cfg["B"], cfg["S"]
    HID = cfg.get("HID", 2816)
    HW = HID // 2
    NHT = HW // 128
    TOK = B * S
    MEGA = min(1024, TOK)
    NM = TOK // MEGA
    NTM = MEGA // 128
    NSUB = MEGA // 512
    tes = contextlib.ExitStack()
    a = lambda n, sh, dt: tes.enter_context(nc.sbuf_tensor(pre + n, sh, dt)).ap()
    _task = P.task
    T = lambda eng, tag, deps, emit, dma=False: _task(
        eng, pre + tag, [pre + d for d in deps], emit, dma)
    AL = lambda tag, deps: P.alias(pre + tag, [pre + d for d in deps])

    ones_bf = a("ones", [1, 128], BF16)
    ident = a("ident", [128, 128], BF16)
    identf = a("identf", [128, 128], F32)
    epscol = a("epscol", [128, 1], F32)
    eself = a("eself", [8, 8 * 128], F32)
    ebias = a("ebias", [128, 8], F32)
    gwT = a("gwT", [128, 8 * 8], BF16)
    xTm = a("xTm", [128, 8 * MEGA], BF16)
    acc = a("acc", [128, NTM * 1024], BF16)
    m8 = a("m8", [8, MEGA], F32)
    NB = 3

    xTm3 = xTm.rearrange("p (dt s) -> p dt s", s=MEGA)
    src_t = src.rearrange("b (t p) d -> (b t) p d", p=128)
    dst_t = dst.rearrange("b (t p) d -> (b t) p d", p=128)

    T("gpsimd", "c0", [], lambda g: g.dma_start(ones_bf[:, :], w["ones_row"][:, :]), dma=True)
    T("gpsimd", "c2", [], lambda g: g.dma_start(ident[:, :], w["ident"][:, :]), dma=True)
    T("gpsimd", "c3", [], lambda g: g.dma_start(identf[:, :], w["ident_f"][:, :]), dma=True)
    T("gpsimd", "c4", [], lambda g: g.dma_start(epscol[:, :], w["epscol"][:, :]), dma=True)
    T("gpsimd", "c5", [], lambda g: g.dma_start(eself[:, :], w["esel_f"][:, :]), dma=True)
    T("gpsimd", "c6", [], lambda g: g.dma_start(ebias[:, :], w["ebias_bc"][:, :]), dma=True)
    T("gpsimd", "c7", [], lambda g: dma_T(g, gwT, w["gatewT"], 8), dma=True)

    for m in range(NM):
        mb = f"m{m}_"
        ses = contextlib.ExitStack()
        sesR = contextlib.ExitStack()
        pscR = lambda n, sh, dt=F32: sesR.enter_context(
            nc.psum_tensor(pre + mb + n, sh, dt)).ap()
        ps_tr = [pscR(f"pstr{i}", [128, 1024], BF16) for i in range(2)]
        ps_l = [pscR(f"psl{i}", [128, 8]) for i in range(2)]
        ps_m = [pscR(f"psm{i}", [8, 128]) for i in range(2)]
        ps_fil = pscR("psfil", [128, 512])
        sR = contextlib.ExitStack()
        aR = lambda n, sh, dt: sR.enter_context(nc.sbuf_tensor(pre + mb + n, sh, dt)).ap()
        xin = [aR(f"x{i}", [128, 1024], F32) for i in range(NB)]
        xbf = [aR(f"xbf{i}", [128, 1024], BF16) for i in range(NB)]
        gw = [aR(f"gw{i}", [128, 8], F32) for i in range(NB)]
        esb = [aR(f"esb{i}", [128, 8], F32) for i in range(NB)]
        scr = [aR(f"scr{i}", [128, 8], F32) for i in range(NB)]
        st = [aR(f"st{i}", [128, 8], F32) for i in range(NB)]

        for tt in range(NTM):
            i = tt % NB
            t_g = m * NTM + tt
            M = lambda tag, s=tt: f"{mb}r_{tag}@{s}"
            deps = [M("mtr", tt - NB)] if tt >= NB else []
            T("sync", M("xin"), deps,
              lambda sy, i=i, t_g=t_g: sy.dma_start(xin[i][:, :], src_t[t_g, :, :]), dma=True)
            T("scalar", M("cast"), [M("xin")],
              lambda s, i=i: s.copy(xbf[i][:, :], xin[i][:, :]))
            deps = [M("cast"), "c2"] + ([M("xTev", tt - 2)] if tt >= 2 else [])
            def em_tr(t_, i=i, tt=tt):
                for _ in range(12):
                    t_.matmul(ps_fil[:, 0:128], ident[:, :], ident[:, :],
                              start=True, stop=True)
                last = None
                for k in range(8):
                    last = t_.transpose(ps_tr[tt % 2][:, 128 * k:128 * (k + 1)],
                                        xbf[i][:, 128 * k:128 * (k + 1)], ident[:, :])
                return last
            T("tensor", M("tr"), deps, em_tr)
            T("scalar", M("xTev"), [M("tr")],
              lambda s, tt=tt: s.copy(xTm3[:, :, 128 * tt:128 * (tt + 1)],
                                      ps_tr[tt % 2].rearrange("p (dt c) -> p dt c", c=128)))
            deps = [M("xTev"), "c7"] + ([M("sexp", tt - 2)] if tt >= 2 else [])
            def em_rl(t_, tt=tt):
                last = None
                for d in range(8):
                    last = t_.matmul(ps_l[tt % 2][:, :], xTm3[:, d, 128 * tt:128 * (tt + 1)],
                                     gwT[:, 8 * d:8 * (d + 1)], start=(d == 0), stop=(d == 7))
                return last
            T("tensor", M("rl"), deps, em_rl)
            smax_deps = [M("rl")] + ([M("wgt", tt - NB)] if tt >= NB else [])
            T("vector", M("smax"), smax_deps,
              lambda v, i=i, tt=tt: v.reduce_max(st[i][:, 0:1], ps_l[tt % 2][:, :],
                                                 axis=AX.X, negate=True))
            T("scalar", M("sexp"), [M("smax")],
              lambda s, i=i, tt=tt: s.activation(esb[i][:, :], ps_l[tt % 2][:, :], AF.Exp,
                                                 bias=st[i][:, 0:1], scale=1.0))
            def em_wgt(v, i=i):
                C = lambda ins: P.ch("vector", v, ins)
                C(v.reduce_sum(st[i][:, 1:2], esb[i][:, :], axis=AX.X))
                C(v.reciprocal(st[i][:, 2:3], st[i][:, 1:2]))
                C(v.tensor_scalar(gw[i][:, :], esb[i][:, :], st[i][:, 2:3], None, ALU.mult))
                C(v.tensor_tensor(esb[i][:, :], gw[i][:, :], ebias[:, :], ALU.add))
                C(v.reduce_max(st[i][:, 3:4], esb[i][:, :], axis=AX.X))
                C(v.tensor_scalar(st[i][:, 4:5], st[i][:, 3:4], BIG, None, ALU.mult))
                C(v.tensor_scalar(scr[i][:, :], esb[i][:, :], st[i][:, 3:4], None, ALU.is_ge))
                C(v.tensor_scalar(scr[i][:, :], scr[i][:, :], BIG, None, ALU.mult))
                C(v.tensor_tensor(scr[i][:, :], esb[i][:, :], scr[i][:, :], ALU.subtract))
                C(v.reduce_max(st[i][:, 5:6], scr[i][:, :], axis=AX.X))
                C(v.tensor_scalar(esb[i][:, :], esb[i][:, :], st[i][:, 5:6], None, ALU.is_ge))
                return v.tensor_tensor(gw[i][:, :], gw[i][:, :], esb[i][:, :], ALU.mult)
            T("vector", M("wgt"), [M("sexp"), "c6"], em_wgt)
            deps = [M("wgt"), "c3"] + ([M("mev", tt - 2)] if tt >= 2 else [])
            T("tensor", M("mtr"), deps,
              lambda t_, i=i, tt=tt: t_.transpose(ps_m[tt % 2][:, :], gw[i][:, :], identf[:, :]))
            T("vector", M("mev"), [M("mtr")],
              lambda v, tt=tt: v.tensor_copy(m8[:, 128 * tt:128 * (tt + 1)], ps_m[tt % 2][:, :]))
        P.flush()
        sR.close()
        sesR.close()
        psc = lambda n, sh, dt=F32: ses.enter_context(
            nc.psum_tensor(pre + mb + n, sh, dt)).ap()
        ps_M = [psc(f"psM{i}", [128, 512]) for i in range(2)]
        ps_U = [psc(f"psU{i}", [128, 512]) for i in range(2)]
        ps_G = [psc(f"psG{i}", [128, 512]) for i in range(2)]
        ps_O = [psc(f"psO{i}", [128, 512]) for i in range(2)]

        sEx = contextlib.ExitStack()
        aX = lambda n, sh, dt: sEx.enter_context(nc.sbuf_tensor(pre + mb + n, sh, dt)).ap()
        zTb = [aX(f"zT{i}", [128, 8 * 512], BF16) for i in range(2)]
        Hsb = aX("Hsb", [128, NHT * 512], BF16)
        silu_sb = aX("silu", [128, 512], F32)
        w1h = [aX(f"w1h{i}", [128, 8 * HW], BF16) for i in range(2)]
        w3h = [aX(f"w3h{i}", [128, 8 * HW], BF16) for i in range(2)]
        w2h = [aX(f"w2h{i}", [128, NHT * 1024], BF16) for i in range(2)]
        NG = 16 * NSUB

        def gtag(g):
            it, sub = divmod(g, NSUB)
            half, e = divmod(it, 8)
            return f"{mb}h{half}e{e}_s{sub}_"

        def emit_Mz(g):
            it, sub = divmod(g, NSUB)
            half, e = divmod(it, 8)
            tg = gtag(g)
            deps = [f"{mb}r_mev@{NTM-1}", "c5"]
            if g >= 2:
                deps.append(gtag(g - 2) + "zTdone")
            T("tensor", tg + "Mmm", deps,
              lambda t_, e=e, sub=sub, g=g: t_.matmul(
                  ps_M[g % 2][:, :], eself[:, 128 * e:128 * (e + 1)],
                  m8[:, 512 * sub:512 * (sub + 1)], start=True, stop=True))
            zdeps = [tg + "Mmm"]
            if g >= 2:
                zdeps.append(gtag(g - 2) + "Adone")
            def em_zT(v, sub=sub, g=g):
                last = None
                for d in range(8):
                    last = v.tensor_tensor(zTb[g % 2][:, 512 * d:512 * (d + 1)],
                                           xTm3[:, d, 512 * sub:512 * (sub + 1)],
                                           ps_M[g % 2][:, :], ALU.mult)
                return last
            T("vector", tg + "zT", zdeps, em_zT)
            AL(tg + "zTdone", [tg + "zT"])

        emit_Mz(0)
        for g in range(NG):
            it, sub = divmod(g, NSUB)
            half, e = divmod(it, 8)
            buf = it % 2
            eb = f"{mb}h{half}e{e}_"
            sb_ = gtag(g)
            Msu = lambda tag: f"{sb_}{tag}"
            prev = gtag(g - 1) if g >= 1 else None
            zT = zTb[g % 2]
            if sub == 0:
                wdep = ([f"{gtag((it-2) * NSUB + NSUB - 1)}Bdone"] if it >= 2 else [])
                T("gpsimd", eb + "w1", wdep, lambda g_, e=e, half=half, buf=buf: g_.dma_start(
                    w1h[buf].rearrange("p (dt h) -> p dt h", h=HW),
                    w["w1"][e, :, half * HW:(half + 1) * HW].rearrange("(dt p) h -> p dt h", p=128)),
                    dma=True)
                T("gpsimd", eb + "w3", wdep, lambda g_, e=e, half=half, buf=buf: g_.dma_start(
                    w3h[buf].rearrange("p (dt h) -> p dt h", h=HW),
                    w["w3"][e, :, half * HW:(half + 1) * HW].rearrange("(dt p) h -> p dt h", p=128)),
                    dma=True)
                T("gpsimd", eb + "w2", wdep, lambda g_, e=e, half=half, buf=buf: g_.dma_start(
                    w2h[buf].rearrange("p (ht o) -> p ht o", o=1024),
                    w["w2"][e, half * HW:(half + 1) * HW, :].rearrange("(ht p) o -> p ht o", p=128)),
                    dma=True)
            for ht in range(NHT):
                hb_ = f"{sb_}ht{ht}_"
                pj = ht % 2
                deps = [Msu("zTdone"), eb + "w1"]
                deps += [f"{sb_}ht{ht-2}_silu"] if ht >= 2 else \
                        ([prev + "Adone"] if g >= 1 else [])
                def em_U(t_, ht=ht, buf=buf, zT=zT, pj=pj):
                    last = None
                    for d in range(8):
                        last = t_.matmul(ps_U[pj][:, :],
                                         w1h[buf][:, HW * d + 128 * ht:HW * d + 128 * (ht + 1)],
                                         zT[:, 512 * d:512 * (d + 1)],
                                         start=(d == 0), stop=(d == 7))
                    return last
                T("tensor", hb_ + "U", deps, em_U)
                deps = [eb + "w3"]
                deps += [f"{sb_}ht{ht-2}_H"] if ht >= 2 else \
                        ([prev + "Adone"] if g >= 1 else [])
                def em_G(t_, ht=ht, buf=buf, zT=zT, pj=pj):
                    last = None
                    for d in range(8):
                        last = t_.matmul(ps_G[pj][:, :],
                                         w3h[buf][:, HW * d + 128 * ht:HW * d + 128 * (ht + 1)],
                                         zT[:, 512 * d:512 * (d + 1)],
                                         start=(d == 0), stop=(d == 7))
                    return last
                T("tensor", hb_ + "G", [Msu("zTdone")] + deps, em_G)
                sdeps = [hb_ + "U"]
                sdeps += [f"{sb_}ht{ht-1}_H"] if ht >= 1 else \
                         ([prev + "Adone"] if g >= 1 else [])
                T("scalar", hb_ + "silu", sdeps,
                  lambda s, pj=pj: s.activation(silu_sb[:, :], ps_U[pj][:, :], AF.Silu))
                T("vector", hb_ + "H", [hb_ + "silu", hb_ + "G"],
                  lambda v, ht=ht, pj=pj: v.tensor_tensor(Hsb[:, 512 * ht:512 * (ht + 1)],
                                                          silu_sb[:, :], ps_G[pj][:, :], ALU.mult))
            if g + 1 < NG:
                emit_Mz(g + 1)
            AL(Msu("Adone"), [f"{sb_}ht{NHT-1}_H"])
            for u in range(8):
                tl, oc = u // 2, u % 2
                ub = f"{sb_}u{u}_"
                deps = [Msu("Adone"), eb + "w2"]
                if u >= 2:
                    deps.append(f"{sb_}u{u-2}_acc")
                elif g >= 1:
                    deps.append(prev + "u7_acc")
                def em_B(t_, tl=tl, oc=oc, buf=buf, u=u):
                    last = None
                    for ht in range(NHT):
                        last = t_.matmul(ps_O[u % 2][:, :],
                                         Hsb[:, 512 * ht + 128 * tl:512 * ht + 128 * (tl + 1)],
                                         w2h[buf][:, 1024 * ht + 512 * oc:1024 * ht + 512 * (oc + 1)],
                                         start=(ht == 0), stop=(ht == NHT - 1))
                    return last
                T("tensor", ub + "B", deps, em_B)
                col = (sub * 4 + tl) * 1024 + oc * 512
                if it == 0:
                    T("vector", ub + "acc", [ub + "B"],
                      lambda v, col=col, u=u: v.tensor_copy(acc[:, col:col + 512],
                                                            ps_O[u % 2][:, :]))
                else:
                    T("vector", ub + "acc", [ub + "B"],
                      lambda v, col=col, u=u: v.tensor_tensor(acc[:, col:col + 512],
                                                             acc[:, col:col + 512],
                                                             ps_O[u % 2][:, :], ALU.add))
            AL(Msu("Bdone"), [f"{sb_}u{u}_acc" for u in range(8)])
        P.flush()
        sEx.close()

        sF = contextlib.ExitStack()
        aF = lambda n, sh, dt: sF.enter_context(nc.sbuf_tensor(pre + mb + n, sh, dt)).ap()
        xin = [aF(f"fx{i}", [128, 1024], F32) for i in range(NB)]
        xo = [aF(f"fxo{i}", [128, 1024], F32) for i in range(NB)]
        scr = [aF(f"fscr{i}", [128, 1024], F32) for i in range(NB)]
        st = [aF(f"fst{i}", [128, 8], F32) for i in range(NB)]
        for tt in range(NTM):
            i = tt % NB
            t_g = m * NTM + tt
            M = lambda tag, s=tt: f"{mb}f_{tag}@{s}"
            deps = [M("xst", tt - NB)] if tt >= NB else []
            T("sync", M("xin"), deps,
              lambda sy, i=i, t_g=t_g: sy.dma_start(xin[i][:, :], src_t[t_g, :, :]), dma=True)
            lastB = f"{mb}h1e7_s{NSUB-1}_Bdone"
            def em_res(v, i=i, tt=tt):
                C = lambda ins: P.ch("vector", v, ins)
                C(v.tensor_tensor(scr[i][:, :], xin[i][:, :], acc[:, 1024 * tt:1024 * (tt + 1)],
                                  ALU.add))
                C(v.tensor_tensor(xo[i][:, :], scr[i][:, :], scr[i][:, :], ALU.mult))
                C(v.reduce_sum(st[i][:, 1:2], xo[i][:, :], axis=AX.X))
                C(v.reduce_sum(st[i][:, 0:1], scr[i][:, :], axis=AX.X))
                C(v.tensor_scalar(st[i][:, 0:1], st[i][:, 0:1], 1.0 / 1024.0, None, ALU.mult))
                C(v.tensor_scalar(st[i][:, 2:3], st[i][:, 1:2], 1.0 / 1024.0, None, ALU.mult))
                C(v.tensor_scalar(st[i][:, 3:4], st[i][:, 0:1], st[i][:, 0:1], None, ALU.mult))
                return v.tensor_tensor(st[i][:, 4:5], st[i][:, 2:3], st[i][:, 3:4], ALU.subtract)
            T("vector", M("res"), [M("xin"), lastB], em_res)
            def em_rstd(s, i=i):
                P.ch("scalar", s, s.activation(st[i][:, 5:6], st[i][:, 4:5], AF.Ln, bias=epscol[:, :]))
                return s.activation(st[i][:, 6:7], st[i][:, 5:6], AF.Exp, scale=-0.5)
            T("scalar", M("rstd"), [M("res"), "c4"], em_rstd)
            def em_lno(v, i=i):
                P.ch("vector", v, v.tensor_scalar(st[i][:, 7:8], st[i][:, 0:1], st[i][:, 6:7],
                                                  -1.0, ALU.mult, ALU.mult))
                return v.tensor_scalar(xo[i][:, :], scr[i][:, :], st[i][:, 6:7], st[i][:, 7:8],
                                       ALU.mult, ALU.add)
            T("vector", M("lno"), [M("rstd")], em_lno)
            T("sync", M("xst"), [M("lno")],
              lambda sy, i=i, t_g=t_g: sy.dma_start(dst_t[t_g, :, :], xo[i][:, :]), dma=True)
        T("sync", mb + "fin", [f"{mb}f_xst@{t}" for t in range(max(0, NTM - NB), NTM)],
          lambda sy: sy.nop())
        P.flush()
        sF.close()
        ses.close()
    tes.close()


def host_prep_p4(inp):
    w = {}
    w["gatewT"] = np.ascontiguousarray(inp["L1_ffn_gate_w"].T).astype(np.float32)
    w["w1"] = inp["L1_ffn_w1"].astype(np.float32)
    w["w2"] = inp["L1_ffn_w2"].astype(np.float32)
    w["w3"] = inp["L1_ffn_w3"].astype(np.float32)
    w["ebias_bc"] = np.broadcast_to(inp["L1_ffn_expert_bias"], (128, 8)).astype(np.float32).copy()
    es = np.zeros((8, 8 * 128), np.float32)
    for e in range(8):
        es[e, 128 * e:128 * (e + 1)] = 1.0
    w["esel_f"] = es
    return w


def np_ref_p4(x, inp):
    xf = x.astype(np.float64)
    B, S, _ = xf.shape
    xr = xf.reshape(-1, 1024)
    sc = xr @ inp["L1_ffn_gate_w"].T.astype(np.float64)
    e = np.exp(sc - sc.max(-1, keepdims=True))
    g = e / e.sum(-1, keepdims=True)
    b = g + inp["L1_ffn_expert_bias"]
    sel = np.argsort(-b, -1, kind="stable")[:, :2]
    out = np.zeros_like(xr)
    for ei in range(8):
        msel = (sel == ei).any(-1)
        z = xr * (g[:, ei] * msel)[:, None]
        h = z @ inp["L1_ffn_w1"][ei].astype(np.float64)
        h = h / (1 + np.exp(-h)) * (z @ inp["L1_ffn_w3"][ei].astype(np.float64))
        out += h @ inp["L1_ffn_w2"][ei].astype(np.float64)
    t = xf + out.reshape(B, S, 1024)
    mu = t.mean(-1, keepdims=True)
    var = t.var(-1, keepdims=True)
    return ((t - mu) / np.sqrt(var + EPS)).astype(np.float32)



def build_p3(nc, P, cfg, src, dst, w, pre="p3"):
    B, S = cfg["B"], cfg["S"]
    NTB = S // 128
    NQ = S // 512
    NK = S // 128
    SCL = 1.0 / float(np.sqrt(96.0))
    tes = contextlib.ExitStack()
    a = lambda n, sh, dt: tes.enter_context(nc.sbuf_tensor(pre + n, sh, dt)).ap()
    _task = P.task
    T = lambda eng, tag, deps, emit, dma=False: _task(
        eng, pre + tag, [pre + d for d in deps], emit, dma)
    AL = lambda tag, deps: P.alias(pre + tag, [pre + d for d in deps])

    ones_bf = a("ones", [1, 128], BF16)
    ones_f = a("onesf", [1, 128], F32)
    ident = a("ident", [128, 128], BF16)
    identf = a("identf", [128, 128], F32)
    epscol = a("epscol", [128, 1], F32)
    loborow = a("lobo", [1, 16], F32)
    Crow = a("Crow", [1, 16], F32)
    mcol = a("mcol", [128, 2], F32)
    NB = 3
    src_t = src.rearrange("b (t p) d -> b t p d", p=128)
    dst_t = dst.rearrange("b (t p) d -> b t p d", p=128)

    T("gpsimd", "c0", [], lambda g: g.dma_start(ones_bf[:, :], w["ones_row"][:, :]), dma=True)
    T("gpsimd", "c1", [], lambda g: g.dma_start(ones_f[:, :], w["ones_row_f"][:, :]), dma=True)
    T("gpsimd", "c2", [], lambda g: g.dma_start(ident[:, :], w["ident"][:, :]), dma=True)
    T("gpsimd", "c3", [], lambda g: g.dma_start(identf[:, :], w["ident_f"][:, :]), dma=True)
    T("gpsimd", "c4", [], lambda g: g.dma_start(epscol[:, :], w["epscol"][:, :]), dma=True)
    T("gpsimd", "c5", [], lambda g: g.dma_start(loborow[:, :], w["lobo_log"][None, :]), dma=True)
    T("scalar", "cexp", ["c5"],
      lambda s: s.activation(Crow[:, :], loborow[:, :], AF.Exp))

    for b in range(B):
        bb = f"b{b}_"
        xi = contextlib.ExitStack()
        axi = lambda n, sh, dt: xi.enter_context(nc.sbuf_tensor(pre + bb + n, sh, dt)).ap()
        qcT = axi("qcT", [128, 8 * S], BF16)
        qrT = axi("qrT", [128, 8 * S], BF16)
        kcT = axi("kcT", [128, 8 * S], BF16)
        krT = axi("krT", [128, S], BF16)
        Vb = axi("Vb", [128, NK * 1040], BF16)
        Vb4 = Vb.rearrange("p (t g c) -> p t g c", g=16, c=65)
        sB = contextlib.ExitStack()
        ab = lambda n, sh, dt: sB.enter_context(nc.sbuf_tensor(pre + bb + n, sh, dt)).ap()
        psb = lambda n, sh, dt=F32: sB.enter_context(
            nc.psum_tensor(pre + bb + n, sh, dt)).ap()
        xT = ab("xT", [128, 8 * S], BF16)
        latT = ab("latT", [128, 2 * S], BF16)
        cos16 = ab("cos16", [128, NTB * 16], F32)
        sin16 = ab("sin16", [128, NTB * 16], F32)
        krb = ab("krb", [128, 32], BF16)
        rA = ab("rA", [128, 256], F32)
        rB = ab("rB", [128, 256], F32)
        ps_tr = psb("pstr", [128, 1024], BF16)
        NQH = NQ if NQ == 1 else NQ // 2
        NHF2 = 1 if NQ == 1 else 2
        ps_qk = [psb(f"psqk{i}", [128, 512 * NQH]) for i in range(2)]
        ps_tm = psb("pstm", [128, 1024])
        ps_kr = psb("pskr", [128, 32])
        xT3 = xT.rearrange("p (dt s) -> p dt s", s=S)

        T("gpsimd", bb + "tc16", [], lambda g: g.dma_start(cos16[:, :], w["cos16"][:, :]), dma=True)
        T("gpsimd", bb + "ts16", [], lambda g: g.dma_start(sin16[:, :], w["sin16"][:, :]), dma=True)
        T("gpsimd", bb + "vset", [], lambda g: g.memset(Vb4[:, :, :, 64:65], 1.0))
        T("gpsimd", bb + "krz", [], lambda g: g.memset(krT[:, :], 0.0))

        sA2 = contextlib.ExitStack()
        xin = [sA2.enter_context(nc.sbuf_tensor(pre + bb + f"x{i}", [128, 1024], F32)).ap()
               for i in range(NB)]
        xbf = [sA2.enter_context(nc.sbuf_tensor(pre + bb + f"xbf{i}", [128, 1024], BF16)).ap()
               for i in range(NB)]
        for t in range(NTB):
            i = t % NB
            M = lambda tag, tt=t: f"{bb}a_{tag}@{tt}"
            deps = [M("xTev", t - NB)] if t >= NB else []
            T("sync", M("xin"), deps,
              lambda sy, i=i, t=t, b=b: sy.dma_start(xin[i][:, :], src_t[b, t, :, :]), dma=True)
            T("scalar", M("cast"), [M("xin")],
              lambda s, i=i: s.copy(xbf[i][:, :], xin[i][:, :]))
            deps = [M("cast"), "c2"] + ([M("xTev", t - 1)] if t >= 1 else [])
            def em_tr(t_, i=i):
                last = None
                for k in range(8):
                    last = t_.transpose(ps_tr[:, 128 * k:128 * (k + 1)],
                                        xbf[i][:, 128 * k:128 * (k + 1)], ident[:, :])
                return last
            T("tensor", M("tr"), deps, em_tr)
            T("scalar", M("xTev"), [M("tr")],
              lambda s, t=t: s.copy(xT3[:, :, 128 * t:128 * (t + 1)],
                                    ps_tr.rearrange("p (dt c) -> p dt c", c=128)))
        lastA = f"{bb}a_xTev@{NTB-1}"
        P.flush()
        sA2.close()

        sWl = contextlib.ExitStack()
        Wkvd = sWl.enter_context(nc.sbuf_tensor(pre + bb + "Wkvd", [128, 8 * 256], BF16)).ap()
        T("gpsimd", bb + "wkvd", [], lambda g: dma_T(g, Wkvd, w["Wkvd"], 256), dma=True)
        uq = 0
        for lt in range(2):
            for hf in range(NHF2):
                M = lambda tag, l=lt, f_=hf: f"{bb}lat_{tag}@{l}_{f_}"
                pz = uq % 2
                deps = [lastA, bb + "wkvd"]
                if uq >= 2:
                    deps.append(f"{bb}lat_ev@{(uq-2)//NHF2}_{(uq-2)%NHF2}")
                def em_mm(t_, lt=lt, hf=hf, pz=pz):
                    last = None
                    for qc in range(hf * NQH, (hf + 1) * NQH):
                        for d in range(8):
                            last = t_.matmul(
                                ps_qk[pz][:, 512 * (qc - hf * NQH):512 * (qc - hf * NQH + 1)],
                                Wkvd[:, 256 * d + 128 * lt:256 * d + 128 * (lt + 1)],
                                xT3[:, d, 512 * qc:512 * (qc + 1)],
                                start=(d == 0), stop=(d == 7))
                    return last
                T("tensor", M("mm"), deps, em_mm)
                T("scalar", M("ev"), [M("mm")],
                  lambda s, lt=lt, hf=hf, pz=pz: s.copy(
                      latT[:, S * lt + 512 * NQH * hf:S * lt + 512 * NQH * (hf + 1)],
                      ps_qk[pz][:, :]))
                uq += 1
        lat_last = f"{bb}lat_ev@1_{NHF2-1}"
        P.flush()
        sWl.close()

        sWq = contextlib.ExitStack()
        Wq = sWq.enter_context(nc.sbuf_tensor(pre + bb + "Wq", [128, 8 * 1024], BF16)).ap()
        T("gpsimd", bb + "wq", [], lambda g: dma_T(g, Wq, w["Wq"], 1024), dma=True)
        uq = 0
        for ot in range(8):
            for hf in range(NHF2):
                M = lambda tag, o=ot, f_=hf: f"{bb}qc_{tag}@{o}_{f_}"
                pz = uq % 2
                deps = [lastA, bb + "wq"]
                if uq >= 2:
                    deps.append(f"{bb}qc_ev@{(uq-2)//NHF2}_{(uq-2)%NHF2}")
                else:
                    deps.append(lat_last)
                def em_mm(t_, ot=ot, hf=hf, pz=pz):
                    last = None
                    for qc in range(hf * NQH, (hf + 1) * NQH):
                        for d in range(8):
                            last = t_.matmul(
                                ps_qk[pz][:, 512 * (qc - hf * NQH):512 * (qc - hf * NQH + 1)],
                                Wq[:, 1024 * d + 128 * ot:1024 * d + 128 * (ot + 1)],
                                xT3[:, d, 512 * qc:512 * (qc + 1)],
                                start=(d == 0), stop=(d == 7))
                    return last
                T("tensor", M("mm"), deps, em_mm)
                T("scalar", M("ev"), [M("mm")],
                  lambda s, ot=ot, hf=hf, pz=pz: s.copy(
                      qcT[:, S * ot + 512 * NQH * hf:S * ot + 512 * NQH * (hf + 1)],
                      ps_qk[pz][:, :]))
                uq += 1
        qc_last = f"{bb}qc_ev@7_{NHF2-1}"
        P.flush()
        sWq.close()
        sWqr = contextlib.ExitStack()
        Wqr = sWqr.enter_context(nc.sbuf_tensor(pre + bb + "Wqr", [128, 8 * 1024], BF16)).ap()
        qrb = sWqr.enter_context(nc.sbuf_tensor(pre + bb + "qrb", [128, 1024], BF16)).ap()
        cosr = [sWqr.enter_context(nc.sbuf_tensor(pre + bb + f"cosr{i}", [128, 256], F32)).ap()
                for i in range(2)]
        sinr = [sWqr.enter_context(nc.sbuf_tensor(pre + bb + f"sinr{i}", [128, 256], F32)).ap()
                for i in range(2)]
        T("gpsimd", bb + "wqr", [], lambda g: dma_T(g, Wqr, w["Wqr_pad"], 1024), dma=True)
        T("gpsimd", bb + "qrz", [], lambda g: g.memset(qrb[:, :], 0.0))
        qr4 = qrb.rearrange("p (h c) -> p h c", c=64)
        for t in range(NTB):
            M = lambda tag, tt=t: f"{bb}qr_{tag}@{tt}"
            deps = [lastA, bb + "wqr"] + ([M("rope", t - 1)] if t >= 1 else [])
            def em_mm(t_, t=t):
                last = None
                for oc in range(2):
                    for d in range(8):
                        last = t_.matmul(ps_tm[:, 512 * oc:512 * (oc + 1)],
                                         xT3[:, d, 128 * t:128 * (t + 1)],
                                         Wqr[:, 1024 * d + 512 * oc:1024 * d + 512 * (oc + 1)],
                                         start=(d == 0), stop=(d == 7))
                return last
            T("tensor", M("mm"), deps, em_mm)
            cdeps = [M("rope", t - 2)] if t >= 2 else []
            T("gpsimd", M("tcos"), cdeps, lambda g, t=t: g.dma_start(
                cosr[t % 2][:, :], w["cosrep"][:, 256 * t:256 * (t + 1)]), dma=True)
            T("gpsimd", M("tsin"), cdeps, lambda g, t=t: g.dma_start(
                sinr[t % 2][:, :], w["sinrep"][:, 256 * t:256 * (t + 1)]), dma=True)
            ps4 = ps_tm.rearrange("p (h c) -> p h c", c=64)
            def em_rope(v, t=t, ps4=ps4, qr4=qr4):
                C = lambda ins: P.ch("vector", v, ins)
                co = cosr[t % 2].rearrange("p (h c) -> p h c", c=16)
                si = sinr[t % 2].rearrange("p (h c) -> p h c", c=16)
                rA3 = rA.rearrange("p (h c) -> p h c", c=16)
                rB3 = rB.rearrange("p (h c) -> p h c", c=16)
                C(v.tensor_tensor(rA3[:, :, :], ps4[:, :, 0:16], co[:, :, :], ALU.mult))
                C(v.tensor_tensor(rB3[:, :, :], ps4[:, :, 16:32], si[:, :, :], ALU.mult))
                C(v.tensor_tensor(qr4[:, :, 0:16], rA3[:, :, :], rB3[:, :, :], ALU.subtract))
                C(v.tensor_tensor(rA3[:, :, :], ps4[:, :, 16:32], co[:, :, :], ALU.mult))
                C(v.tensor_tensor(rB3[:, :, :], ps4[:, :, 0:16], si[:, :, :], ALU.mult))
                return v.tensor_tensor(qr4[:, :, 16:32], rA3[:, :, :], rB3[:, :, :], ALU.add)
            deps = [M("mm"), M("tcos"), M("tsin"), bb + "qrz"]
            if t >= 1:
                deps.append(M("tr", t - 1))
            T("vector", M("rope"), deps, em_rope)
            deps = [M("rope"), "c2"] + ([M("ev", t - 1)] if t >= 1 else [])
            def em_tr2(t_):
                last = None
                for k in range(8):
                    last = t_.transpose(ps_tr[:, 128 * k:128 * (k + 1)],
                                        qrb[:, 128 * k:128 * (k + 1)], ident[:, :])
                return last
            T("tensor", M("tr"), deps, em_tr2)
            T("scalar", M("ev"), [M("tr")],
              lambda s, t=t: s.copy(qrT.rearrange("p (dt s) -> p dt s", s=S)[:, :, 128 * t:128 * (t + 1)],
                                    ps_tr.rearrange("p (dt c) -> p dt c", c=128)))
        P.flush()
        sWqr.close()
        sWkr = contextlib.ExitStack()
        Wkr = sWkr.enter_context(nc.sbuf_tensor(pre + bb + "Wkr", [128, 8 * 32], BF16)).ap()
        T("gpsimd", bb + "wkr", [], lambda g: dma_T(g, Wkr, w["Wkr"], 32), dma=True)
        for t in range(NTB):
            M = lambda tag, tt=t: f"{bb}kr_{tag}@{tt}"
            deps = [lastA, bb + "wkr"] + ([M("rope", t - 1)] if t >= 1 else [])
            def em_mm(t_, t=t):
                last = None
                for d in range(8):
                    last = t_.matmul(ps_kr[:, :], xT3[:, d, 128 * t:128 * (t + 1)],
                                     Wkr[:, 32 * d:32 * (d + 1)], start=(d == 0), stop=(d == 7))
                return last
            T("tensor", M("mm"), deps, em_mm)
            def em_rope(v, t=t):
                C = lambda ins: P.ch("vector", v, ins)
                C(v.tensor_tensor(rA[:, 0:16], ps_kr[:, 0:16], cos16[:, 16 * t:16 * (t + 1)], ALU.mult))
                C(v.tensor_tensor(rB[:, 0:16], ps_kr[:, 16:32], sin16[:, 16 * t:16 * (t + 1)], ALU.mult))
                C(v.tensor_tensor(krb[:, 0:16], rA[:, 0:16], rB[:, 0:16], ALU.subtract))
                C(v.tensor_tensor(rA[:, 0:16], ps_kr[:, 16:32], cos16[:, 16 * t:16 * (t + 1)], ALU.mult))
                C(v.tensor_tensor(rB[:, 0:16], ps_kr[:, 0:16], sin16[:, 16 * t:16 * (t + 1)], ALU.mult))
                return v.tensor_tensor(krb[:, 16:32], rA[:, 0:16], rB[:, 0:16], ALU.add)
            deps = [M("mm"), bb + "tc16", bb + "ts16"]
            if t >= 1:
                deps.append(M("tr", t - 1))
            T("vector", M("rope"), deps, em_rope)
            deps = [M("rope"), "c2"] + ([M("ev", t - 1)] if t >= 1 else [f"{bb}qr_ev@{NTB-1}"])
            T("tensor", M("tr"), deps,
              lambda t_: t_.transpose(ps_tr[0:32, 0:128], krb[:, :], ident[:, :]))
            def em_krev(s, t=t):
                s.copy(krT[0:32, 128 * t:128 * (t + 1)], ps_tr[0:32, 0:128])
                return s.copy(krT[64:96, 128 * t:128 * (t + 1)], ps_tr[0:32, 0:128])
            T("scalar", M("ev"), [M("tr"), bb + "krz"], em_krev)
        P.flush()
        sWkr.close()
        sWku = contextlib.ExitStack()
        Wku = sWku.enter_context(nc.sbuf_tensor(pre + bb + "Wku", [128, 2 * 1024], BF16)).ap()
        T("gpsimd", bb + "wku", [], lambda g: dma_T(g, Wku, w["Wku"], 1024), dma=True)
        uq = 0
        for ot in range(8):
            for hf in range(NHF2):
                M = lambda tag, o=ot, f_=hf: f"{bb}kc_{tag}@{o}_{f_}"
                pz = uq % 2
                deps = [lat_last, bb + "wku"]
                if uq >= 2:
                    deps.append(f"{bb}kc_ev@{(uq-2)//NHF2}_{(uq-2)%NHF2}")
                else:
                    deps.append(qc_last)
                def em_mm(t_, ot=ot, hf=hf, pz=pz):
                    last = None
                    for qc in range(hf * NQH, (hf + 1) * NQH):
                        for lt in range(2):
                            last = t_.matmul(
                                ps_qk[pz][:, 512 * (qc - hf * NQH):512 * (qc - hf * NQH + 1)],
                                Wku[:, 1024 * lt + 128 * ot:1024 * lt + 128 * (ot + 1)],
                                latT[:, S * lt + 512 * qc:S * lt + 512 * (qc + 1)],
                                start=(lt == 0), stop=(lt == 1))
                    return last
                T("tensor", M("mm"), deps, em_mm)
                T("scalar", M("ev"), [M("mm")],
                  lambda s, ot=ot, hf=hf, pz=pz: s.copy(
                      kcT[:, S * ot + 512 * NQH * hf:S * ot + 512 * NQH * (hf + 1)],
                      ps_qk[pz][:, :]))
                uq += 1
        kc_last = f"{bb}kc_ev@7_{NHF2-1}"
        P.flush()
        sWku.close()
        sWvu = contextlib.ExitStack()
        Wvu = sWvu.enter_context(nc.sbuf_tensor(pre + bb + "Wvu", [128, 2 * 1024], BF16)).ap()
        T("gpsimd", bb + "wvu", [], lambda g: dma_T(g, Wvu, w["Wvu"], 1024), dma=True)
        for tt in range(NTB):
            M = lambda tag, t=tt: f"{bb}v_{tag}@{t}"
            deps = [lat_last, bb + "wvu", bb + "vset"] + \
                   ([M("ev", tt - 1)] if tt >= 1 else [f"{bb}qr_rope@{NTB-1}"])
            def em_mm(t_, tt=tt):
                last = None
                for oc in range(2):
                    for lt in range(2):
                        last = t_.matmul(ps_tm[:, 512 * oc:512 * (oc + 1)],
                                         latT[:, S * lt + 128 * tt:S * lt + 128 * (tt + 1)],
                                         Wvu[:, 1024 * lt + 512 * oc:1024 * lt + 512 * (oc + 1)],
                                         start=(lt == 0), stop=(lt == 1))
                return last
            T("tensor", M("mm"), deps, em_mm)
            T("scalar", M("ev"), [M("mm")],
              lambda s, tt=tt: s.copy(Vb4[:, tt, :, 0:64],
                                      ps_tm.rearrange("p (g dd) -> p g dd", dd=64)))
        P.flush()
        sWvu.close()
        sB.close()
        if cfg.get("p3_stop") == "B":
            xi.close()
            tes.close()
            return

        sDE = contextlib.ExitStack()
        ctxT = sDE.enter_context(nc.sbuf_tensor(pre + bb + "ctxT", [128, 8 * S], BF16,
                                                side="right")).ap()
        sD = contextlib.ExitStack()
        aD = lambda n, sh, dt: sD.enter_context(nc.sbuf_tensor(pre + bb + n, sh, dt)).ap()
        negmS = aD("negmS", [128, 4 * 512], F32)
        maskT = aD("maskT", [128, 4 * 512], BF16)
        mrow = aD("mrow", [1, S], BF16)
        drow = aD("drow", [1, S], F32)
        rrow = aD("rrow", [1, S], F32)
        mcols = aD("mcols", [128, NK], F32)
        Pb = [aD(f"Pb{i}", [128, S], BF16) for i in range(2)]
        ps_X = sD.enter_context(nc.psum_tensor(pre + bb + "psX", [128, S], F32)).ap()
        ps_Y = sD.enter_context(nc.psum_tensor(pre + bb + "psY", [128, S], F32)).ap()
        T("gpsimd", bb + "mload", [], lambda g: g.dma_start(negmS[:, :], w["negmS"][:, :]), dma=True)
        T("gpsimd", bb + "mload2", [], lambda g: g.dma_start(maskT[:, :], w["maskT"][:, :]), dma=True)
        NHF = 2 if NQ >= 2 else 1
        HCW = NQ // NHF

        def p3_tail_rbnorm(hprev):
            hbp = f"{bb}h{hprev}_"
            r0p = (hprev % 2) * 64
            cofp = (hprev // 2) * S
            def em_rb(t_):
                last = None
                for qc in range(NQ):
                    last = t_.matmul(ps_Y[0:64, 512 * qc:512 * (qc + 1)], ones_f[:, 0:64],
                                     rrow[:, 512 * qc:512 * (qc + 1)], start=True, stop=True)
                return last
            T("tensor", hbp + "rb", [hbp + "den", hbp + "cev", "c1"], em_rb)
            T("vector", hbp + "norm", [hbp + "rb"],
              lambda v, r0p=r0p, cofp=cofp: v.tensor_tensor(
                  ctxT[r0p:r0p + 64, cofp:cofp + S],
                  ctxT[r0p:r0p + 64, cofp:cofp + S], ps_Y[0:64, :], ALU.mult))

        for h in range(16):
            hb = f"{bb}h{h}_"
            r0 = (h % 2) * 64
            cof = (h // 2) * S
            qt_list = list(range(NK - 1, -1, -1))
            for pi_, qt in enumerate(qt_list):
                M = lambda tag, q=qt: f"{hb}m_{tag}@{q}"
                kmax = qt // 4
                psZ = ps_X if pi_ % 2 == 0 else ps_Y
                deps = [kc_last, qc_last, f"{bb}qr_ev@{NTB-1}",
                        f"{bb}kr_ev@{NTB-1}", bb + "mload"]
                if pi_ >= 2:
                    deps.append(f"{hb}m_red@{qt_list[pi_ - 2]}")
                elif h >= 1:
                    deps.append(f"{bb}h{h-1}_" + ("lastexp" if pi_ % 2 == 0 else "norm"))
                def em_smm(t_, qt=qt, kmax=kmax, r0=r0, cof=cof, psZ=psZ):
                    last = None
                    for kc in range(kmax + 1):
                        t_.matmul(psZ[:, 512 * kc:512 * (kc + 1)],
                                  qcT[r0:r0 + 64, cof + 128 * qt:cof + 128 * (qt + 1)],
                                  kcT[r0:r0 + 64, cof + 512 * kc:cof + 512 * (kc + 1)],
                                  start=True, stop=False)
                        last = t_.matmul(psZ[:, 512 * kc:512 * (kc + 1)],
                                         qrT[r0:r0 + 64, cof + 128 * qt:cof + 128 * (qt + 1)],
                                         krT[r0:r0 + 64, 512 * kc:512 * (kc + 1)],
                                         start=False, stop=True)
                    return last
                T("tensor", M("smm"), deps, em_smm)
                def em_red(v, qt=qt, kmax=kmax, psZ=psZ):
                    C = lambda ins: P.ch("vector", v, ins)
                    C(v.tensor_tensor(Pb[0][:, 0:512], psZ[:, 512 * kmax:512 * (kmax + 1)],
                                      negmS[:, 512 * (qt % 4):512 * (qt % 4 + 1)], ALU.add))
                    C(v.reduce_max(mcols[:, qt:qt + 1], Pb[0][:, 0:512], axis=AX.X))
                    if kmax >= 1:
                        C(v.reduce_max(mcols[:, NK - 1:NK] if False else Pb[0][0:1, 0:1], psZ[:, 0:1], axis=AX.X))
                    return v.tensor_scalar(mcols[:, qt:qt + 1], mcols[:, qt:qt + 1],
                                           -1.0, None, ALU.mult)
                def em_red2(v, qt=qt, kmax=kmax, psZ=psZ):
                    C = lambda ins: P.ch("vector", v, ins)
                    C(v.tensor_tensor(Pb[0][:, 0:512], psZ[:, 512 * kmax:512 * (kmax + 1)],
                                      negmS[:, 512 * (qt % 4):512 * (qt % 4 + 1)], ALU.add))
                    C(v.reduce_max(mcols[:, qt:qt + 1], Pb[0][:, 0:512], axis=AX.X))
                    if kmax >= 1:
                        C(v.reduce_max(Pb[0].bitcast(F32)[:, 0:1] if False else drow[0:1, 0:1],
                                       psZ[0:1, 0:1], axis=AX.X))
                    return v.tensor_scalar(mcols[:, qt:qt + 1], mcols[:, qt:qt + 1],
                                           -1.0, None, ALU.mult)
                def em_red3(v, qt=qt, kmax=kmax, psZ=psZ):
                    C = lambda ins: P.ch("vector", v, ins)
                    C(v.tensor_tensor(Pb[0][:, 0:512], psZ[:, 512 * kmax:512 * (kmax + 1)],
                                      negmS[:, 512 * (qt % 4):512 * (qt % 4 + 1)], ALU.add))
                    C(v.reduce_max(mcols[:, qt:qt + 1], Pb[0][:, 0:512], axis=AX.X))
                    if kmax >= 1:
                        C(v.reduce_max(Pb[1][:, 0:1], psZ[:, 0:512 * kmax], axis=AX.X))
                        C(v.tensor_tensor(mcols[:, qt:qt + 1], mcols[:, qt:qt + 1],
                                          Pb[1][:, 0:1], ALU.max))
                    return v.tensor_scalar(mcols[:, qt:qt + 1], mcols[:, qt:qt + 1],
                                           -1.0, None, ALU.mult)
                T("vector", M("red"), [M("smm")], em_red3)
                if pi_ == 0 and h >= 1:
                    p3_tail_rbnorm(h - 1)
            def em_mtrs(t_):
                last = None
                for qt in range(NK):
                    last = t_.transpose(ps_X[0:1, 128 * qt:128 * (qt + 1)],
                                        mcols[:, qt:qt + 1], identf[:, :])
                return last
            T("tensor", hb + "mtrs", [f"{hb}m_red@{qt_list[-1]}", f"{hb}m_red@{qt_list[-2]}", "c3"],
              em_mtrs)
            T("vector", hb + "mrowev", [hb + "mtrs"],
              lambda v: v.tensor_copy(mrow[:, :], ps_X[0:1, :]))
            units = []
            for kt in range(NK):
                qs = kt // 4
                for hf in range(NHF):
                    qcs = [qc for qc in range(hf * HCW, (hf + 1) * HCW) if qc >= qs]
                    if qcs:
                        units.append((kt, hf, qcs))

            def emit_s2(kt, hf, qcs):
                M = lambda tag: f"{hb}{tag}@{kt}_{hf}"
                qs = kt // 4
                deps = [hb + "mrowev", "c0"]
                prev_same_half = [(k2, h2) for (k2, h2, _) in units
                                  if h2 == hf and k2 < kt]
                if prev_same_half:
                    deps.append(f"{hb}exp@{prev_same_half[-1][0]}_{hf}")
                def em(t_, kt=kt, qcs=qcs, r0=r0, cof=cof):
                    last = None
                    for qc in qcs:
                        t_.matmul(ps_X[:, 512 * qc:512 * (qc + 1)],
                                  kcT[r0:r0 + 64, cof + 128 * kt:cof + 128 * (kt + 1)],
                                  qcT[r0:r0 + 64, cof + 512 * qc:cof + 512 * (qc + 1)],
                                  start=True, stop=False)
                        t_.matmul(ps_X[:, 512 * qc:512 * (qc + 1)],
                                  krT[r0:r0 + 64, 128 * kt:128 * (kt + 1)],
                                  qrT[r0:r0 + 64, cof + 512 * qc:cof + 512 * (qc + 1)],
                                  start=False, stop=False)
                        last = t_.matmul(ps_X[:, 512 * qc:512 * (qc + 1)],
                                         ones_bf[:, :], mrow[:, 512 * qc:512 * (qc + 1)],
                                         start=False, stop=True)
                    return last
                T("tensor", M("s2"), deps, em)
                deps = [M("s2")]
                prev2 = [(k2, h2) for (k2, h2, _) in units if h2 == hf and k2 < kt]
                if len(prev2) >= 2:
                    deps.append(f"{hb}ctx@{prev2[-2][0]}_{hf}")
                elif h >= 1:
                    deps.append(f"{bb}h{h-1}_lastctx@{hf}")
                def em_exp(s, kt=kt, qcs=qcs):
                    last = None
                    for qc in qcs:
                        last = s.activation(Pb[kt % 2][:, 512 * qc:512 * (qc + 1)],
                                            ps_X[:, 512 * qc:512 * (qc + 1)], AF.Exp, scale=SCL)
                    return last
                T("scalar", M("exp"), deps, em_exp)
                ddeps = [M("exp"), bb + "mload2"]
                if qs in qcs:
                    T("vector", M("dz"), ddeps,
                      lambda v, kt=kt, qs=qs: v.tensor_tensor(
                          Pb[kt % 2][:, 512 * qs:512 * (qs + 1)],
                          Pb[kt % 2][:, 512 * qs:512 * (qs + 1)],
                          maskT[:, 512 * (kt % 4):512 * (kt % 4 + 1)], ALU.mult))

            def emit_ctx(kt, hf, qcs):
                M = lambda tag: f"{hb}{tag}@{kt}_{hf}"
                qs = kt // 4
                deps = [M("exp")]
                if qs in qcs:
                    deps.append(M("dz"))
                if kt == 0 and h >= 1:
                    deps.append(f"{bb}h{h-1}_norm")
                def em(t_, kt=kt, qcs=qcs, h=h):
                    last = None
                    for qc in qcs:
                        last = t_.matmul(ps_Y[0:65, 512 * qc:512 * (qc + 1)],
                                         Vb[:, 1040 * kt + 65 * h:1040 * kt + 65 * (h + 1)],
                                         Pb[kt % 2][:, 512 * qc:512 * (qc + 1)],
                                         start=(kt == 0), stop=(kt == 4 * qc + 3))
                    return last
                T("tensor", M("ctx"), deps, em)

            for ui, u in enumerate(units):
                emit_s2(*u)
                if ui >= 1:
                    emit_ctx(*units[ui - 1])
            emit_ctx(*units[-1])
            AL(hb + "lastctx@0",
               [f"{hb}ctx@{units[-2][0]}_{units[-2][1]}" if units[-1][1] != 0 else
                f"{hb}ctx@{units[-1][0]}_{units[-1][1]}"])
            AL(hb + "lastctx@1", [f"{hb}ctx@{units[-1][0]}_{units[-1][1]}"])
            AL(hb + "lastexp",
               [f"{hb}exp@{units[-1][0]}_{units[-1][1]}",
                f"{hb}exp@{units[-2][0]}_{units[-2][1]}"])
            T("vector", hb + "cev", [f"{hb}ctx@{units[-1][0]}_{units[-1][1]}"],
              lambda v, r0=r0, cof=cof: v.tensor_copy(ctxT[r0:r0 + 64, cof:cof + S],
                                                      ps_Y[0:64, :]))
            def em_den(s, h=h):
                P.ch("scalar", s, s.activation(drow[:, :], ps_Y[64:65, :], AF.Ln,
                                               bias=Crow[0:1, h:h + 1]))
                return s.activation(rrow[:, :], drow[:, :], AF.Exp, scale=-1.0)
            T("scalar", hb + "den", [f"{hb}ctx@{units[-1][0]}_{units[-1][1]}", "cexp"], em_den)
            def em_fill(t_):
                last = None
                for _ in range(14):
                    last = t_.matmul(ps_X[:, 0:512], ident[:, :], Pb[0][:, 0:512],
                                     start=True, stop=True)
                return last
            T("tensor", hb + "fill", [f"{hb}ctx@{units[-1][0]}_{units[-1][1]}"], em_fill)
        p3_tail_rbnorm(15)
        P.flush()
        sD.close()
        xi.close()
        if cfg.get("p3_stop") == "D":
            sDE.close()
            tes.close()
            return
        sE = contextlib.ExitStack()
        aE = lambda n, sh, dt: sE.enter_context(nc.sbuf_tensor(pre + bb + n, sh, dt)).ap()
        WoT = aE("WoT", [128, 8 * 1024], BF16)
        xin = [aE(f"ex{i}", [128, 1024], F32) for i in range(NB)]
        xo = [aE(f"exo{i}", [128, 1024], F32) for i in range(NB)]
        scr = [aE(f"escr{i}", [128, 1024], F32) for i in range(NB)]
        st = [aE(f"est{i}", [128, 8], F32) for i in range(NB)]
        ps_o = [sE.enter_context(nc.psum_tensor(pre + bb + f"pso{i}", [128, 1024], F32)).ap()
                for i in range(2)]
        T("gpsimd", bb + "wo", [], lambda g: dma_T(g, WoT, w["Wo"], 1024), dma=True)
        lastD = f"{bb}h15_norm"
        for t in range(NTB):
            i = t % NB
            j = t % 2
            M = lambda tag, tt=t: f"{bb}e_{tag}@{tt}"
            deps = [M("xst", t - NB)] if t >= NB else []
            T("sync", M("xin"), deps,
              lambda sy, i=i, t=t, b=b: sy.dma_start(xin[i][:, :], src_t[b, t, :, :]), dma=True)
            deps = [lastD, bb + "wo"] + ([M("res", t - 2)] if t >= 2 else [])
            def em_omm(t_, t=t, j=j):
                last = None
                for oc in range(2):
                    for d in range(8):
                        last = t_.matmul(ps_o[j][:, 512 * oc:512 * (oc + 1)],
                                         ctxT[:, S * d + 128 * t:S * d + 128 * (t + 1)],
                                         WoT[:, 1024 * d + 512 * oc:1024 * d + 512 * (oc + 1)],
                                         start=(d == 0), stop=(d == 7))
                return last
            T("tensor", M("omm"), deps, em_omm)
            def em_res(v, i=i, j=j):
                C = lambda ins: P.ch("vector", v, ins)
                C(v.tensor_tensor(scr[i][:, :], xin[i][:, :], ps_o[j][:, :], ALU.add))
                C(v.tensor_tensor(xo[i][:, :], scr[i][:, :], scr[i][:, :], ALU.mult))
                C(v.reduce_sum(st[i][:, 1:2], xo[i][:, :], axis=AX.X))
                C(v.reduce_sum(st[i][:, 0:1], scr[i][:, :], axis=AX.X))
                C(v.tensor_scalar(st[i][:, 0:1], st[i][:, 0:1], 1.0 / 1024.0, None, ALU.mult))
                C(v.tensor_scalar(st[i][:, 2:3], st[i][:, 1:2], 1.0 / 1024.0, None, ALU.mult))
                C(v.tensor_scalar(st[i][:, 3:4], st[i][:, 0:1], st[i][:, 0:1], None, ALU.mult))
                return v.tensor_tensor(st[i][:, 4:5], st[i][:, 2:3], st[i][:, 3:4], ALU.subtract)
            T("vector", M("res"), [M("omm"), M("xin")], em_res)
            def em_rstd(s, i=i):
                P.ch("scalar", s, s.activation(st[i][:, 5:6], st[i][:, 4:5], AF.Ln, bias=epscol[:, :]))
                return s.activation(st[i][:, 6:7], st[i][:, 5:6], AF.Exp, scale=-0.5)
            T("scalar", M("rstd"), [M("res"), "c4"], em_rstd)
            def em_lno(v, i=i):
                P.ch("vector", v, v.tensor_scalar(st[i][:, 7:8], st[i][:, 0:1], st[i][:, 6:7],
                                                  -1.0, ALU.mult, ALU.mult))
                return v.tensor_scalar(xo[i][:, :], scr[i][:, :], st[i][:, 6:7], st[i][:, 7:8],
                                       ALU.mult, ALU.add)
            T("vector", M("lno"), [M("rstd")], em_lno)
            T("sync", M("xst"), [M("lno")],
              lambda sy, i=i, t=t, b=b: sy.dma_start(dst_t[b, t, :, :], xo[i][:, :]), dma=True)
        T("sync", bb + "fin", [f"{bb}e_xst@{t}" for t in range(max(0, NTB - NB), NTB)],
          lambda sy: sy.nop())
        P.flush()
        sE.close()
        sDE.close()
    tes.close()


def host_prep_p3(inp):
    w = {}
    w["Wq"] = inp["L1_attn_Wq"].astype(np.float32)
    Wqr = inp["L1_attn_Wqr"].astype(np.float32)
    pad = np.zeros((1024, 1024), np.float32)
    for h in range(16):
        pad[:, 64 * h:64 * h + 32] = Wqr[:, 32 * h:32 * (h + 1)]
    w["Wqr_pad"] = pad
    w["Wkr"] = inp["L1_attn_Wkr"].astype(np.float32)
    w["Wkvd"] = inp["L1_attn_Wkvd"].astype(np.float32)
    w["Wku"] = inp["L1_attn_Wku"].astype(np.float32)
    w["Wvu"] = inp["L1_attn_Wvu"].astype(np.float32)
    w["Wo"] = inp["L1_attn_Wo"].astype(np.float32)
    w["lobo_log"] = inp["L1_attn_lobo_log"].astype(np.float32)
    return w


def host_rope_consts(S):
    half = 16
    inv = 1.0 / 10000.0 ** (np.arange(half, dtype=np.float64) / half)
    ang = np.arange(S, dtype=np.float64)[:, None] * inv[None, :]
    cos = np.cos(ang).astype(np.float32)
    sin = np.sin(ang).astype(np.float32)
    NTB = S // 128
    c = {}
    c["cos16"] = cos.reshape(NTB, 128, 16).transpose(1, 0, 2).reshape(128, NTB * 16).copy()
    c["sin16"] = sin.reshape(NTB, 128, 16).transpose(1, 0, 2).reshape(128, NTB * 16).copy()
    cr = np.tile(cos, (1, 16)).reshape(NTB, 128, 256).transpose(1, 0, 2).reshape(128, NTB * 256)
    sr = np.tile(sin, (1, 16)).reshape(NTB, 128, 256).transpose(1, 0, 2).reshape(128, NTB * 256)
    c["cosrep"] = cr.copy()
    c["sinrep"] = sr.copy()
    negm = np.zeros((128, 4 * 512), np.float32)
    mt = np.zeros((128, 4 * 512), np.float32)
    for m in range(4):
        p = np.arange(128)[:, None]
        j = np.arange(512)[None, :]
        valid_s = j <= (m * 128 + p)
        negm[:, 512 * m:512 * (m + 1)] = np.where(valid_s, 0.0, -BIG)
        valid_t = j >= (m * 128 + p)
        mt[:, 512 * m:512 * (m + 1)] = np.where(valid_t, 1.0, 0.0)
    c["negmS"] = negm
    c["maskT"] = mt.astype(ml_dtypes.bfloat16)
    return c


def np_ref_p3(x, inp):
    xf = x.astype(np.float64)
    B, S, _ = xf.shape
    H, DH_, DR_ = 16, 64, 32

    def rope(z):
        d = z.shape[-1]
        t = z.shape[-2]
        half = d // 2
        inv = 1.0 / 10000.0 ** (np.arange(half) / half)
        ang = np.arange(t)[:, None] * inv[None, :]
        c_, s_ = np.cos(ang), np.sin(ang)
        z1, z2 = z[..., :half], z[..., half:]
        return np.concatenate([z1 * c_ - z2 * s_, z2 * c_ + z1 * s_], axis=-1)

    def sh(y, d):
        return y.reshape(B, S, H, d).transpose(0, 2, 1, 3)

    q_c = sh(xf @ inp["L1_attn_Wq"].astype(np.float64), DH_)
    q_r = rope(sh(xf @ inp["L1_attn_Wqr"].astype(np.float64), DR_))
    lat = xf @ inp["L1_attn_Wkvd"].astype(np.float64)
    k_c = sh(lat @ inp["L1_attn_Wku"].astype(np.float64), DH_)
    v = sh(lat @ inp["L1_attn_Wvu"].astype(np.float64), DH_)
    k_r = rope(xf @ inp["L1_attn_Wkr"].astype(np.float64))
    k_r = np.broadcast_to(k_r[:, None], (B, H, S, DR_))
    q = np.concatenate([q_c, q_r], -1)
    k = np.concatenate([k_c, k_r], -1)
    sc = np.einsum("bhqd,bhkd->bhqk", q, k) / np.sqrt(96.0)
    causal = np.tril(np.ones((S, S), bool))
    sc = np.where(causal[None, None], sc, -np.inf)
    m = sc.max(-1, keepdims=True)
    e = np.exp(sc - m)
    den = e.sum(-1, keepdims=True)
    Cv = np.exp(inp["L1_attn_lobo_log"].astype(np.float64)).reshape(1, H, 1, 1)
    attn = e / (den + Cv)
    y = np.einsum("bhqk,bhkd->bhqd", attn, v)
    y = y.transpose(0, 2, 1, 3).reshape(B, S, H * DH_)
    out = y @ inp["L1_attn_Wo"].astype(np.float64)
    t = xf + out
    mu = t.mean(-1, keepdims=True)
    var = t.var(-1, keepdims=True)
    return ((t - mu) / np.sqrt(var + EPS)).astype(np.float32)


def host_consts():
    c = {}
    c["ident"] = np.eye(128, dtype=ml_dtypes.bfloat16)
    c["ident_f"] = np.eye(128, dtype=np.float32)
    c["ones_row"] = np.ones((1, 128), dtype=ml_dtypes.bfloat16)
    c["ones_row_f"] = np.ones((1, 128), dtype=np.float32)
    c["epscol"] = np.full((128, 1), EPS, np.float32)
    return c


def host_prep_p2(inp):
    w = {}
    w["baseWT"] = np.ascontiguousarray(inp["L0_ffn_base_W"].T).astype(np.float32)
    A = np.zeros((1024, 112), np.float32)
    A[:, 0:16] = inp["L0_ffn_shared_A"].T
    for e in range(6):
        A[:, 16 * (e + 1):16 * (e + 2)] = inp["L0_ffn_routed_A"][e].T
    w["AallT"] = A
    Bm = np.zeros((112, 1024), np.float32)
    Bm[0:16, :] = inp["L0_ffn_shared_B"].T
    for e in range(6):
        Bm[16 * (e + 1):16 * (e + 2), :] = inp["L0_ffn_routed_B"][e].T
    w["Ball"] = Bm
    w["routerWT"] = np.ascontiguousarray(inp["L0_ffn_router_W"].T).astype(np.float32)
    w["base_b"] = inp["L0_ffn_base_b"].astype(np.float32)
    w["router_b"] = inp["L0_ffn_router_b"].astype(np.float32)
    return w


def np_ref_p2(x, inp):
    xf = x.astype(np.float64)
    base = xf @ inp["L0_ffn_base_W"].T.astype(np.float64) + inp["L0_ffn_base_b"]
    shared = (xf @ inp["L0_ffn_shared_A"].T.astype(np.float64)) @ inp["L0_ffn_shared_B"].T.astype(np.float64)
    logits = xf @ inp["L0_ffn_router_W"].T.astype(np.float64) + inp["L0_ffn_router_b"]
    e = np.exp(logits - logits.max(-1, keepdims=True))
    g = e / e.sum(-1, keepdims=True)
    idx = np.argsort(-g, axis=-1, kind="stable")[..., :2]
    sw = np.zeros_like(g)
    np.put_along_axis(sw, idx, np.take_along_axis(g, idx, -1), -1)
    sw = sw / (sw.sum(-1, keepdims=True) + 1e-6)
    h = np.einsum("...i,eri->...er", xf, inp["L0_ffn_routed_A"].astype(np.float64))
    ro = np.einsum("...er,eor->...eo", h, inp["L0_ffn_routed_B"].astype(np.float64))
    routed = np.einsum("...eo,...e->...o", ro, sw)
    t = xf + base + shared + routed
    mu = t.mean(-1, keepdims=True)
    var = t.var(-1, keepdims=True)
    return ((t - mu) / np.sqrt(var + EPS)).astype(np.float32)


FUSED = True
N_CORES = 8
B_CORE = 2
S_FULL = 2048


def _declare_inputs(nc, wnp):
    aps, hin = {}, {}
    for k, v in wnp.items():
        dt = BF16 if v.dtype == ml_dtypes.bfloat16 else F32
        aps[k] = nc.dram_tensor("w_" + k, list(v.shape), dt, kind="ExternalInput").ap()
        hin["w_" + k] = np.ascontiguousarray(v)
    return aps, hin


def kernel(**inputs):
    x = np.ascontiguousarray(inputs["L0_attn_hidden_states"].astype(np.float32, copy=False))
    shards = [np.ascontiguousarray(x[2 * c:2 * c + 2]) for c in range(N_CORES)]
    cfg = {"B": B_CORE, "S": S_FULL, "HID": 2816}

    consts = host_consts()
    rope = host_rope_consts(S_FULL)
    w_all = [host_prep_p1(inputs), host_prep_p2(inputs),
             {**host_prep_p3(inputs), **rope}, host_prep_p4(inputs)]
    builders = [build_p1, build_p2, build_p3, build_p4]

    if not FUSED:
        cur = shards
        for pi, (bld, wnp) in enumerate(zip(builders, w_all)):
            wnp = {**wnp, **consts}
            nc = bass.Bass("TRN2", target_bir_lowering=False)
            w_aps, hin = _declare_inputs(nc, wnp)
            src = nc.dram_tensor("xsrc", [B_CORE, S_FULL, 1024], F32, kind="ExternalInput").ap()
            dst = nc.dram_tensor("xdst", [B_CORE, S_FULL, 1024], F32, kind="ExternalOutput").ap()
            with contextlib.ExitStack() as es:
                P = Prog(nc, es, "mk")
                bld(nc, P, cfg, src, dst, w_aps, pre=f"p{pi+1}_")
            in_maps = [{**hin, "xsrc": cur[c]} for c in range(N_CORES)]
            res = run_bass_kernel_spmd(nc, in_maps, core_ids=list(range(N_CORES)))
            cur = [np.asarray(res.results[c]["xdst"], dtype=np.float32) for c in range(N_CORES)]
        return np.concatenate(cur, axis=0)

    nc = bass.Bass("TRN2", target_bir_lowering=False)
    wnp_merged = {}
    for wnp in w_all:
        wnp_merged.update(wnp)
    wnp_merged.update(consts)
    w_aps, hin = _declare_inputs(nc, wnp_merged)
    src = nc.dram_tensor("xsrc", [B_CORE, S_FULL, 1024], F32, kind="ExternalInput").ap()
    dst = nc.dram_tensor("xdst", [B_CORE, S_FULL, 1024], F32, kind="ExternalOutput").ap()
    xh = [nc.dram_tensor(f"xh{i}", [B_CORE, S_FULL, 1024], F32, kind="Internal").ap()
          for i in range(3)]
    hops = [(src, xh[0]), (xh[0], xh[1]), (xh[1], xh[2]), (xh[2], dst)]
    with contextlib.ExitStack() as es:
        P = Prog(nc, es, "mk")
        for pi, (bld, (s_, d_)) in enumerate(zip(builders, hops)):
            bld(nc, P, cfg, s_, d_, w_aps, pre=f"p{pi+1}_")
    in_maps = [{**hin, "xsrc": shards[c]} for c in range(N_CORES)]
    res = run_bass_kernel_spmd(nc, in_maps, core_ids=list(range(N_CORES)))
    return np.concatenate([np.asarray(res.results[c]["xdst"], dtype=np.float32)
                           for c in range(N_CORES)], axis=0)


# version 12
# speedup vs baseline: 1.0183x; 1.0183x over previous
import contextlib
import numpy as np
import ml_dtypes
import concourse.bass as bass
import concourse.mybir as mybir
from concourse.bass_utils import run_bass_kernel_spmd

F32 = mybir.dt.float32
BF16 = mybir.dt.bfloat16
AF = mybir.ActivationFunctionType
ALU = mybir.AluOpType
AX = mybir.AxisListType

D = 1024
NH = 16
HD = 64
DR = 32
DKV = 256
EPS = 1e-5
BIG = 1.0e30

ENGINES = ("tensor", "vector", "scalar", "gpsimd", "sync")


class Task:
    __slots__ = ("eng", "tag", "deps", "emit", "dma", "idx")

    def __init__(self, eng, tag, deps, emit, dma):
        self.eng, self.tag, self.deps, self.emit, self.dma = eng, tag, deps, emit, dma
        self.idx = None


NDMASEM = 8


class Prog:

    def __init__(self, nc, es, name="P"):
        self.nc = nc
        self.name = name
        self.tasks = {}
        self.order = {e: [] for e in ENGINES}
        self.sems = {}
        self.cnt = {}
        for e in ENGINES:
            self.sems[e] = es.enter_context(nc.semaphore(f"{name}_{e}"))
            self.cnt[e] = 0
        for q in ("g", "s"):
            for i in range(NDMASEM):
                self.sems[f"{q}{i}"] = es.enter_context(nc.semaphore(f"{name}_{q}dma{i}"))
                self.cnt[f"{q}{i}"] = 0
        self.dma_i = {"g": 0, "s": 0}
        self.chsem = {}
        self.chcnt = {}
        for e in ENGINES:
            self.chsem[e] = es.enter_context(nc.semaphore(f"{name}_ch_{e}"))
            self.chcnt[e] = 0
        self.aliases = {}

    def alias(self, tag, deps):
        assert tag not in self.tasks and tag not in self.aliases, tag
        self.aliases[tag] = list(deps)
        return tag

    def _resolve(self, deps):
        out = []
        for d in deps:
            if d in self.aliases:
                out.extend(self._resolve(self.aliases[d]))
            else:
                out.append(d)
        return out

    def ch(self, e, eng, instr):
        self.chcnt[e] += 1
        instr.then_inc(self.chsem[e], 1)
        eng.wait_ge(self.chsem[e], self.chcnt[e])
        return instr

    def task(self, eng, tag, deps, emit, dma=False):
        assert tag not in self.tasks, tag
        t = Task(eng, tag, list(deps), emit, dma)
        if dma:
            q = "g" if eng == "gpsimd" else "s"
            k = f"{q}{self.dma_i[q] % NDMASEM}"
            self.dma_i[q] += 1
            self.cnt[k] += 16
            t.idx = (k, self.cnt[k])
        else:
            self.cnt[eng] += 1
            t.idx = (eng, self.cnt[eng])
        self.tasks[tag] = t
        self.order[eng].append(t)
        return tag

    def flush(self):
        nc = self.nc
        with nc.Block() as blk:
            for e in ENGINES:
                if not self.order[e]:
                    continue

                def body(eng, e=e):
                    hw = {k: 0 for k in self.sems}
                    for t in self.order[e]:
                        if t.dma:
                            sem_key, val = t.idx
                            if val > 16 and hw[sem_key] < val - 16:
                                eng.wait_ge(self.sems[sem_key], val - 16)
                                hw[sem_key] = val - 16
                        for dtag in self._resolve(t.deps):
                            dt_ = self.tasks[dtag]
                            sem_key, val = dt_.idx
                            if hw[sem_key] < val:
                                eng.wait_ge(self.sems[sem_key], val)
                                hw[sem_key] = val
                        instr = t.emit(eng)
                        assert instr is not None, f"task {t.tag} emit returned None"
                        sem_key, val = t.idx
                        instr.then_inc(self.sems[sem_key], 16 if t.dma else 1)

                getattr(blk, e)(body)
        self.order = {e: [] for e in ENGINES}

    def emit(self):
        self.flush()


def ap2(t):
    return t.ap() if hasattr(t, "ap") and callable(getattr(t, "ap", None)) else t



def dma_T(g, dst2d, src2d, o):
    return g.dma_start(dst2d.rearrange("p (dt o) -> p dt o", o=o),
                       src2d.rearrange("(dt p) o -> p dt o", p=128))

def ln_emit(v, x_ap, out_ap, wsq, c_mu, c_var, c_rstd, extra_in=None, extra_in2=None):
    raise NotImplementedError


def build_p2(nc, P, cfg, src, dst, w, pre="p2"):
    B, S = cfg["B"], cfg["S"]
    NT = B * S // 128
    tes = contextlib.ExitStack()
    a = lambda n, sh, dt: tes.enter_context(nc.sbuf_tensor(pre + n, sh, dt)).ap()
    psum = lambda n, sh, dt: tes.enter_context(nc.psum_tensor(pre + n, sh, dt)).ap()
    _task = P.task
    P_task = lambda eng, tag, deps, emit, dma=False: _task(
        eng, pre + tag, [pre + d for d in deps], emit, dma)

    baseWT = a("baseWT", [128, 8 * 1024], BF16)
    AallT = a("AallT", [128, 8 * 112], BF16)
    Ball = a("Ball", [112, 1024], BF16)
    routerWT = a("routerWT", [128, 8 * 6], BF16)
    basebrow = a("basebrow", [1, 1024], BF16)
    brrow = a("brrow", [1, 6], BF16)
    ones_bf = a("ones", [1, 128], BF16)
    ident = a("ident", [128, 128], BF16)
    epscol = a("epscol", [128, 1], F32)

    NB = 3
    xin = [a(f"x{i}", [128, 1024], F32) for i in range(NB)]
    xbf = [a(f"xbf{i}", [128, 1024], BF16) for i in range(NB)]
    xT = [a(f"xT{i}", [128, 1024], BF16) for i in range(NB)]
    hwbf = [a(f"hw{i}", [128, 112], BF16) for i in range(NB)]
    hwT = [a(f"hwT{i}", [112, 128], BF16) for i in range(NB)]
    wgt = [a(f"wgt{i}", [128, 6], F32) for i in range(NB)]
    esb = [a(f"esb{i}", [128, 6], F32) for i in range(NB)]
    xout = [a(f"xo{i}", [128, 1024], F32) for i in range(NB)]
    scr = [a(f"scr{i}", [128, 1024], F32) for i in range(NB)]
    st = [a(f"st{i}", [128, 8], F32) for i in range(NB)]

    ps_tr = [psum("pstr", [128, 1024], BF16)] * 2
    ps_l = [psum("psl", [128, 6], F32)] * 2
    ps_hw = [psum("pshw", [128, 112], F32)] * 2
    ps_hwT = [psum("pshwT", [112, 128], BF16)] * 2
    ps_o = [psum(f"pso{i}", [128, 1024], F32) for i in range(2)]

    src_t = src.rearrange("b (t p) d -> (b t) p d", p=128)
    dst_t = dst.rearrange("b (t p) d -> (b t) p d", p=128)

    P_task("gpsimd", "wl0", [], lambda g: dma_T(g, baseWT, w["baseWT"], 1024), dma=True)
    P_task("gpsimd", "wl1", [], lambda g: dma_T(g, AallT, w["AallT"], 112), dma=True)
    P_task("gpsimd", "wl2", [], lambda g: g.dma_start(Ball[:, :], w["Ball"][:, :]), dma=True)
    P_task("gpsimd", "wl3", [], lambda g: dma_T(g, routerWT, w["routerWT"], 6), dma=True)
    P_task("gpsimd", "wl4", [], lambda g: g.dma_start(basebrow[:, :], w["base_b"][None, :]), dma=True)
    P_task("gpsimd", "wl5", [], lambda g: g.dma_start(brrow[:, :], w["router_b"][None, :]), dma=True)
    P_task("gpsimd", "wl6", [], lambda g: g.dma_start(ones_bf[:, :], w["ones_row"][:, :]), dma=True)
    P_task("gpsimd", "wl7", [], lambda g: g.dma_start(ident[:, :], w["ident"][:, :]), dma=True)
    P_task("gpsimd", "wl8", [], lambda g: g.dma_start(epscol[:, :], w["epscol"][:, :]), dma=True)
    WL = [f"wl{i}" for i in range(9)]

    for t in range(NT):
        i = t % NB
        j = t % 2
        pv = lambda tag, tt=t: f"{tag}@{tt}"
        me = lambda tag, tt=t: f"{tag}@{tt}"

        deps = [pv("resid", t - NB)] if t >= NB else []
        P_task("sync", me("xin"), deps,
               lambda sy, i=i, t=t: sy.dma_start(xin[i][:, :], src_t[t, :, :]), dma=True)
        P_task("scalar", me("cast"), [me("xin")],
               lambda s, i=i: s.copy(xbf[i][:, :], xin[i][:, :]))
        deps = [me("cast")] + ([pv("xTev", t - 1)] if t >= 1 else [])
        def em_tr(t_, i=i, j=j):
            last = None
            for k in range(8):
                last = t_.transpose(ps_tr[j][:, 128 * k:128 * (k + 1)],
                                    xbf[i][:, 128 * k:128 * (k + 1)], ident[:, :])
            return last
        P_task("tensor", me("tr"), deps + [WL[7]], em_tr)
        P_task("scalar", me("xTev"), [me("tr")],
               lambda s, i=i, j=j: s.copy(xT[i][:, :], ps_tr[j][:, :]))
        deps = [me("xTev"), WL[3], WL[5], WL[6]] + ([pv("sexp", t - 1)] if t >= 1 else [])
        def em_rl(t_, i=i, j=j):
            for d in range(8):
                t_.matmul(ps_l[j][:, :], xT[i][:, 128 * d:128 * (d + 1)],
                          routerWT[:, 6 * d:6 * (d + 1)], start=(d == 0), stop=False)
            return t_.matmul(ps_l[j][:, :], ones_bf[:, :], brrow[:, :], start=False, stop=True)
        P_task("tensor", me("rl"), deps, em_rl)
        smax_deps = [me("rl")] + ([pv("xout", t - NB)] if t >= NB else [])
        P_task("vector", me("smax"), smax_deps,
               lambda v, i=i, j=j: v.reduce_max(st[i][:, 0:1], ps_l[j][:, :], axis=AX.X,
                                                negate=True))
        P_task("scalar", me("sexp"), [me("smax")],
               lambda s, i=i, j=j: s.activation(esb[i][:, :], ps_l[j][:, :], AF.Exp,
                                                bias=st[i][:, 0:1], scale=1.0))
        def em_wgt(v, i=i):
            C = lambda ins: P.ch("vector", v, ins)
            C(v.reduce_sum(st[i][:, 1:2], esb[i][:, :], axis=AX.X))
            C(v.reciprocal(st[i][:, 2:3], st[i][:, 1:2]))
            C(v.tensor_scalar(wgt[i][:, :], esb[i][:, :], st[i][:, 2:3], None, ALU.mult))
            C(v.reduce_max(st[i][:, 3:4], wgt[i][:, :], axis=AX.X))
            C(v.tensor_scalar(esb[i][:, :], wgt[i][:, :], st[i][:, 3:4], None, ALU.is_ge))
            C(v.tensor_scalar(esb[i][:, :], esb[i][:, :], BIG, None, ALU.mult))
            C(v.tensor_tensor(esb[i][:, :], wgt[i][:, :], esb[i][:, :], ALU.subtract))
            C(v.reduce_max(st[i][:, 4:5], esb[i][:, :], axis=AX.X))
            C(v.tensor_scalar(esb[i][:, :], wgt[i][:, :], st[i][:, 4:5], None, ALU.is_ge))
            C(v.tensor_tensor(wgt[i][:, :], wgt[i][:, :], esb[i][:, :], ALU.mult))
            C(v.reduce_sum(st[i][:, 5:6], wgt[i][:, :], axis=AX.X))
            C(v.tensor_scalar(st[i][:, 5:6], st[i][:, 5:6], 1e-6, None, ALU.add))
            C(v.reciprocal(st[i][:, 6:7], st[i][:, 5:6]))
            return v.tensor_scalar(wgt[i][:, :], wgt[i][:, :], st[i][:, 6:7], None, ALU.mult)
        wgt_deps = [me("sexp")] + ([pv("xout", t - NB)] if t >= NB else [])
        P_task("vector", me("wgt"), wgt_deps, em_wgt)
        deps = [me("xTev"), WL[0], WL[1], WL[4]]
        if t >= 1:
            deps += [pv("hwev", t - 1)]
        if t >= 2:
            deps += [pv("resid", t - 2)]
        def em_base(t_, i=i, j=j):
            for oc in range(2):
                for d in range(8):
                    t_.matmul(ps_o[j][:, 512 * oc:512 * (oc + 1)],
                              xT[i][:, 128 * d:128 * (d + 1)],
                              baseWT[:, 1024 * d + 512 * oc:1024 * d + 512 * (oc + 1)],
                              start=(d == 0), stop=False)
                t_.matmul(ps_o[j][:, 512 * oc:512 * (oc + 1)], ones_bf[:, :],
                          basebrow[:, 512 * oc:512 * (oc + 1)], start=False, stop=False)
            last = None
            for d in range(8):
                last = t_.matmul(ps_hw[j][:, :], xT[i][:, 128 * d:128 * (d + 1)],
                                 AallT[:, 112 * d:112 * (d + 1)], start=(d == 0), stop=(d == 7))
            return last
        P_task("tensor", me("base"), deps, em_base)
        def em_hwev(s, i=i, j=j):
            s.copy(hwbf[i][:, 0:16], ps_hw[j][:, 0:16])
            last = None
            for e in range(6):
                last = s.activation(hwbf[i][:, 16 * (e + 1):16 * (e + 2)],
                                    ps_hw[j][:, 16 * (e + 1):16 * (e + 2)],
                                    AF.Copy, scale=wgt[i][:, e:e + 1])
            return last
        P_task("scalar", me("hwev"), [me("base"), me("wgt")], em_hwev)
        deps = [me("hwev")] + ([pv("hwTev", t - 1)] if t >= 1 else [])
        P_task("tensor", me("hwtr"), deps,
               lambda t_, i=i, j=j: t_.transpose(ps_hwT[j][:, :], hwbf[i][:, :], ident[:, :]))
        P_task("scalar", me("hwTev"), [me("hwtr")],
               lambda s, i=i, j=j: s.copy(hwT[i][:, :], ps_hwT[j][:, :]))
        def em_ball(t_, i=i, j=j):
            last = None
            for oc in range(2):
                last = t_.matmul(ps_o[j][:, 512 * oc:512 * (oc + 1)], hwT[i][:, :],
                                 Ball[:, 512 * oc:512 * (oc + 1)], start=False, stop=True)
            return last
        P_task("tensor", me("ball"), [me("hwTev"), WL[2], me("base")], em_ball)
        def em_res(v, i=i, j=j):
            C = lambda ins: P.ch("vector", v, ins)
            C(v.tensor_tensor(scr[i][:, :], xin[i][:, :], ps_o[j][:, :], ALU.add))
            C(v.reduce_sum(st[i][:, 0:1], scr[i][:, :], axis=AX.X))
            C(v.tensor_scalar(st[i][:, 0:1], st[i][:, 0:1], 1.0 / 1024.0, None, ALU.mult))
            C(v.tensor_tensor(xout[i][:, :], scr[i][:, :], scr[i][:, :], ALU.mult))
            return v.reduce_sum(st[i][:, 1:2], xout[i][:, :], axis=AX.X)
        res_deps = [me("ball")] + ([pv("xst", t - NB)] if t >= NB else [])
        P_task("vector", me("resid"), res_deps, em_res)
        def em_rstd(s, i=i):
            s.mul(st[i][:, 2:3], st[i][:, 1:2], 1.0 / 1024.0)
            return s.activation(st[i][:, 3:4], st[i][:, 0:1], AF.Square)
        P_task("scalar", me("rstd0"), [me("resid")], em_rstd)
        def em_rstd1(v, i=i):
            return v.tensor_tensor(st[i][:, 4:5], st[i][:, 2:3], st[i][:, 3:4], ALU.subtract)
        P_task("vector", me("rstd1"), [me("rstd0")], em_rstd1)
        def em_rstd2(s, i=i):
            P.ch("scalar", s, s.activation(st[i][:, 5:6], st[i][:, 4:5], AF.Ln, bias=epscol[:, :]))
            return s.activation(st[i][:, 6:7], st[i][:, 5:6], AF.Exp, scale=-0.5)
        P_task("scalar", me("rstd2"), [me("rstd1"), WL[8]], em_rstd2)
        def em_xout(v, i=i):
            P.ch("vector", v, v.tensor_scalar(st[i][:, 7:8], st[i][:, 0:1], st[i][:, 6:7],
                                              -1.0, ALU.mult, ALU.mult))
            return v.tensor_scalar(xout[i][:, :], scr[i][:, :], st[i][:, 6:7], st[i][:, 7:8],
                                   ALU.mult, ALU.add)
        P_task("vector", me("xout"), [me("rstd2")], em_xout)
        P_task("sync", me("xst"), [me("xout")],
               lambda sy, i=i, t=t: sy.dma_start(dst_t[t, :, :], xout[i][:, :]), dma=True)

    P_task("sync", "fin", [f"xst@{t}" for t in range(max(0, NT - NB), NT)],
           lambda sy: sy.nop())
    P.flush()
    tes.close()



def build_p1(nc, P, cfg, src, dst, w, pre="p1"):
    B, S = cfg["B"], cfg["S"]
    NTB = S // 128
    NQ = S // 512
    NK = S // 128
    tes = contextlib.ExitStack()
    a = lambda n, sh, dt: tes.enter_context(nc.sbuf_tensor(pre + n, sh, dt)).ap()
    _task = P.task
    T = lambda eng, tag, deps, emit, dma=False: _task(
        eng, pre + tag, [pre + d for d in deps], emit, dma)

    ones_bf = a("ones", [1, 128], BF16)
    ones_f = a("onesf", [1, 128], F32)
    ident = a("ident", [128, 128], BF16)
    identf = a("identf", [128, 128], F32)
    epscol = a("epscol", [128, 1], F32)
    bqkvT = a("bqkvT", [128, 24], F32)
    bqkv_row = a("bqr", [1, 3072], BF16)
    bout_row = a("bor", [1, 1024], BF16)
    headsel = a("hsel", [128, 8 * 16], BF16)
    QT = a("QT", [128, 8 * S], BF16)
    KT = a("KT", [128, 8 * S], BF16)
    Vb = a("Vb", [128, NK * 1040], BF16)
    drow = a("drow", [1, S], F32)
    rrow = a("rrow", [1, S], F32)
    negCall = a("negCall", [128, 16], F32)
    negCrow = a("negCrow", [1, 16], F32)
    nqk = a("nqk", [16, 2], F32)
    ncmax = a("ncmax", [16, 8], F32)
    NB = 3

    Vb4 = Vb.rearrange("p (t g c) -> p t g c", g=16, c=65)

    src_t = src.rearrange("b (t p) d -> b t p d", p=128)
    dst_t = dst.rearrange("b (t p) d -> b t p d", p=128)

    T("gpsimd", "c0", [], lambda g: g.dma_start(ones_bf[:, :], w["ones_row"][:, :]), dma=True)
    T("gpsimd", "c1", [], lambda g: g.dma_start(ones_f[:, :], w["ones_row_f"][:, :]), dma=True)
    T("gpsimd", "c2", [], lambda g: g.dma_start(ident[:, :], w["ident"][:, :]), dma=True)
    T("gpsimd", "c3", [], lambda g: g.dma_start(identf[:, :], w["ident_f"][:, :]), dma=True)
    T("gpsimd", "c4", [], lambda g: g.dma_start(epscol[:, :], w["epscol"][:, :]), dma=True)
    T("gpsimd", "c7", [], lambda g: g.dma_start(bqkvT[:, :], w["bqkvT"][:, :]), dma=True)
    T("gpsimd", "c8", [], lambda g: g.dma_start(bqkv_row[:, :], w["b_qkv"][None, :]), dma=True)
    T("gpsimd", "c9", [], lambda g: g.dma_start(bout_row[:, :], w["b_out"][None, :]), dma=True)
    T("gpsimd", "c10", [], lambda g: g.dma_start(headsel[:, :], w["headsel"][:, :]), dma=True)

    for b in range(B):
        bb = f"b{b}_"
        bes = contextlib.ExitStack()
        ab = lambda n, sh, dt: bes.enter_context(nc.sbuf_tensor(pre + bb + n, sh, dt)).ap()

        sA = contextlib.ExitStack()
        aA = lambda n, sh, dt: sA.enter_context(nc.sbuf_tensor(pre + bb + n, sh, dt)).ap()
        xnT = ab("xnT", [128, 8 * S], BF16)
        gamma_bc = aA("gbc", [128, 1024], F32)
        beta_bc = aA("bbc", [128, 1024], F32)
        xin = [aA(f"x{i}", [128, 1024], F32) for i in range(NB)]
        xnb = [aA(f"xnb{i}", [128, 1024], BF16) for i in range(NB)]
        scr = [aA(f"scr{i}", [128, 1024], F32) for i in range(NB)]
        st = [aA(f"st{i}", [128, 8], F32) for i in range(NB)]
        ps_tr = sA.enter_context(nc.psum_tensor(pre + bb + "pstr", [128, 1024], BF16)).ap()
        ps_fil = sA.enter_context(nc.psum_tensor(pre + bb + "psfil", [128, 128], F32)).ap()
        T("gpsimd", bb + "gbc", [], lambda g: g.dma_start(gamma_bc[:, :], w["gamma_bc"][:, :]), dma=True)
        T("gpsimd", bb + "bbc", [], lambda g: g.dma_start(beta_bc[:, :], w["beta_bc"][:, :]), dma=True)
        T("gpsimd", bb + "vset", [],
          lambda g: g.memset(Vb4[:, :, :, 64:65], 1.0))
        xnT3 = xnT.rearrange("p (dt s) -> p dt s", s=S)
        def em_filA(t_):
            last = None
            for _ in range(24):
                last = t_.matmul(ps_fil[:, :], ident[:, :], ident[:, :],
                                 start=True, stop=True)
            return last
        T("tensor", bb + "fil@pre", ["c2"], em_filA)
        for t in range(NTB):
            i = t % NB
            M = lambda tag, tt=t: f"{bb}{tag}@{tt}"
            deps = [M("lnf", t - NB), M("xnev", t - NB)] if t >= NB else []
            T("sync", M("xin"), deps,
              lambda sy, i=i, t=t, b=b: sy.dma_start(xin[i][:, :], src_t[b, t, :, :]), dma=True)
            def em_stats(v, i=i):
                C = lambda ins: P.ch("vector", v, ins)
                C(v.tensor_tensor(scr[i][:, :], xin[i][:, :], xin[i][:, :], ALU.mult))
                C(v.reduce_sum(st[i][:, 1:2], scr[i][:, :], axis=AX.X))
                C(v.reduce_sum(st[i][:, 0:1], xin[i][:, :], axis=AX.X))
                C(v.tensor_scalar(st[i][:, 0:1], st[i][:, 0:1], 1.0 / 1024.0, None, ALU.mult))
                C(v.tensor_scalar(st[i][:, 2:3], st[i][:, 1:2], 1.0 / 1024.0, None, ALU.mult))
                C(v.tensor_scalar(st[i][:, 3:4], st[i][:, 0:1], st[i][:, 0:1], None, ALU.mult))
                return v.tensor_tensor(st[i][:, 4:5], st[i][:, 2:3], st[i][:, 3:4], ALU.subtract)
            T("vector", M("stats"), [M("xin")], em_stats)
            def em_rstd(s, i=i):
                P.ch("scalar", s, s.activation(st[i][:, 5:6], st[i][:, 4:5], AF.Ln, bias=epscol[:, :]))
                return s.activation(st[i][:, 6:7], st[i][:, 5:6], AF.Exp, scale=-0.5)
            T("scalar", M("rstd"), [M("stats"), "c4"], em_rstd)
            def em_lnf(v, i=i):
                C = lambda ins: P.ch("vector", v, ins)
                C(v.tensor_scalar(st[i][:, 7:8], st[i][:, 0:1], st[i][:, 6:7], -1.0,
                                  ALU.mult, ALU.mult))
                C(v.tensor_scalar(scr[i][:, :], xin[i][:, :], st[i][:, 6:7], st[i][:, 7:8],
                                  ALU.mult, ALU.add))
                C(v.tensor_tensor(scr[i][:, :], scr[i][:, :], gamma_bc[:, :], ALU.mult))
                return v.tensor_tensor(xnb[i][:, :], scr[i][:, :], beta_bc[:, :], ALU.add)
            T("vector", M("lnf"), [M("rstd"), bb + "gbc", bb + "bbc"], em_lnf)
            deps = [M("lnf"), "c2"] + ([M("xnev", t - 1)] if t >= 1 else [])
            def em_tr(t_, i=i):
                last = None
                for k in range(8):
                    last = t_.transpose(ps_tr[:, 128 * k:128 * (k + 1)],
                                        xnb[i][:, 128 * k:128 * (k + 1)], ident[:, :])
                return last
            T("tensor", M("tr"), deps, em_tr)
            if t < NTB - 1:
                def em_fil(t_):
                    last = None
                    for _ in range(16):
                        last = t_.matmul(ps_fil[:, :], ident[:, :], ident[:, :],
                                         start=True, stop=True)
                    return last
                T("tensor", M("fil"), ["c2"], em_fil)
            T("scalar", M("xnev"), [M("tr")],
              lambda s, t=t: s.copy(xnT3[:, :, 128 * t:128 * (t + 1)],
                                    ps_tr.rearrange("p (dt c) -> p dt c", c=128)))
        P.flush()
        sA.close()

        sBv = contextlib.ExitStack()
        WvT = sBv.enter_context(nc.sbuf_tensor(pre + bb + "WvT", [128, 8 * 1024], BF16)).ap()
        ps_v = [sBv.enter_context(nc.psum_tensor(pre + bb + f"psv{i}", [128, 1024], F32)).ap()
                for i in range(2)]
        lastA = f"{bb}xnev@{NTB-1}"
        T("gpsimd", bb + "wv", [], lambda g: g.dma_start(
            WvT.rearrange("p (dt o) -> p dt o", o=1024),
            w["WqkvT"][:, 2048:3072].rearrange("(dt p) o -> p dt o", p=128)), dma=True)
        for tt in range(NTB):
            M = lambda tag, ttt=tt: f"{bb}{tag}@{ttt}"
            deps = [lastA, bb + "wv", "c8"] + ([M("vev", tt - 2)] if tt >= 2 else [])
            def em_vmm(t_, tt=tt):
                last = None
                for oc in range(2):
                    for d in range(8):
                        t_.matmul(ps_v[tt % 2][:, 512 * oc:512 * (oc + 1)],
                                  xnT3[:, d, 128 * tt:128 * (tt + 1)],
                                  WvT[:, 1024 * d + 512 * oc:1024 * d + 512 * (oc + 1)],
                                  start=(d == 0), stop=False)
                    last = t_.matmul(ps_v[tt % 2][:, 512 * oc:512 * (oc + 1)], ones_bf[:, :],
                                     bqkv_row[:, 2048 + 512 * oc:2048 + 512 * (oc + 1)],
                                     start=False, stop=True)
                return last
            T("tensor", M("vmm"), deps, em_vmm)
            T("scalar", M("vev"), [M("vmm"), bb + "vset"],
              lambda s, tt=tt: s.copy(Vb4[:, tt, :, 0:64],
                                      ps_v[tt % 2].rearrange("p (g dd) -> p g dd", dd=64)))
        P.flush()
        sBv.close()

        sBq = contextlib.ExitStack()
        WqkT = sBq.enter_context(nc.sbuf_tensor(pre + bb + "WqkT", [128, 8 * 2048], BF16)).ap()
        NQH = NQ if NQ == 1 else NQ // 2
        ps_qk = [sBq.enter_context(nc.psum_tensor(pre + bb + f"psqk{i}", [128, 512 * NQH],
                                                  F32)).ap() for i in range(2)]
        T("gpsimd", bb + "wqk", [], lambda g: g.dma_start(
            WqkT.rearrange("p (dt o) -> p dt o", o=2048),
            w["WqkvT"][:, 0:2048].rearrange("(dt p) o -> p dt o", p=128)), dma=True)
        uq = 0
        for ot in range(16):
            tgt = QT if ot < 8 else KT
            for hf in range(2 if NQ >= 2 else 1):
                M = lambda tag, o=ot, f_=hf: f"{bb}{tag}@{o}_{f_}"
                pz = uq % 2
                deps = [lastA, bb + "wqk", "c7"]
                if uq >= 2:
                    po, pf = (uq - 2) // (2 if NQ >= 2 else 1), (uq - 2) % (2 if NQ >= 2 else 1)
                    deps.append(f"{bb}qkev@{po}_{pf}")
                def em_qkmm(t_, ot=ot, hf=hf, pz=pz):
                    last = None
                    for qc in range(hf * NQH, (hf + 1) * NQH):
                        for d in range(8):
                            last = t_.matmul(
                                ps_qk[pz][:, 512 * (qc - hf * NQH):512 * (qc - hf * NQH + 1)],
                                WqkT[:, 2048 * d + 128 * ot:2048 * d + 128 * (ot + 1)],
                                xnT3[:, d, 512 * qc:512 * (qc + 1)],
                                start=(d == 0), stop=(d == 7))
                    return last
                T("tensor", M("qkmm"), deps, em_qkmm)
                def em_qkev(s, ot=ot, tgt=tgt, hf=hf, pz=pz):
                    return s.activation(
                        tgt[:, S * (ot % 8) + 512 * NQH * hf:S * (ot % 8) + 512 * NQH * (hf + 1)],
                        ps_qk[pz][:, 0:512 * NQH],
                        AF.Identity, bias=bqkvT[:, ot:ot + 1], scale=1.0)
                T("scalar", M("qkev"), [M("qkmm")], em_qkev)
                uq += 1
        qkev_last = f"{bb}qkev@15_{1 if NQ >= 2 else 0}"
        P.flush()
        sBq.close()
        bes.close()

        sC = contextlib.ExitStack()
        sqb = sC.enter_context(nc.sbuf_tensor(pre + bb + "sqb", [128, 8 * 512], BF16)).ap()
        ps_n = sC.enter_context(nc.psum_tensor(pre + bb + "psn", [128, 512], F32)).ap()
        ps_t16 = sC.enter_context(nc.psum_tensor(pre + bb + "pst16", [1, 16], F32)).ap()
        NCH = S // 512
        for which, buf in (("q", QT), ("k", KT)):
            for ch in range(NCH):
                M = f"{bb}n{which}{ch}"
                def em_sq(v, buf=buf, ch=ch):
                    last = None
                    for o in range(8):
                        last = v.tensor_tensor(sqb[:, 512 * o:512 * (o + 1)],
                                               buf[:, S * o + 512 * ch:S * o + 512 * (ch + 1)],
                                               buf[:, S * o + 512 * ch:S * o + 512 * (ch + 1)],
                                               ALU.mult)
                    return last
                hfl = 1 if NQ >= 2 else 0
                dq = [f"{bb}qkev@{(o if which == 'q' else o + 8)}_{hfl}" for o in range(8)]
                dq += [f"{bb}qkev@{(o if which == 'q' else o + 8)}_0" for o in range(8)]
                if ch >= 1:
                    dq += [f"{bb}n{which}{ch-1}_red"]
                T("vector", M + "_sq", dq, em_sq)
                def em_nmm(t_, ch=ch):
                    last = None
                    for o in range(8):
                        last = t_.matmul(ps_n[0:16, :], headsel[:, 16 * o:16 * (o + 1)],
                                         sqb[:, 512 * o:512 * (o + 1)],
                                         start=(o == 0), stop=(o == 7))
                    return last
                T("tensor", M + "_mm", [M + "_sq", "c10"], em_nmm)
                cidx = ch if which == "q" else NCH + ch
                T("vector", M + "_red", [M + "_mm"],
                  lambda v, cidx=cidx: v.reduce_max(ncmax[:, cidx:cidx + 1], ps_n[0:16, :], axis=AX.X))
        def em_nfin(v):
            C = lambda ins: P.ch("vector", v, ins)
            C(v.reduce_max(nqk[:, 0:1], ncmax[:, 0:NCH], axis=AX.X))
            C(v.reduce_max(nqk[:, 1:2], ncmax[:, NCH:2 * NCH], axis=AX.X))
            return v.tensor_tensor(nqk[:, 0:1], nqk[:, 0:1], nqk[:, 1:2], ALU.mult)
        T("vector", bb + "nfin", [f"{bb}nq{NCH-1}_red", f"{bb}nk{NCH-1}_red"], em_nfin)
        T("scalar", bb + "nsqrt", [bb + "nfin"],
          lambda s: s.activation(nqk[:, 1:2], nqk[:, 0:1], AF.Sqrt, scale=1.0))
        T("vector", bb + "nscale", [bb + "nsqrt"],
          lambda v: v.tensor_scalar(nqk[:, 0:1], nqk[:, 1:2], -0.125, None, ALU.mult))
        T("tensor", bb + "ntr", [bb + "nscale", "c3"],
          lambda t_: t_.transpose(ps_t16[:, :], nqk[:, 0:1], identf[0:16, 0:16]))
        T("vector", bb + "nrow", [bb + "ntr"],
          lambda v: v.tensor_copy(negCrow[:, :], ps_t16[:, :]))
        T("tensor", bb + "nbc", [bb + "nrow", "c1"],
          lambda t_: t_.matmul(ps_n[:, 0:16], ones_f[:, :], negCrow[:, :],
                               start=True, stop=True))
        T("vector", bb + "nbcev", [bb + "nbc"],
          lambda v: v.tensor_copy(negCall[:, :], ps_n[:, 0:16]))
        P.flush()
        sC.close()

        sDE = contextlib.ExitStack()
        ctxT = sDE.enter_context(nc.sbuf_tensor(pre + bb + "ctxT", [128, 8 * S], BF16,
                                                side="right")).ap()
        sD = contextlib.ExitStack()
        Pb = [sD.enter_context(nc.sbuf_tensor(pre + bb + f"Pb{i}", [128, S], BF16)).ap()
              for i in range(2)]
        ps_S = sD.enter_context(nc.psum_tensor(pre + bb + "psS", [128, S], F32)).ap()
        ps_ctx = sD.enter_context(nc.psum_tensor(pre + bb + "psctx", [65, S], F32)).ap()
        NHF = 2 if NQ >= 2 else 1
        HCW = NQ // NHF

        def p1_tail_rbnorm(hprev):
            hbp = f"{bb}h{hprev}_"
            r0p = (hprev % 2) * 64
            cofp = (hprev // 2) * S
            def em_rb(t_):
                last = None
                for qc in range(NQ):
                    last = t_.matmul(ps_ctx[0:64, 512 * qc:512 * (qc + 1)], ones_f[:, 0:64],
                                     rrow[:, 512 * qc:512 * (qc + 1)], start=True, stop=True)
                return last
            T("tensor", hbp + "rb", [hbp + "den", hbp + "cev", "c1"], em_rb)
            T("vector", hbp + "norm", [hbp + "rb"],
              lambda v, r0p=r0p, cofp=cofp: v.tensor_tensor(
                  ctxT[r0p:r0p + 64, cofp:cofp + S],
                  ctxT[r0p:r0p + 64, cofp:cofp + S], ps_ctx[0:64, :], ALU.mult))

        for h in range(16):
            hb = f"{bb}h{h}_"
            r0 = (h % 2) * 64
            cof = (h // 2) * S
            units = [(kt, hf) for kt in range(NK) for hf in range(NHF)]

            def emit_smm(kt, hf):
                M = lambda tag: f"{hb}{tag}@{kt}_{hf}"
                deps = [bb + "nbcev"]
                if kt >= 1:
                    deps.append(f"{hb}exp@{kt-1}_{hf}")
                elif h >= 1:
                    deps.append(f"{bb}h{h-1}_exp@{NK-1}_{hf}")
                def em(t_, h=h, kt=kt, hf=hf, r0=r0, cof=cof):
                    last = None
                    for qc in range(hf * HCW, (hf + 1) * HCW):
                        last = t_.matmul(ps_S[:, 512 * qc:512 * (qc + 1)],
                                         KT[r0:r0 + 64, cof + 128 * kt:cof + 128 * (kt + 1)],
                                         QT[r0:r0 + 64, cof + 512 * qc:cof + 512 * (qc + 1)],
                                         start=True, stop=True)
                    return last
                T("tensor", M("smm"), deps, em)
                deps = [M("smm")]
                if kt >= 2:
                    deps.append(f"{hb}ctx@{kt-2}_{hf}")
                elif h >= 1:
                    deps.append(f"{bb}h{h-1}_ctx@{NK-2+kt}_{hf}")
                T("scalar", M("exp"), deps,
                  lambda s, kt=kt, hf=hf, h=h: s.activation(
                      Pb[kt % 2][:, 512 * HCW * hf:512 * HCW * (hf + 1)],
                      ps_S[:, 512 * HCW * hf:512 * HCW * (hf + 1)], AF.Exp,
                      bias=negCall[:, h:h + 1], scale=0.125))

            def emit_ctx(kt, hf):
                M = lambda tag: f"{hb}{tag}@{kt}_{hf}"
                deps = [M("exp")]
                if h >= 1 and kt == 0:
                    deps.append(f"{bb}h{h-1}_norm")
                def em(t_, h=h, kt=kt, hf=hf):
                    last = None
                    for qc in range(hf * HCW, (hf + 1) * HCW):
                        last = t_.matmul(ps_ctx[:, 512 * qc:512 * (qc + 1)],
                                         Vb[:, 1040 * kt + 65 * h:1040 * kt + 65 * (h + 1)],
                                         Pb[kt % 2][:, 512 * qc:512 * (qc + 1)],
                                         start=(kt == 0), stop=(kt == NK - 1))
                    return last
                T("tensor", M("ctx"), deps, em)

            for ui, (kt, hf) in enumerate(units):
                emit_smm(kt, hf)
                if ui == 0 and h >= 1:
                    p1_tail_rbnorm(h - 1)
                if ui >= 1:
                    emit_ctx(*units[ui - 1])
            emit_ctx(*units[-1])
            T("vector", hb + "cev", [f"{hb}ctx@{NK-1}_{NHF-1}"],
              lambda v, r0=r0, cof=cof: v.tensor_copy(ctxT[r0:r0 + 64, cof:cof + S],
                                                      ps_ctx[0:64, :]))
            def em_den(s):
                P.ch("scalar", s, s.activation(drow[:, :], ps_ctx[64:65, :], AF.Ln,
                                               bias=epscol[0:1, :]))
                return s.activation(rrow[:, :], drow[:, :], AF.Exp, scale=-1.0)
            T("scalar", hb + "den", [f"{hb}ctx@{NK-1}_{NHF-1}", "c4"], em_den)
            def em_fill(t_):
                last = None
                for _ in range(14):
                    last = t_.matmul(ps_S[:, 0:512], ident[:, :], Pb[0][:, 0:512],
                                     start=True, stop=True)
                return last
            T("tensor", hb + "fill", [f"{hb}ctx@{NK-1}_{NHF-1}"], em_fill)
        p1_tail_rbnorm(15)
        P.flush()
        sD.close()

        sE = contextlib.ExitStack()
        aE = lambda n, sh, dt: sE.enter_context(nc.sbuf_tensor(pre + bb + n, sh, dt)).ap()
        WoutT = aE("WoutT", [128, 8 * 1024], BF16)
        xin = [aE(f"ex{i}", [128, 1024], F32) for i in range(NB)]
        xo = [aE(f"exo{i}", [128, 1024], F32) for i in range(NB)]
        scr = [aE(f"escr{i}", [128, 1024], F32) for i in range(NB)]
        st = [aE(f"est{i}", [128, 8], F32) for i in range(NB)]
        ps_o = [sE.enter_context(nc.psum_tensor(pre + bb + f"pso{i}", [128, 1024], F32)).ap()
                for i in range(2)]
        ps_fil = sE.enter_context(nc.psum_tensor(pre + bb + "epsfil", [128, 128], F32)).ap()
        T("gpsimd", bb + "wout", [], lambda g: dma_T(g, WoutT, w["WoutT"], 1024), dma=True)
        lastD = f"{bb}h15_norm"
        for t in range(NTB):
            i = t % NB
            j = t % 2
            M = lambda tag, tt=t: f"{bb}e_{tag}@{tt}"
            deps = [M("xst", t - NB)] if t >= NB else []
            T("sync", M("xin"), deps,
              lambda sy, i=i, t=t, b=b: sy.dma_start(xin[i][:, :], src_t[b, t, :, :]), dma=True)
            deps = [lastD, bb + "wout", "c9"] + ([M("res", t - 2)] if t >= 2 else [])
            def em_omm(t_, t=t, j=j):
                last = None
                for oc in range(2):
                    for d in range(8):
                        t_.matmul(ps_o[j][:, 512 * oc:512 * (oc + 1)],
                                  ctxT[:, S * d + 128 * t:S * d + 128 * (t + 1)],
                                  WoutT[:, 1024 * d + 512 * oc:1024 * d + 512 * (oc + 1)],
                                  start=(d == 0), stop=False)
                    last = t_.matmul(ps_o[j][:, 512 * oc:512 * (oc + 1)], ones_bf[:, :],
                                     bout_row[:, 512 * oc:512 * (oc + 1)], start=False, stop=True)
                return last
            T("tensor", M("omm"), deps, em_omm)
            if t < NTB - 1:
                def em_efil(t_):
                    last = None
                    for _ in range(16):
                        last = t_.matmul(ps_fil[:, :], ident[:, :], ident[:, :],
                                         start=True, stop=True)
                    return last
                T("tensor", M("fil"), ["c2"], em_efil)
            def em_res(v, i=i, j=j):
                C = lambda ins: P.ch("vector", v, ins)
                C(v.tensor_scalar(scr[i][:, :], xin[i][:, :], 2.0, None, ALU.mult))
                C(v.tensor_tensor(scr[i][:, :], scr[i][:, :], ps_o[j][:, :], ALU.add))
                C(v.tensor_tensor(xo[i][:, :], scr[i][:, :], scr[i][:, :], ALU.mult))
                C(v.reduce_sum(st[i][:, 1:2], xo[i][:, :], axis=AX.X))
                C(v.reduce_sum(st[i][:, 0:1], scr[i][:, :], axis=AX.X))
                C(v.tensor_scalar(st[i][:, 0:1], st[i][:, 0:1], 1.0 / 1024.0, None, ALU.mult))
                C(v.tensor_scalar(st[i][:, 2:3], st[i][:, 1:2], 1.0 / 1024.0, None, ALU.mult))
                C(v.tensor_scalar(st[i][:, 3:4], st[i][:, 0:1], st[i][:, 0:1], None, ALU.mult))
                return v.tensor_tensor(st[i][:, 4:5], st[i][:, 2:3], st[i][:, 3:4], ALU.subtract)
            T("vector", M("res"), [M("omm"), M("xin")], em_res)
            def em_rstd(s, i=i):
                P.ch("scalar", s, s.activation(st[i][:, 5:6], st[i][:, 4:5], AF.Ln, bias=epscol[:, :]))
                return s.activation(st[i][:, 6:7], st[i][:, 5:6], AF.Exp, scale=-0.5)
            T("scalar", M("rstd"), [M("res"), "c4"], em_rstd)
            def em_lno(v, i=i):
                C = lambda ins: P.ch("vector", v, ins)
                C(v.tensor_scalar(st[i][:, 7:8], st[i][:, 0:1], st[i][:, 6:7], -1.0,
                                  ALU.mult, ALU.mult))
                return v.tensor_scalar(xo[i][:, :], scr[i][:, :], st[i][:, 6:7], st[i][:, 7:8],
                                       ALU.mult, ALU.add)
            T("vector", M("lno"), [M("rstd")], em_lno)
            T("sync", M("xst"), [M("lno")],
              lambda sy, i=i, t=t, b=b: sy.dma_start(dst_t[b, t, :, :], xo[i][:, :]), dma=True)
        T("sync", bb + "fin", [f"{bb}e_xst@{t}" for t in range(max(0, NTB - NB), NTB)],
          lambda sy: sy.nop())
        P.flush()
        sE.close()
        sDE.close()
    tes.close()


def host_prep_p1(inp):
    w = {}
    w["WqkvT"] = np.ascontiguousarray(inp["L0_attn_W_qkv"].T).astype(np.float32)
    w["WoutT"] = np.ascontiguousarray(inp["L0_attn_W_out"].T).astype(np.float32)
    w["b_qkv"] = inp["L0_attn_b_qkv"].astype(np.float32)
    w["b_out"] = inp["L0_attn_b_out"].astype(np.float32)
    w["bqkvT"] = np.ascontiguousarray(inp["L0_attn_b_qkv"].reshape(24, 128).T).astype(np.float32)
    w["gamma_bc"] = np.broadcast_to(inp["L0_attn_ln_gamma"], (128, 1024)).astype(np.float32).copy()
    w["beta_bc"] = np.broadcast_to(inp["L0_attn_ln_beta"], (128, 1024)).astype(np.float32).copy()
    hs = np.zeros((128, 8 * 16), np.float32)
    for o in range(8):
        hs[0:64, 16 * o + 2 * o] = 1.0
        hs[64:128, 16 * o + 2 * o + 1] = 1.0
    w["headsel"] = hs
    return w


def np_ref_p1(x, inp):
    xf = x.astype(np.float64)
    B, S, _ = xf.shape
    mu = xf.mean(-1, keepdims=True)
    var = xf.var(-1, keepdims=True)
    xn = (xf - mu) / np.sqrt(var + EPS) * inp["L0_attn_ln_gamma"] + inp["L0_attn_ln_beta"]
    qkv = xn @ inp["L0_attn_W_qkv"].T.astype(np.float64) + inp["L0_attn_b_qkv"]
    qkv = qkv.reshape(B, S, 3, 16, 64).transpose(2, 0, 3, 1, 4)
    q, k, v = qkv[0], qkv[1], qkv[2]
    sc = np.einsum("bhqd,bhkd->bhqk", q, k) / 8.0
    sc = sc - sc.max(-1, keepdims=True)
    e = np.exp(sc)
    p = e / e.sum(-1, keepdims=True)
    ctx = np.einsum("bhqk,bhkd->bhqd", p, v)
    ctx = ctx.transpose(0, 2, 1, 3).reshape(B, S, 1024)
    out = ctx @ inp["L0_attn_W_out"].T.astype(np.float64) + inp["L0_attn_b_out"]
    t = 2 * xf + out
    mu = t.mean(-1, keepdims=True)
    var = t.var(-1, keepdims=True)
    return ((t - mu) / np.sqrt(var + EPS)).astype(np.float32)



def build_p4(nc, P, cfg, src, dst, w, pre="p4"):
    B, S = cfg["B"], cfg["S"]
    HID = cfg.get("HID", 2816)
    HW = HID // 2
    NHT = HW // 128
    TOK = B * S
    MEGA = min(1024, TOK)
    NM = TOK // MEGA
    NTM = MEGA // 128
    NSUB = MEGA // 512
    tes = contextlib.ExitStack()
    a = lambda n, sh, dt: tes.enter_context(nc.sbuf_tensor(pre + n, sh, dt)).ap()
    _task = P.task
    T = lambda eng, tag, deps, emit, dma=False: _task(
        eng, pre + tag, [pre + d for d in deps], emit, dma)
    AL = lambda tag, deps: P.alias(pre + tag, [pre + d for d in deps])

    ones_bf = a("ones", [1, 128], BF16)
    ident = a("ident", [128, 128], BF16)
    identf = a("identf", [128, 128], F32)
    epscol = a("epscol", [128, 1], F32)
    eself = a("eself", [8, 8 * 128], F32)
    ebias = a("ebias", [128, 8], F32)
    gwT = a("gwT", [128, 8 * 8], BF16)
    xTm = a("xTm", [128, 8 * MEGA], BF16)
    acc = a("acc", [128, NTM * 1024], BF16)
    m8 = a("m8", [8, MEGA], F32)
    NB = 3

    xTm3 = xTm.rearrange("p (dt s) -> p dt s", s=MEGA)
    src_t = src.rearrange("b (t p) d -> (b t) p d", p=128)
    dst_t = dst.rearrange("b (t p) d -> (b t) p d", p=128)

    T("gpsimd", "c0", [], lambda g: g.dma_start(ones_bf[:, :], w["ones_row"][:, :]), dma=True)
    T("gpsimd", "c2", [], lambda g: g.dma_start(ident[:, :], w["ident"][:, :]), dma=True)
    T("gpsimd", "c3", [], lambda g: g.dma_start(identf[:, :], w["ident_f"][:, :]), dma=True)
    T("gpsimd", "c4", [], lambda g: g.dma_start(epscol[:, :], w["epscol"][:, :]), dma=True)
    T("gpsimd", "c5", [], lambda g: g.dma_start(eself[:, :], w["esel_f"][:, :]), dma=True)
    T("gpsimd", "c6", [], lambda g: g.dma_start(ebias[:, :], w["ebias_bc"][:, :]), dma=True)
    T("gpsimd", "c7", [], lambda g: dma_T(g, gwT, w["gatewT"], 8), dma=True)

    for m in range(NM):
        mb = f"m{m}_"
        ses = contextlib.ExitStack()
        sesR = contextlib.ExitStack()
        pscR = lambda n, sh, dt=F32: sesR.enter_context(
            nc.psum_tensor(pre + mb + n, sh, dt)).ap()
        ps_tr = [pscR(f"pstr{i}", [128, 1024], BF16) for i in range(2)]
        ps_l = [pscR(f"psl{i}", [128, 8]) for i in range(2)]
        ps_m = [pscR(f"psm{i}", [8, 128]) for i in range(2)]
        ps_fil = pscR("psfil", [128, 512])
        sR = contextlib.ExitStack()
        aR = lambda n, sh, dt: sR.enter_context(nc.sbuf_tensor(pre + mb + n, sh, dt)).ap()
        xin = [aR(f"x{i}", [128, 1024], F32) for i in range(NB)]
        xbf = [aR(f"xbf{i}", [128, 1024], BF16) for i in range(NB)]
        gw = [aR(f"gw{i}", [128, 8], F32) for i in range(NB)]
        esb = [aR(f"esb{i}", [128, 8], F32) for i in range(NB)]
        scr = [aR(f"scr{i}", [128, 8], F32) for i in range(NB)]
        st = [aR(f"st{i}", [128, 8], F32) for i in range(NB)]

        for tt in range(NTM):
            i = tt % NB
            t_g = m * NTM + tt
            M = lambda tag, s=tt: f"{mb}r_{tag}@{s}"
            deps = [M("mtr", tt - NB)] if tt >= NB else []
            T("sync", M("xin"), deps,
              lambda sy, i=i, t_g=t_g: sy.dma_start(xin[i][:, :], src_t[t_g, :, :]), dma=True)
            T("scalar", M("cast"), [M("xin")],
              lambda s, i=i: s.copy(xbf[i][:, :], xin[i][:, :]))
            deps = [M("cast"), "c2"] + ([M("xTev", tt - 2)] if tt >= 2 else [])
            def em_tr(t_, i=i, tt=tt):
                for _ in range(12):
                    t_.matmul(ps_fil[:, 0:128], ident[:, :], ident[:, :],
                              start=True, stop=True)
                last = None
                for k in range(8):
                    last = t_.transpose(ps_tr[tt % 2][:, 128 * k:128 * (k + 1)],
                                        xbf[i][:, 128 * k:128 * (k + 1)], ident[:, :])
                return last
            T("tensor", M("tr"), deps, em_tr)
            T("scalar", M("xTev"), [M("tr")],
              lambda s, tt=tt: s.copy(xTm3[:, :, 128 * tt:128 * (tt + 1)],
                                      ps_tr[tt % 2].rearrange("p (dt c) -> p dt c", c=128)))
            deps = [M("xTev"), "c7"] + ([M("sexp", tt - 2)] if tt >= 2 else [])
            def em_rl(t_, tt=tt):
                last = None
                for d in range(8):
                    last = t_.matmul(ps_l[tt % 2][:, :], xTm3[:, d, 128 * tt:128 * (tt + 1)],
                                     gwT[:, 8 * d:8 * (d + 1)], start=(d == 0), stop=(d == 7))
                return last
            T("tensor", M("rl"), deps, em_rl)
            smax_deps = [M("rl")] + ([M("wgt", tt - NB)] if tt >= NB else [])
            T("vector", M("smax"), smax_deps,
              lambda v, i=i, tt=tt: v.reduce_max(st[i][:, 0:1], ps_l[tt % 2][:, :],
                                                 axis=AX.X, negate=True))
            T("scalar", M("sexp"), [M("smax")],
              lambda s, i=i, tt=tt: s.activation(esb[i][:, :], ps_l[tt % 2][:, :], AF.Exp,
                                                 bias=st[i][:, 0:1], scale=1.0))
            def em_wgt(v, i=i):
                C = lambda ins: P.ch("vector", v, ins)
                C(v.reduce_sum(st[i][:, 1:2], esb[i][:, :], axis=AX.X))
                C(v.reciprocal(st[i][:, 2:3], st[i][:, 1:2]))
                C(v.tensor_scalar(gw[i][:, :], esb[i][:, :], st[i][:, 2:3], None, ALU.mult))
                C(v.tensor_tensor(esb[i][:, :], gw[i][:, :], ebias[:, :], ALU.add))
                C(v.reduce_max(st[i][:, 3:4], esb[i][:, :], axis=AX.X))
                C(v.tensor_scalar(st[i][:, 4:5], st[i][:, 3:4], BIG, None, ALU.mult))
                C(v.tensor_scalar(scr[i][:, :], esb[i][:, :], st[i][:, 3:4], None, ALU.is_ge))
                C(v.tensor_scalar(scr[i][:, :], scr[i][:, :], BIG, None, ALU.mult))
                C(v.tensor_tensor(scr[i][:, :], esb[i][:, :], scr[i][:, :], ALU.subtract))
                C(v.reduce_max(st[i][:, 5:6], scr[i][:, :], axis=AX.X))
                C(v.tensor_scalar(esb[i][:, :], esb[i][:, :], st[i][:, 5:6], None, ALU.is_ge))
                return v.tensor_tensor(gw[i][:, :], gw[i][:, :], esb[i][:, :], ALU.mult)
            T("vector", M("wgt"), [M("sexp"), "c6"], em_wgt)
            deps = [M("wgt"), "c3"] + ([M("mev", tt - 2)] if tt >= 2 else [])
            T("tensor", M("mtr"), deps,
              lambda t_, i=i, tt=tt: t_.transpose(ps_m[tt % 2][:, :], gw[i][:, :], identf[:, :]))
            T("vector", M("mev"), [M("mtr")],
              lambda v, tt=tt: v.tensor_copy(m8[:, 128 * tt:128 * (tt + 1)], ps_m[tt % 2][:, :]))
        P.flush()
        sR.close()
        sesR.close()
        psc = lambda n, sh, dt=F32: ses.enter_context(
            nc.psum_tensor(pre + mb + n, sh, dt)).ap()
        ps_M = [psc(f"psM{i}", [128, 512]) for i in range(2)]
        ps_U = [psc(f"psU{i}", [128, 512]) for i in range(2)]
        ps_G = [psc(f"psG{i}", [128, 512]) for i in range(2)]
        ps_O = [psc(f"psO{i}", [128, 512]) for i in range(2)]

        sEx = contextlib.ExitStack()
        aX = lambda n, sh, dt: sEx.enter_context(nc.sbuf_tensor(pre + mb + n, sh, dt)).ap()
        zTb = [aX(f"zT{i}", [128, 8 * 512], BF16) for i in range(2)]
        Hsb = aX("Hsb", [128, NHT * 512], BF16)
        silu_sb = aX("silu", [128, 512], F32)
        w1h = [aX(f"w1h{i}", [128, 8 * HW], BF16) for i in range(2)]
        w3h = [aX(f"w3h{i}", [128, 8 * HW], BF16) for i in range(2)]
        w2h = [aX(f"w2h{i}", [128, NHT * 1024], BF16) for i in range(2)]
        NG = 16 * NSUB

        def gtag(g):
            it, sub = divmod(g, NSUB)
            half, e = divmod(it, 8)
            return f"{mb}h{half}e{e}_s{sub}_"

        def emit_Mz(g):
            it, sub = divmod(g, NSUB)
            half, e = divmod(it, 8)
            tg = gtag(g)
            deps = [f"{mb}r_mev@{NTM-1}", "c5"]
            if g >= 2:
                deps.append(gtag(g - 2) + "zTdone")
            T("tensor", tg + "Mmm", deps,
              lambda t_, e=e, sub=sub, g=g: t_.matmul(
                  ps_M[g % 2][:, :], eself[:, 128 * e:128 * (e + 1)],
                  m8[:, 512 * sub:512 * (sub + 1)], start=True, stop=True))
            zdeps = [tg + "Mmm"]
            if g >= 2:
                zdeps.append(gtag(g - 2) + "Adone")
            def em_zT(v, sub=sub, g=g):
                last = None
                for d in range(8):
                    last = v.tensor_tensor(zTb[g % 2][:, 512 * d:512 * (d + 1)],
                                           xTm3[:, d, 512 * sub:512 * (sub + 1)],
                                           ps_M[g % 2][:, :], ALU.mult)
                return last
            T("vector", tg + "zT", zdeps, em_zT)
            AL(tg + "zTdone", [tg + "zT"])

        emit_Mz(0)
        for g in range(NG):
            it, sub = divmod(g, NSUB)
            half, e = divmod(it, 8)
            buf = it % 2
            eb = f"{mb}h{half}e{e}_"
            sb_ = gtag(g)
            Msu = lambda tag: f"{sb_}{tag}"
            prev = gtag(g - 1) if g >= 1 else None
            zT = zTb[g % 2]
            if sub == 0:
                wdep = ([f"{gtag((it-2) * NSUB + NSUB - 1)}Bdone"] if it >= 2 else [])
                T("gpsimd", eb + "w1", wdep, lambda g_, e=e, half=half, buf=buf: g_.dma_start(
                    w1h[buf].rearrange("p (dt h) -> p dt h", h=HW),
                    w["w1"][e, :, half * HW:(half + 1) * HW].rearrange("(dt p) h -> p dt h", p=128)),
                    dma=True)
                T("gpsimd", eb + "w3", wdep, lambda g_, e=e, half=half, buf=buf: g_.dma_start(
                    w3h[buf].rearrange("p (dt h) -> p dt h", h=HW),
                    w["w3"][e, :, half * HW:(half + 1) * HW].rearrange("(dt p) h -> p dt h", p=128)),
                    dma=True)
                T("gpsimd", eb + "w2", wdep, lambda g_, e=e, half=half, buf=buf: g_.dma_start(
                    w2h[buf].rearrange("p (ht o) -> p ht o", o=1024),
                    w["w2"][e, half * HW:(half + 1) * HW, :].rearrange("(ht p) o -> p ht o", p=128)),
                    dma=True)
            for ht in range(NHT):
                hb_ = f"{sb_}ht{ht}_"
                pj = ht % 2
                deps = [Msu("zTdone"), eb + "w1"]
                deps += [f"{sb_}ht{ht-2}_silu"] if ht >= 2 else \
                        ([prev + "Adone"] if g >= 1 else [])
                def em_U(t_, ht=ht, buf=buf, zT=zT, pj=pj):
                    last = None
                    for d in range(8):
                        last = t_.matmul(ps_U[pj][:, :],
                                         w1h[buf][:, HW * d + 128 * ht:HW * d + 128 * (ht + 1)],
                                         zT[:, 512 * d:512 * (d + 1)],
                                         start=(d == 0), stop=(d == 7))
                    return last
                T("tensor", hb_ + "U", deps, em_U)
                deps = [eb + "w3"]
                deps += [f"{sb_}ht{ht-2}_H"] if ht >= 2 else \
                        ([prev + "Adone"] if g >= 1 else [])
                def em_G(t_, ht=ht, buf=buf, zT=zT, pj=pj):
                    last = None
                    for d in range(8):
                        last = t_.matmul(ps_G[pj][:, :],
                                         w3h[buf][:, HW * d + 128 * ht:HW * d + 128 * (ht + 1)],
                                         zT[:, 512 * d:512 * (d + 1)],
                                         start=(d == 0), stop=(d == 7))
                    return last
                T("tensor", hb_ + "G", [Msu("zTdone")] + deps, em_G)
                sdeps = [hb_ + "U"]
                sdeps += [f"{sb_}ht{ht-1}_H"] if ht >= 1 else \
                         ([prev + "Adone"] if g >= 1 else [])
                T("scalar", hb_ + "silu", sdeps,
                  lambda s, pj=pj: s.activation(silu_sb[:, :], ps_U[pj][:, :], AF.Silu))
                T("vector", hb_ + "H", [hb_ + "silu", hb_ + "G"],
                  lambda v, ht=ht, pj=pj: v.tensor_tensor(Hsb[:, 512 * ht:512 * (ht + 1)],
                                                          silu_sb[:, :], ps_G[pj][:, :], ALU.mult))
            if g + 1 < NG:
                emit_Mz(g + 1)
            AL(Msu("Adone"), [f"{sb_}ht{NHT-1}_H"])
            for u in range(8):
                tl, oc = u // 2, u % 2
                ub = f"{sb_}u{u}_"
                deps = [Msu("Adone"), eb + "w2"]
                if u >= 2:
                    deps.append(f"{sb_}u{u-2}_acc")
                elif g >= 1:
                    deps.append(prev + "u7_acc")
                def em_B(t_, tl=tl, oc=oc, buf=buf, u=u):
                    last = None
                    for ht in range(NHT):
                        last = t_.matmul(ps_O[u % 2][:, :],
                                         Hsb[:, 512 * ht + 128 * tl:512 * ht + 128 * (tl + 1)],
                                         w2h[buf][:, 1024 * ht + 512 * oc:1024 * ht + 512 * (oc + 1)],
                                         start=(ht == 0), stop=(ht == NHT - 1))
                    return last
                T("tensor", ub + "B", deps, em_B)
                col = (sub * 4 + tl) * 1024 + oc * 512
                if it == 0:
                    T("vector", ub + "acc", [ub + "B"],
                      lambda v, col=col, u=u: v.tensor_copy(acc[:, col:col + 512],
                                                            ps_O[u % 2][:, :]))
                else:
                    T("vector", ub + "acc", [ub + "B"],
                      lambda v, col=col, u=u: v.tensor_tensor(acc[:, col:col + 512],
                                                             acc[:, col:col + 512],
                                                             ps_O[u % 2][:, :], ALU.add))
            AL(Msu("Bdone"), [f"{sb_}u{u}_acc" for u in range(8)])
        P.flush()
        sEx.close()

        sF = contextlib.ExitStack()
        aF = lambda n, sh, dt: sF.enter_context(nc.sbuf_tensor(pre + mb + n, sh, dt)).ap()
        xin = [aF(f"fx{i}", [128, 1024], F32) for i in range(NB)]
        xo = [aF(f"fxo{i}", [128, 1024], F32) for i in range(NB)]
        scr = [aF(f"fscr{i}", [128, 1024], F32) for i in range(NB)]
        st = [aF(f"fst{i}", [128, 8], F32) for i in range(NB)]
        for tt in range(NTM):
            i = tt % NB
            t_g = m * NTM + tt
            M = lambda tag, s=tt: f"{mb}f_{tag}@{s}"
            deps = [M("xst", tt - NB)] if tt >= NB else []
            T("sync", M("xin"), deps,
              lambda sy, i=i, t_g=t_g: sy.dma_start(xin[i][:, :], src_t[t_g, :, :]), dma=True)
            lastB = f"{mb}h1e7_s{NSUB-1}_Bdone"
            def em_res(v, i=i, tt=tt):
                C = lambda ins: P.ch("vector", v, ins)
                C(v.tensor_tensor(scr[i][:, :], xin[i][:, :], acc[:, 1024 * tt:1024 * (tt + 1)],
                                  ALU.add))
                C(v.tensor_tensor(xo[i][:, :], scr[i][:, :], scr[i][:, :], ALU.mult))
                C(v.reduce_sum(st[i][:, 1:2], xo[i][:, :], axis=AX.X))
                C(v.reduce_sum(st[i][:, 0:1], scr[i][:, :], axis=AX.X))
                C(v.tensor_scalar(st[i][:, 0:1], st[i][:, 0:1], 1.0 / 1024.0, None, ALU.mult))
                C(v.tensor_scalar(st[i][:, 2:3], st[i][:, 1:2], 1.0 / 1024.0, None, ALU.mult))
                C(v.tensor_scalar(st[i][:, 3:4], st[i][:, 0:1], st[i][:, 0:1], None, ALU.mult))
                return v.tensor_tensor(st[i][:, 4:5], st[i][:, 2:3], st[i][:, 3:4], ALU.subtract)
            T("vector", M("res"), [M("xin"), lastB], em_res)
            def em_rstd(s, i=i):
                P.ch("scalar", s, s.activation(st[i][:, 5:6], st[i][:, 4:5], AF.Ln, bias=epscol[:, :]))
                return s.activation(st[i][:, 6:7], st[i][:, 5:6], AF.Exp, scale=-0.5)
            T("scalar", M("rstd"), [M("res"), "c4"], em_rstd)
            def em_lno(v, i=i):
                P.ch("vector", v, v.tensor_scalar(st[i][:, 7:8], st[i][:, 0:1], st[i][:, 6:7],
                                                  -1.0, ALU.mult, ALU.mult))
                return v.tensor_scalar(xo[i][:, :], scr[i][:, :], st[i][:, 6:7], st[i][:, 7:8],
                                       ALU.mult, ALU.add)
            T("vector", M("lno"), [M("rstd")], em_lno)
            T("sync", M("xst"), [M("lno")],
              lambda sy, i=i, t_g=t_g: sy.dma_start(dst_t[t_g, :, :], xo[i][:, :]), dma=True)
        T("sync", mb + "fin", [f"{mb}f_xst@{t}" for t in range(max(0, NTM - NB), NTM)],
          lambda sy: sy.nop())
        P.flush()
        sF.close()
        ses.close()
    tes.close()


def host_prep_p4(inp):
    w = {}
    w["gatewT"] = np.ascontiguousarray(inp["L1_ffn_gate_w"].T).astype(np.float32)
    w["w1"] = inp["L1_ffn_w1"].astype(np.float32)
    w["w2"] = inp["L1_ffn_w2"].astype(np.float32)
    w["w3"] = inp["L1_ffn_w3"].astype(np.float32)
    w["ebias_bc"] = np.broadcast_to(inp["L1_ffn_expert_bias"], (128, 8)).astype(np.float32).copy()
    es = np.zeros((8, 8 * 128), np.float32)
    for e in range(8):
        es[e, 128 * e:128 * (e + 1)] = 1.0
    w["esel_f"] = es
    return w


def np_ref_p4(x, inp):
    xf = x.astype(np.float64)
    B, S, _ = xf.shape
    xr = xf.reshape(-1, 1024)
    sc = xr @ inp["L1_ffn_gate_w"].T.astype(np.float64)
    e = np.exp(sc - sc.max(-1, keepdims=True))
    g = e / e.sum(-1, keepdims=True)
    b = g + inp["L1_ffn_expert_bias"]
    sel = np.argsort(-b, -1, kind="stable")[:, :2]
    out = np.zeros_like(xr)
    for ei in range(8):
        msel = (sel == ei).any(-1)
        z = xr * (g[:, ei] * msel)[:, None]
        h = z @ inp["L1_ffn_w1"][ei].astype(np.float64)
        h = h / (1 + np.exp(-h)) * (z @ inp["L1_ffn_w3"][ei].astype(np.float64))
        out += h @ inp["L1_ffn_w2"][ei].astype(np.float64)
    t = xf + out.reshape(B, S, 1024)
    mu = t.mean(-1, keepdims=True)
    var = t.var(-1, keepdims=True)
    return ((t - mu) / np.sqrt(var + EPS)).astype(np.float32)



def build_p3(nc, P, cfg, src, dst, w, pre="p3"):
    B, S = cfg["B"], cfg["S"]
    NTB = S // 128
    NQ = S // 512
    NK = S // 128
    SCL = 1.0 / float(np.sqrt(96.0))
    tes = contextlib.ExitStack()
    a = lambda n, sh, dt: tes.enter_context(nc.sbuf_tensor(pre + n, sh, dt)).ap()
    _task = P.task
    T = lambda eng, tag, deps, emit, dma=False: _task(
        eng, pre + tag, [pre + d for d in deps], emit, dma)
    AL = lambda tag, deps: P.alias(pre + tag, [pre + d for d in deps])

    ones_bf = a("ones", [1, 128], BF16)
    ones_f = a("onesf", [1, 128], F32)
    ident = a("ident", [128, 128], BF16)
    identf = a("identf", [128, 128], F32)
    epscol = a("epscol", [128, 1], F32)
    loborow = a("lobo", [1, 16], F32)
    Crow = a("Crow", [1, 16], F32)
    mcol = a("mcol", [128, 2], F32)
    NB = 3
    src_t = src.rearrange("b (t p) d -> b t p d", p=128)
    dst_t = dst.rearrange("b (t p) d -> b t p d", p=128)

    T("gpsimd", "c0", [], lambda g: g.dma_start(ones_bf[:, :], w["ones_row"][:, :]), dma=True)
    T("gpsimd", "c1", [], lambda g: g.dma_start(ones_f[:, :], w["ones_row_f"][:, :]), dma=True)
    T("gpsimd", "c2", [], lambda g: g.dma_start(ident[:, :], w["ident"][:, :]), dma=True)
    T("gpsimd", "c3", [], lambda g: g.dma_start(identf[:, :], w["ident_f"][:, :]), dma=True)
    T("gpsimd", "c4", [], lambda g: g.dma_start(epscol[:, :], w["epscol"][:, :]), dma=True)
    T("gpsimd", "c5", [], lambda g: g.dma_start(loborow[:, :], w["lobo_log"][None, :]), dma=True)
    T("scalar", "cexp", ["c5"],
      lambda s: s.activation(Crow[:, :], loborow[:, :], AF.Exp))

    for b in range(B):
        bb = f"b{b}_"
        xi = contextlib.ExitStack()
        axi = lambda n, sh, dt: xi.enter_context(nc.sbuf_tensor(pre + bb + n, sh, dt)).ap()
        qcT = axi("qcT", [128, 8 * S], BF16)
        qrT = axi("qrT", [128, 8 * S], BF16)
        kcT = axi("kcT", [128, 8 * S], BF16)
        krT = axi("krT", [128, S], BF16)
        Vb = axi("Vb", [128, NK * 1040], BF16)
        Vb4 = Vb.rearrange("p (t g c) -> p t g c", g=16, c=65)
        sB = contextlib.ExitStack()
        ab = lambda n, sh, dt: sB.enter_context(nc.sbuf_tensor(pre + bb + n, sh, dt)).ap()
        psb = lambda n, sh, dt=F32: sB.enter_context(
            nc.psum_tensor(pre + bb + n, sh, dt)).ap()
        xT = ab("xT", [128, 8 * S], BF16)
        latT = ab("latT", [128, 2 * S], BF16)
        cos16 = ab("cos16", [128, NTB * 16], F32)
        sin16 = ab("sin16", [128, NTB * 16], F32)
        krb = ab("krb", [128, 32], BF16)
        rA = ab("rA", [128, 256], F32)
        rB = ab("rB", [128, 256], F32)
        ps_tr = psb("pstr", [128, 1024], BF16)
        NQH = NQ if NQ == 1 else NQ // 2
        NHF2 = 1 if NQ == 1 else 2
        ps_qk = [psb(f"psqk{i}", [128, 512 * NQH]) for i in range(2)]
        ps_tm = psb("pstm", [128, 1024])
        ps_kr = psb("pskr", [128, 32])
        xT3 = xT.rearrange("p (dt s) -> p dt s", s=S)

        T("gpsimd", bb + "tc16", [], lambda g: g.dma_start(cos16[:, :], w["cos16"][:, :]), dma=True)
        T("gpsimd", bb + "ts16", [], lambda g: g.dma_start(sin16[:, :], w["sin16"][:, :]), dma=True)
        T("gpsimd", bb + "vset", [], lambda g: g.memset(Vb4[:, :, :, 64:65], 1.0))
        T("gpsimd", bb + "krz", [], lambda g: g.memset(krT[:, :], 0.0))

        sA2 = contextlib.ExitStack()
        xin = [sA2.enter_context(nc.sbuf_tensor(pre + bb + f"x{i}", [128, 1024], F32)).ap()
               for i in range(NB)]
        xbf = [sA2.enter_context(nc.sbuf_tensor(pre + bb + f"xbf{i}", [128, 1024], BF16)).ap()
               for i in range(NB)]
        for t in range(NTB):
            i = t % NB
            M = lambda tag, tt=t: f"{bb}a_{tag}@{tt}"
            deps = [M("xTev", t - NB)] if t >= NB else []
            T("sync", M("xin"), deps,
              lambda sy, i=i, t=t, b=b: sy.dma_start(xin[i][:, :], src_t[b, t, :, :]), dma=True)
            T("scalar", M("cast"), [M("xin")],
              lambda s, i=i: s.copy(xbf[i][:, :], xin[i][:, :]))
            deps = [M("cast"), "c2"] + ([M("xTev", t - 1)] if t >= 1 else [])
            def em_tr(t_, i=i):
                last = None
                for k in range(8):
                    last = t_.transpose(ps_tr[:, 128 * k:128 * (k + 1)],
                                        xbf[i][:, 128 * k:128 * (k + 1)], ident[:, :])
                return last
            T("tensor", M("tr"), deps, em_tr)
            T("scalar", M("xTev"), [M("tr")],
              lambda s, t=t: s.copy(xT3[:, :, 128 * t:128 * (t + 1)],
                                    ps_tr.rearrange("p (dt c) -> p dt c", c=128)))
        lastA = f"{bb}a_xTev@{NTB-1}"
        P.flush()
        sA2.close()

        sWl = contextlib.ExitStack()
        Wkvd = sWl.enter_context(nc.sbuf_tensor(pre + bb + "Wkvd", [128, 8 * 256], BF16)).ap()
        T("gpsimd", bb + "wkvd", [], lambda g: dma_T(g, Wkvd, w["Wkvd"], 256), dma=True)
        uq = 0
        for lt in range(2):
            for hf in range(NHF2):
                M = lambda tag, l=lt, f_=hf: f"{bb}lat_{tag}@{l}_{f_}"
                pz = uq % 2
                deps = [lastA, bb + "wkvd"]
                if uq >= 2:
                    deps.append(f"{bb}lat_ev@{(uq-2)//NHF2}_{(uq-2)%NHF2}")
                def em_mm(t_, lt=lt, hf=hf, pz=pz):
                    last = None
                    for qc in range(hf * NQH, (hf + 1) * NQH):
                        for d in range(8):
                            last = t_.matmul(
                                ps_qk[pz][:, 512 * (qc - hf * NQH):512 * (qc - hf * NQH + 1)],
                                Wkvd[:, 256 * d + 128 * lt:256 * d + 128 * (lt + 1)],
                                xT3[:, d, 512 * qc:512 * (qc + 1)],
                                start=(d == 0), stop=(d == 7))
                    return last
                T("tensor", M("mm"), deps, em_mm)
                T("scalar", M("ev"), [M("mm")],
                  lambda s, lt=lt, hf=hf, pz=pz: s.copy(
                      latT[:, S * lt + 512 * NQH * hf:S * lt + 512 * NQH * (hf + 1)],
                      ps_qk[pz][:, :]))
                uq += 1
        lat_last = f"{bb}lat_ev@1_{NHF2-1}"
        P.flush()
        sWl.close()

        sWq = contextlib.ExitStack()
        Wq = sWq.enter_context(nc.sbuf_tensor(pre + bb + "Wq", [128, 8 * 1024], BF16)).ap()
        T("gpsimd", bb + "wq", [], lambda g: dma_T(g, Wq, w["Wq"], 1024), dma=True)
        uq = 0
        for ot in range(8):
            for hf in range(NHF2):
                M = lambda tag, o=ot, f_=hf: f"{bb}qc_{tag}@{o}_{f_}"
                pz = uq % 2
                deps = [lastA, bb + "wq"]
                if uq >= 2:
                    deps.append(f"{bb}qc_ev@{(uq-2)//NHF2}_{(uq-2)%NHF2}")
                else:
                    deps.append(lat_last)
                def em_mm(t_, ot=ot, hf=hf, pz=pz):
                    last = None
                    for qc in range(hf * NQH, (hf + 1) * NQH):
                        for d in range(8):
                            last = t_.matmul(
                                ps_qk[pz][:, 512 * (qc - hf * NQH):512 * (qc - hf * NQH + 1)],
                                Wq[:, 1024 * d + 128 * ot:1024 * d + 128 * (ot + 1)],
                                xT3[:, d, 512 * qc:512 * (qc + 1)],
                                start=(d == 0), stop=(d == 7))
                    return last
                T("tensor", M("mm"), deps, em_mm)
                T("scalar", M("ev"), [M("mm")],
                  lambda s, ot=ot, hf=hf, pz=pz: s.copy(
                      qcT[:, S * ot + 512 * NQH * hf:S * ot + 512 * NQH * (hf + 1)],
                      ps_qk[pz][:, :]))
                uq += 1
        qc_last = f"{bb}qc_ev@7_{NHF2-1}"
        P.flush()
        sWq.close()
        sWqr = contextlib.ExitStack()
        Wqr = sWqr.enter_context(nc.sbuf_tensor(pre + bb + "Wqr", [128, 8 * 1024], BF16)).ap()
        qrb = sWqr.enter_context(nc.sbuf_tensor(pre + bb + "qrb", [128, 1024], BF16)).ap()
        cosr = [sWqr.enter_context(nc.sbuf_tensor(pre + bb + f"cosr{i}", [128, 256], F32)).ap()
                for i in range(2)]
        sinr = [sWqr.enter_context(nc.sbuf_tensor(pre + bb + f"sinr{i}", [128, 256], F32)).ap()
                for i in range(2)]
        T("gpsimd", bb + "wqr", [], lambda g: dma_T(g, Wqr, w["Wqr_pad"], 1024), dma=True)
        T("gpsimd", bb + "qrz", [], lambda g: g.memset(qrb[:, :], 0.0))
        qr4 = qrb.rearrange("p (h c) -> p h c", c=64)
        for t in range(NTB):
            M = lambda tag, tt=t: f"{bb}qr_{tag}@{tt}"
            deps = [lastA, bb + "wqr"] + ([M("rope", t - 1)] if t >= 1 else [])
            def em_mm(t_, t=t):
                last = None
                for oc in range(2):
                    for d in range(8):
                        last = t_.matmul(ps_tm[:, 512 * oc:512 * (oc + 1)],
                                         xT3[:, d, 128 * t:128 * (t + 1)],
                                         Wqr[:, 1024 * d + 512 * oc:1024 * d + 512 * (oc + 1)],
                                         start=(d == 0), stop=(d == 7))
                return last
            T("tensor", M("mm"), deps, em_mm)
            cdeps = [M("rope", t - 2)] if t >= 2 else []
            T("gpsimd", M("tcos"), cdeps, lambda g, t=t: g.dma_start(
                cosr[t % 2][:, :], w["cosrep"][:, 256 * t:256 * (t + 1)]), dma=True)
            T("gpsimd", M("tsin"), cdeps, lambda g, t=t: g.dma_start(
                sinr[t % 2][:, :], w["sinrep"][:, 256 * t:256 * (t + 1)]), dma=True)
            ps4 = ps_tm.rearrange("p (h c) -> p h c", c=64)
            def em_rope(v, t=t, ps4=ps4, qr4=qr4):
                C = lambda ins: P.ch("vector", v, ins)
                co = cosr[t % 2].rearrange("p (h c) -> p h c", c=16)
                si = sinr[t % 2].rearrange("p (h c) -> p h c", c=16)
                rA3 = rA.rearrange("p (h c) -> p h c", c=16)
                rB3 = rB.rearrange("p (h c) -> p h c", c=16)
                C(v.tensor_tensor(rA3[:, :, :], ps4[:, :, 0:16], co[:, :, :], ALU.mult))
                C(v.tensor_tensor(rB3[:, :, :], ps4[:, :, 16:32], si[:, :, :], ALU.mult))
                C(v.tensor_tensor(qr4[:, :, 0:16], rA3[:, :, :], rB3[:, :, :], ALU.subtract))
                C(v.tensor_tensor(rA3[:, :, :], ps4[:, :, 16:32], co[:, :, :], ALU.mult))
                C(v.tensor_tensor(rB3[:, :, :], ps4[:, :, 0:16], si[:, :, :], ALU.mult))
                return v.tensor_tensor(qr4[:, :, 16:32], rA3[:, :, :], rB3[:, :, :], ALU.add)
            deps = [M("mm"), M("tcos"), M("tsin"), bb + "qrz"]
            if t >= 1:
                deps.append(M("tr", t - 1))
            T("vector", M("rope"), deps, em_rope)
            deps = [M("rope"), "c2"] + ([M("ev", t - 1)] if t >= 1 else [])
            def em_tr2(t_):
                last = None
                for k in range(8):
                    last = t_.transpose(ps_tr[:, 128 * k:128 * (k + 1)],
                                        qrb[:, 128 * k:128 * (k + 1)], ident[:, :])
                return last
            T("tensor", M("tr"), deps, em_tr2)
            T("scalar", M("ev"), [M("tr")],
              lambda s, t=t: s.copy(qrT.rearrange("p (dt s) -> p dt s", s=S)[:, :, 128 * t:128 * (t + 1)],
                                    ps_tr.rearrange("p (dt c) -> p dt c", c=128)))
        P.flush()
        sWqr.close()
        sWkr = contextlib.ExitStack()
        Wkr = sWkr.enter_context(nc.sbuf_tensor(pre + bb + "Wkr", [128, 8 * 32], BF16)).ap()
        T("gpsimd", bb + "wkr", [], lambda g: dma_T(g, Wkr, w["Wkr"], 32), dma=True)
        for t in range(NTB):
            M = lambda tag, tt=t: f"{bb}kr_{tag}@{tt}"
            deps = [lastA, bb + "wkr"] + ([M("rope", t - 1)] if t >= 1 else [])
            def em_mm(t_, t=t):
                last = None
                for d in range(8):
                    last = t_.matmul(ps_kr[:, :], xT3[:, d, 128 * t:128 * (t + 1)],
                                     Wkr[:, 32 * d:32 * (d + 1)], start=(d == 0), stop=(d == 7))
                return last
            T("tensor", M("mm"), deps, em_mm)
            def em_rope(v, t=t):
                C = lambda ins: P.ch("vector", v, ins)
                C(v.tensor_tensor(rA[:, 0:16], ps_kr[:, 0:16], cos16[:, 16 * t:16 * (t + 1)], ALU.mult))
                C(v.tensor_tensor(rB[:, 0:16], ps_kr[:, 16:32], sin16[:, 16 * t:16 * (t + 1)], ALU.mult))
                C(v.tensor_tensor(krb[:, 0:16], rA[:, 0:16], rB[:, 0:16], ALU.subtract))
                C(v.tensor_tensor(rA[:, 0:16], ps_kr[:, 16:32], cos16[:, 16 * t:16 * (t + 1)], ALU.mult))
                C(v.tensor_tensor(rB[:, 0:16], ps_kr[:, 0:16], sin16[:, 16 * t:16 * (t + 1)], ALU.mult))
                return v.tensor_tensor(krb[:, 16:32], rA[:, 0:16], rB[:, 0:16], ALU.add)
            deps = [M("mm"), bb + "tc16", bb + "ts16"]
            if t >= 1:
                deps.append(M("tr", t - 1))
            T("vector", M("rope"), deps, em_rope)
            deps = [M("rope"), "c2"] + ([M("ev", t - 1)] if t >= 1 else [f"{bb}qr_ev@{NTB-1}"])
            T("tensor", M("tr"), deps,
              lambda t_: t_.transpose(ps_tr[0:32, 0:128], krb[:, :], ident[:, :]))
            def em_krev(s, t=t):
                s.copy(krT[0:32, 128 * t:128 * (t + 1)], ps_tr[0:32, 0:128])
                return s.copy(krT[64:96, 128 * t:128 * (t + 1)], ps_tr[0:32, 0:128])
            T("scalar", M("ev"), [M("tr"), bb + "krz"], em_krev)
        P.flush()
        sWkr.close()
        sWku = contextlib.ExitStack()
        Wku = sWku.enter_context(nc.sbuf_tensor(pre + bb + "Wku", [128, 2 * 1024], BF16)).ap()
        T("gpsimd", bb + "wku", [], lambda g: dma_T(g, Wku, w["Wku"], 1024), dma=True)
        uq = 0
        for ot in range(8):
            for hf in range(NHF2):
                M = lambda tag, o=ot, f_=hf: f"{bb}kc_{tag}@{o}_{f_}"
                pz = uq % 2
                deps = [lat_last, bb + "wku"]
                if uq >= 2:
                    deps.append(f"{bb}kc_ev@{(uq-2)//NHF2}_{(uq-2)%NHF2}")
                else:
                    deps.append(qc_last)
                def em_mm(t_, ot=ot, hf=hf, pz=pz):
                    last = None
                    for qc in range(hf * NQH, (hf + 1) * NQH):
                        for lt in range(2):
                            last = t_.matmul(
                                ps_qk[pz][:, 512 * (qc - hf * NQH):512 * (qc - hf * NQH + 1)],
                                Wku[:, 1024 * lt + 128 * ot:1024 * lt + 128 * (ot + 1)],
                                latT[:, S * lt + 512 * qc:S * lt + 512 * (qc + 1)],
                                start=(lt == 0), stop=(lt == 1))
                    return last
                T("tensor", M("mm"), deps, em_mm)
                T("scalar", M("ev"), [M("mm")],
                  lambda s, ot=ot, hf=hf, pz=pz: s.copy(
                      kcT[:, S * ot + 512 * NQH * hf:S * ot + 512 * NQH * (hf + 1)],
                      ps_qk[pz][:, :]))
                uq += 1
        kc_last = f"{bb}kc_ev@7_{NHF2-1}"
        P.flush()
        sWku.close()
        sWvu = contextlib.ExitStack()
        Wvu = sWvu.enter_context(nc.sbuf_tensor(pre + bb + "Wvu", [128, 2 * 1024], BF16)).ap()
        T("gpsimd", bb + "wvu", [], lambda g: dma_T(g, Wvu, w["Wvu"], 1024), dma=True)
        for tt in range(NTB):
            M = lambda tag, t=tt: f"{bb}v_{tag}@{t}"
            deps = [lat_last, bb + "wvu", bb + "vset"] + \
                   ([M("ev", tt - 1)] if tt >= 1 else [f"{bb}qr_rope@{NTB-1}"])
            def em_mm(t_, tt=tt):
                last = None
                for oc in range(2):
                    for lt in range(2):
                        last = t_.matmul(ps_tm[:, 512 * oc:512 * (oc + 1)],
                                         latT[:, S * lt + 128 * tt:S * lt + 128 * (tt + 1)],
                                         Wvu[:, 1024 * lt + 512 * oc:1024 * lt + 512 * (oc + 1)],
                                         start=(lt == 0), stop=(lt == 1))
                return last
            T("tensor", M("mm"), deps, em_mm)
            T("scalar", M("ev"), [M("mm")],
              lambda s, tt=tt: s.copy(Vb4[:, tt, :, 0:64],
                                      ps_tm.rearrange("p (g dd) -> p g dd", dd=64)))
        P.flush()
        sWvu.close()
        sB.close()
        if cfg.get("p3_stop") == "B":
            xi.close()
            tes.close()
            return

        sDE = contextlib.ExitStack()
        ctxT = sDE.enter_context(nc.sbuf_tensor(pre + bb + "ctxT", [128, 8 * S], BF16,
                                                side="right")).ap()
        sD = contextlib.ExitStack()
        aD = lambda n, sh, dt: sD.enter_context(nc.sbuf_tensor(pre + bb + n, sh, dt)).ap()
        negmS = aD("negmS", [128, 4 * 512], F32)
        maskT = aD("maskT", [128, 4 * 512], BF16)
        mrow = aD("mrow", [1, S], BF16)
        drow = aD("drow", [1, S], F32)
        rrow = aD("rrow", [1, S], F32)
        mcols = aD("mcols", [128, NK], F32)
        Pb = [aD(f"Pb{i}", [128, S], BF16) for i in range(2)]
        ps_X = sD.enter_context(nc.psum_tensor(pre + bb + "psX", [128, S], F32)).ap()
        ps_Y = sD.enter_context(nc.psum_tensor(pre + bb + "psY", [128, S], F32)).ap()
        T("gpsimd", bb + "mload", [], lambda g: g.dma_start(negmS[:, :], w["negmS"][:, :]), dma=True)
        T("gpsimd", bb + "mload2", [], lambda g: g.dma_start(maskT[:, :], w["maskT"][:, :]), dma=True)
        NHF = 2 if NQ >= 2 else 1
        HCW = NQ // NHF

        def p3_tail_rbnorm(hprev):
            hbp = f"{bb}h{hprev}_"
            r0p = (hprev % 2) * 64
            cofp = (hprev // 2) * S
            def em_rb(t_):
                last = None
                for qc in range(NQ):
                    last = t_.matmul(ps_Y[0:64, 512 * qc:512 * (qc + 1)], ones_f[:, 0:64],
                                     rrow[:, 512 * qc:512 * (qc + 1)], start=True, stop=True)
                return last
            T("tensor", hbp + "rb", [hbp + "den", hbp + "cev", "c1"], em_rb)
            T("vector", hbp + "norm", [hbp + "rb"],
              lambda v, r0p=r0p, cofp=cofp: v.tensor_tensor(
                  ctxT[r0p:r0p + 64, cofp:cofp + S],
                  ctxT[r0p:r0p + 64, cofp:cofp + S], ps_Y[0:64, :], ALU.mult))

        for h in range(16):
            hb = f"{bb}h{h}_"
            r0 = (h % 2) * 64
            cof = (h // 2) * S
            qt_list = list(range(NK - 1, -1, -1))
            for pi_, qt in enumerate(qt_list):
                M = lambda tag, q=qt: f"{hb}m_{tag}@{q}"
                kmax = qt // 4
                psZ = ps_X if pi_ % 2 == 0 else ps_Y
                deps = [kc_last, qc_last, f"{bb}qr_ev@{NTB-1}",
                        f"{bb}kr_ev@{NTB-1}", bb + "mload"]
                if pi_ >= 2:
                    deps.append(f"{hb}m_red@{qt_list[pi_ - 2]}")
                elif h >= 1:
                    deps.append(f"{bb}h{h-1}_" + ("lastexp" if pi_ % 2 == 0 else "norm"))
                def em_smm(t_, qt=qt, kmax=kmax, r0=r0, cof=cof, psZ=psZ):
                    last = None
                    for kc in range(kmax + 1):
                        t_.matmul(psZ[:, 512 * kc:512 * (kc + 1)],
                                  qcT[r0:r0 + 64, cof + 128 * qt:cof + 128 * (qt + 1)],
                                  kcT[r0:r0 + 64, cof + 512 * kc:cof + 512 * (kc + 1)],
                                  start=True, stop=False)
                        last = t_.matmul(psZ[:, 512 * kc:512 * (kc + 1)],
                                         qrT[r0:r0 + 64, cof + 128 * qt:cof + 128 * (qt + 1)],
                                         krT[r0:r0 + 64, 512 * kc:512 * (kc + 1)],
                                         start=False, stop=True)
                    return last
                T("tensor", M("smm"), deps, em_smm)
                def em_red(v, qt=qt, kmax=kmax, psZ=psZ):
                    C = lambda ins: P.ch("vector", v, ins)
                    C(v.tensor_tensor(Pb[0][:, 0:512], psZ[:, 512 * kmax:512 * (kmax + 1)],
                                      negmS[:, 512 * (qt % 4):512 * (qt % 4 + 1)], ALU.add))
                    C(v.reduce_max(mcols[:, qt:qt + 1], Pb[0][:, 0:512], axis=AX.X))
                    if kmax >= 1:
                        C(v.reduce_max(mcols[:, NK - 1:NK] if False else Pb[0][0:1, 0:1], psZ[:, 0:1], axis=AX.X))
                    return v.tensor_scalar(mcols[:, qt:qt + 1], mcols[:, qt:qt + 1],
                                           -1.0, None, ALU.mult)
                def em_red2(v, qt=qt, kmax=kmax, psZ=psZ):
                    C = lambda ins: P.ch("vector", v, ins)
                    C(v.tensor_tensor(Pb[0][:, 0:512], psZ[:, 512 * kmax:512 * (kmax + 1)],
                                      negmS[:, 512 * (qt % 4):512 * (qt % 4 + 1)], ALU.add))
                    C(v.reduce_max(mcols[:, qt:qt + 1], Pb[0][:, 0:512], axis=AX.X))
                    if kmax >= 1:
                        C(v.reduce_max(Pb[0].bitcast(F32)[:, 0:1] if False else drow[0:1, 0:1],
                                       psZ[0:1, 0:1], axis=AX.X))
                    return v.tensor_scalar(mcols[:, qt:qt + 1], mcols[:, qt:qt + 1],
                                           -1.0, None, ALU.mult)
                def em_red3(v, qt=qt, kmax=kmax, psZ=psZ):
                    C = lambda ins: P.ch("vector", v, ins)
                    C(v.tensor_tensor(Pb[0][:, 0:512], psZ[:, 512 * kmax:512 * (kmax + 1)],
                                      negmS[:, 512 * (qt % 4):512 * (qt % 4 + 1)], ALU.add))
                    C(v.reduce_max(mcols[:, qt:qt + 1], Pb[0][:, 0:512], axis=AX.X))
                    if kmax >= 1:
                        C(v.reduce_max(Pb[1][:, 0:1], psZ[:, 0:512 * kmax], axis=AX.X))
                        C(v.tensor_tensor(mcols[:, qt:qt + 1], mcols[:, qt:qt + 1],
                                          Pb[1][:, 0:1], ALU.max))
                    return v.tensor_scalar(mcols[:, qt:qt + 1], mcols[:, qt:qt + 1],
                                           -1.0, None, ALU.mult)
                T("vector", M("red"), [M("smm")], em_red3)
                if pi_ == 0 and h >= 1:
                    p3_tail_rbnorm(h - 1)
            def em_mtrs(t_):
                last = None
                for qt in range(NK):
                    last = t_.transpose(ps_X[0:1, 128 * qt:128 * (qt + 1)],
                                        mcols[:, qt:qt + 1], identf[:, :])
                return last
            T("tensor", hb + "mtrs", [f"{hb}m_red@{qt_list[-1]}", f"{hb}m_red@{qt_list[-2]}", "c3"],
              em_mtrs)
            T("vector", hb + "mrowev", [hb + "mtrs"],
              lambda v: v.tensor_copy(mrow[:, :], ps_X[0:1, :]))
            units = []
            for kt in range(NK):
                qs = kt // 4
                for hf in range(NHF):
                    qcs = [qc for qc in range(hf * HCW, (hf + 1) * HCW) if qc >= qs]
                    if qcs:
                        units.append((kt, hf, qcs))

            def emit_s2(kt, hf, qcs):
                M = lambda tag: f"{hb}{tag}@{kt}_{hf}"
                qs = kt // 4
                deps = [hb + "mrowev", "c0"]
                prev_same_half = [(k2, h2) for (k2, h2, _) in units
                                  if h2 == hf and k2 < kt]
                if prev_same_half:
                    deps.append(f"{hb}exp@{prev_same_half[-1][0]}_{hf}")
                def em(t_, kt=kt, qcs=qcs, r0=r0, cof=cof):
                    last = None
                    for qc in qcs:
                        t_.matmul(ps_X[:, 512 * qc:512 * (qc + 1)],
                                  kcT[r0:r0 + 64, cof + 128 * kt:cof + 128 * (kt + 1)],
                                  qcT[r0:r0 + 64, cof + 512 * qc:cof + 512 * (qc + 1)],
                                  start=True, stop=False)
                        t_.matmul(ps_X[:, 512 * qc:512 * (qc + 1)],
                                  krT[r0:r0 + 64, 128 * kt:128 * (kt + 1)],
                                  qrT[r0:r0 + 64, cof + 512 * qc:cof + 512 * (qc + 1)],
                                  start=False, stop=False)
                        last = t_.matmul(ps_X[:, 512 * qc:512 * (qc + 1)],
                                         ones_bf[:, :], mrow[:, 512 * qc:512 * (qc + 1)],
                                         start=False, stop=True)
                    return last
                T("tensor", M("s2"), deps, em)
                deps = [M("s2")]
                prev2 = [(k2, h2) for (k2, h2, _) in units if h2 == hf and k2 < kt]
                if len(prev2) >= 2:
                    deps.append(f"{hb}ctx@{prev2[-2][0]}_{hf}")
                elif h >= 1:
                    deps.append(f"{bb}h{h-1}_lastctx@{hf}")
                def em_exp(s, kt=kt, qcs=qcs):
                    last = None
                    for qc in qcs:
                        last = s.activation(Pb[kt % 2][:, 512 * qc:512 * (qc + 1)],
                                            ps_X[:, 512 * qc:512 * (qc + 1)], AF.Exp, scale=SCL)
                    return last
                T("scalar", M("exp"), deps, em_exp)
                ddeps = [M("exp"), bb + "mload2"]
                if qs in qcs:
                    T("vector", M("dz"), ddeps,
                      lambda v, kt=kt, qs=qs: v.tensor_tensor(
                          Pb[kt % 2][:, 512 * qs:512 * (qs + 1)],
                          Pb[kt % 2][:, 512 * qs:512 * (qs + 1)],
                          maskT[:, 512 * (kt % 4):512 * (kt % 4 + 1)], ALU.mult))

            def emit_ctx(kt, hf, qcs):
                M = lambda tag: f"{hb}{tag}@{kt}_{hf}"
                qs = kt // 4
                deps = [M("exp")]
                if qs in qcs:
                    deps.append(M("dz"))
                if kt == 0 and h >= 1:
                    deps.append(f"{bb}h{h-1}_norm")
                def em(t_, kt=kt, qcs=qcs, h=h):
                    last = None
                    for qc in qcs:
                        last = t_.matmul(ps_Y[0:65, 512 * qc:512 * (qc + 1)],
                                         Vb[:, 1040 * kt + 65 * h:1040 * kt + 65 * (h + 1)],
                                         Pb[kt % 2][:, 512 * qc:512 * (qc + 1)],
                                         start=(kt == 0), stop=(kt == 4 * qc + 3))
                    return last
                T("tensor", M("ctx"), deps, em)

            for ui, u in enumerate(units):
                emit_s2(*u)
                if ui >= 1:
                    emit_ctx(*units[ui - 1])
            emit_ctx(*units[-1])
            AL(hb + "lastctx@0",
               [f"{hb}ctx@{units[-2][0]}_{units[-2][1]}" if units[-1][1] != 0 else
                f"{hb}ctx@{units[-1][0]}_{units[-1][1]}"])
            AL(hb + "lastctx@1", [f"{hb}ctx@{units[-1][0]}_{units[-1][1]}"])
            AL(hb + "lastexp",
               [f"{hb}exp@{units[-1][0]}_{units[-1][1]}",
                f"{hb}exp@{units[-2][0]}_{units[-2][1]}"])
            T("vector", hb + "cev", [f"{hb}ctx@{units[-1][0]}_{units[-1][1]}"],
              lambda v, r0=r0, cof=cof: v.tensor_copy(ctxT[r0:r0 + 64, cof:cof + S],
                                                      ps_Y[0:64, :]))
            def em_den(s, h=h):
                P.ch("scalar", s, s.activation(drow[:, :], ps_Y[64:65, :], AF.Ln,
                                               bias=Crow[0:1, h:h + 1]))
                return s.activation(rrow[:, :], drow[:, :], AF.Exp, scale=-1.0)
            T("scalar", hb + "den", [f"{hb}ctx@{units[-1][0]}_{units[-1][1]}", "cexp"], em_den)
            def em_fill(t_):
                last = None
                for _ in range(14):
                    last = t_.matmul(ps_X[:, 0:512], ident[:, :], Pb[0][:, 0:512],
                                     start=True, stop=True)
                return last
            T("tensor", hb + "fill", [f"{hb}ctx@{units[-1][0]}_{units[-1][1]}"], em_fill)
        p3_tail_rbnorm(15)
        P.flush()
        sD.close()
        xi.close()
        if cfg.get("p3_stop") == "D":
            sDE.close()
            tes.close()
            return
        sE = contextlib.ExitStack()
        aE = lambda n, sh, dt: sE.enter_context(nc.sbuf_tensor(pre + bb + n, sh, dt)).ap()
        WoT = aE("WoT", [128, 8 * 1024], BF16)
        xin = [aE(f"ex{i}", [128, 1024], F32) for i in range(NB)]
        xo = [aE(f"exo{i}", [128, 1024], F32) for i in range(NB)]
        scr = [aE(f"escr{i}", [128, 1024], F32) for i in range(NB)]
        st = [aE(f"est{i}", [128, 8], F32) for i in range(NB)]
        ps_o = [sE.enter_context(nc.psum_tensor(pre + bb + f"pso{i}", [128, 1024], F32)).ap()
                for i in range(2)]
        ps_fil = sE.enter_context(nc.psum_tensor(pre + bb + "epsfil", [128, 128], F32)).ap()
        T("gpsimd", bb + "wo", [], lambda g: dma_T(g, WoT, w["Wo"], 1024), dma=True)
        lastD = f"{bb}h15_norm"
        for t in range(NTB):
            i = t % NB
            j = t % 2
            M = lambda tag, tt=t: f"{bb}e_{tag}@{tt}"
            deps = [M("xst", t - NB)] if t >= NB else []
            T("sync", M("xin"), deps,
              lambda sy, i=i, t=t, b=b: sy.dma_start(xin[i][:, :], src_t[b, t, :, :]), dma=True)
            deps = [lastD, bb + "wo"] + ([M("res", t - 2)] if t >= 2 else [])
            def em_omm(t_, t=t, j=j):
                last = None
                for oc in range(2):
                    for d in range(8):
                        last = t_.matmul(ps_o[j][:, 512 * oc:512 * (oc + 1)],
                                         ctxT[:, S * d + 128 * t:S * d + 128 * (t + 1)],
                                         WoT[:, 1024 * d + 512 * oc:1024 * d + 512 * (oc + 1)],
                                         start=(d == 0), stop=(d == 7))
                return last
            T("tensor", M("omm"), deps, em_omm)
            if t < NTB - 1:
                def em_efil(t_):
                    last = None
                    for _ in range(16):
                        last = t_.matmul(ps_fil[:, :], ident[:, :], ident[:, :],
                                         start=True, stop=True)
                    return last
                T("tensor", M("fil"), ["c2"], em_efil)
            def em_res(v, i=i, j=j):
                C = lambda ins: P.ch("vector", v, ins)
                C(v.tensor_tensor(scr[i][:, :], xin[i][:, :], ps_o[j][:, :], ALU.add))
                C(v.tensor_tensor(xo[i][:, :], scr[i][:, :], scr[i][:, :], ALU.mult))
                C(v.reduce_sum(st[i][:, 1:2], xo[i][:, :], axis=AX.X))
                C(v.reduce_sum(st[i][:, 0:1], scr[i][:, :], axis=AX.X))
                C(v.tensor_scalar(st[i][:, 0:1], st[i][:, 0:1], 1.0 / 1024.0, None, ALU.mult))
                C(v.tensor_scalar(st[i][:, 2:3], st[i][:, 1:2], 1.0 / 1024.0, None, ALU.mult))
                C(v.tensor_scalar(st[i][:, 3:4], st[i][:, 0:1], st[i][:, 0:1], None, ALU.mult))
                return v.tensor_tensor(st[i][:, 4:5], st[i][:, 2:3], st[i][:, 3:4], ALU.subtract)
            T("vector", M("res"), [M("omm"), M("xin")], em_res)
            def em_rstd(s, i=i):
                P.ch("scalar", s, s.activation(st[i][:, 5:6], st[i][:, 4:5], AF.Ln, bias=epscol[:, :]))
                return s.activation(st[i][:, 6:7], st[i][:, 5:6], AF.Exp, scale=-0.5)
            T("scalar", M("rstd"), [M("res"), "c4"], em_rstd)
            def em_lno(v, i=i):
                P.ch("vector", v, v.tensor_scalar(st[i][:, 7:8], st[i][:, 0:1], st[i][:, 6:7],
                                                  -1.0, ALU.mult, ALU.mult))
                return v.tensor_scalar(xo[i][:, :], scr[i][:, :], st[i][:, 6:7], st[i][:, 7:8],
                                       ALU.mult, ALU.add)
            T("vector", M("lno"), [M("rstd")], em_lno)
            T("sync", M("xst"), [M("lno")],
              lambda sy, i=i, t=t, b=b: sy.dma_start(dst_t[b, t, :, :], xo[i][:, :]), dma=True)
        T("sync", bb + "fin", [f"{bb}e_xst@{t}" for t in range(max(0, NTB - NB), NTB)],
          lambda sy: sy.nop())
        P.flush()
        sE.close()
        sDE.close()
    tes.close()


def host_prep_p3(inp):
    w = {}
    w["Wq"] = inp["L1_attn_Wq"].astype(np.float32)
    Wqr = inp["L1_attn_Wqr"].astype(np.float32)
    pad = np.zeros((1024, 1024), np.float32)
    for h in range(16):
        pad[:, 64 * h:64 * h + 32] = Wqr[:, 32 * h:32 * (h + 1)]
    w["Wqr_pad"] = pad
    w["Wkr"] = inp["L1_attn_Wkr"].astype(np.float32)
    w["Wkvd"] = inp["L1_attn_Wkvd"].astype(np.float32)
    w["Wku"] = inp["L1_attn_Wku"].astype(np.float32)
    w["Wvu"] = inp["L1_attn_Wvu"].astype(np.float32)
    w["Wo"] = inp["L1_attn_Wo"].astype(np.float32)
    w["lobo_log"] = inp["L1_attn_lobo_log"].astype(np.float32)
    return w


def host_rope_consts(S):
    half = 16
    inv = 1.0 / 10000.0 ** (np.arange(half, dtype=np.float64) / half)
    ang = np.arange(S, dtype=np.float64)[:, None] * inv[None, :]
    cos = np.cos(ang).astype(np.float32)
    sin = np.sin(ang).astype(np.float32)
    NTB = S // 128
    c = {}
    c["cos16"] = cos.reshape(NTB, 128, 16).transpose(1, 0, 2).reshape(128, NTB * 16).copy()
    c["sin16"] = sin.reshape(NTB, 128, 16).transpose(1, 0, 2).reshape(128, NTB * 16).copy()
    cr = np.tile(cos, (1, 16)).reshape(NTB, 128, 256).transpose(1, 0, 2).reshape(128, NTB * 256)
    sr = np.tile(sin, (1, 16)).reshape(NTB, 128, 256).transpose(1, 0, 2).reshape(128, NTB * 256)
    c["cosrep"] = cr.copy()
    c["sinrep"] = sr.copy()
    negm = np.zeros((128, 4 * 512), np.float32)
    mt = np.zeros((128, 4 * 512), np.float32)
    for m in range(4):
        p = np.arange(128)[:, None]
        j = np.arange(512)[None, :]
        valid_s = j <= (m * 128 + p)
        negm[:, 512 * m:512 * (m + 1)] = np.where(valid_s, 0.0, -BIG)
        valid_t = j >= (m * 128 + p)
        mt[:, 512 * m:512 * (m + 1)] = np.where(valid_t, 1.0, 0.0)
    c["negmS"] = negm
    c["maskT"] = mt.astype(ml_dtypes.bfloat16)
    return c


def np_ref_p3(x, inp):
    xf = x.astype(np.float64)
    B, S, _ = xf.shape
    H, DH_, DR_ = 16, 64, 32

    def rope(z):
        d = z.shape[-1]
        t = z.shape[-2]
        half = d // 2
        inv = 1.0 / 10000.0 ** (np.arange(half) / half)
        ang = np.arange(t)[:, None] * inv[None, :]
        c_, s_ = np.cos(ang), np.sin(ang)
        z1, z2 = z[..., :half], z[..., half:]
        return np.concatenate([z1 * c_ - z2 * s_, z2 * c_ + z1 * s_], axis=-1)

    def sh(y, d):
        return y.reshape(B, S, H, d).transpose(0, 2, 1, 3)

    q_c = sh(xf @ inp["L1_attn_Wq"].astype(np.float64), DH_)
    q_r = rope(sh(xf @ inp["L1_attn_Wqr"].astype(np.float64), DR_))
    lat = xf @ inp["L1_attn_Wkvd"].astype(np.float64)
    k_c = sh(lat @ inp["L1_attn_Wku"].astype(np.float64), DH_)
    v = sh(lat @ inp["L1_attn_Wvu"].astype(np.float64), DH_)
    k_r = rope(xf @ inp["L1_attn_Wkr"].astype(np.float64))
    k_r = np.broadcast_to(k_r[:, None], (B, H, S, DR_))
    q = np.concatenate([q_c, q_r], -1)
    k = np.concatenate([k_c, k_r], -1)
    sc = np.einsum("bhqd,bhkd->bhqk", q, k) / np.sqrt(96.0)
    causal = np.tril(np.ones((S, S), bool))
    sc = np.where(causal[None, None], sc, -np.inf)
    m = sc.max(-1, keepdims=True)
    e = np.exp(sc - m)
    den = e.sum(-1, keepdims=True)
    Cv = np.exp(inp["L1_attn_lobo_log"].astype(np.float64)).reshape(1, H, 1, 1)
    attn = e / (den + Cv)
    y = np.einsum("bhqk,bhkd->bhqd", attn, v)
    y = y.transpose(0, 2, 1, 3).reshape(B, S, H * DH_)
    out = y @ inp["L1_attn_Wo"].astype(np.float64)
    t = xf + out
    mu = t.mean(-1, keepdims=True)
    var = t.var(-1, keepdims=True)
    return ((t - mu) / np.sqrt(var + EPS)).astype(np.float32)


def host_consts():
    c = {}
    c["ident"] = np.eye(128, dtype=ml_dtypes.bfloat16)
    c["ident_f"] = np.eye(128, dtype=np.float32)
    c["ones_row"] = np.ones((1, 128), dtype=ml_dtypes.bfloat16)
    c["ones_row_f"] = np.ones((1, 128), dtype=np.float32)
    c["epscol"] = np.full((128, 1), EPS, np.float32)
    return c


def host_prep_p2(inp):
    w = {}
    w["baseWT"] = np.ascontiguousarray(inp["L0_ffn_base_W"].T).astype(np.float32)
    A = np.zeros((1024, 112), np.float32)
    A[:, 0:16] = inp["L0_ffn_shared_A"].T
    for e in range(6):
        A[:, 16 * (e + 1):16 * (e + 2)] = inp["L0_ffn_routed_A"][e].T
    w["AallT"] = A
    Bm = np.zeros((112, 1024), np.float32)
    Bm[0:16, :] = inp["L0_ffn_shared_B"].T
    for e in range(6):
        Bm[16 * (e + 1):16 * (e + 2), :] = inp["L0_ffn_routed_B"][e].T
    w["Ball"] = Bm
    w["routerWT"] = np.ascontiguousarray(inp["L0_ffn_router_W"].T).astype(np.float32)
    w["base_b"] = inp["L0_ffn_base_b"].astype(np.float32)
    w["router_b"] = inp["L0_ffn_router_b"].astype(np.float32)
    return w


def np_ref_p2(x, inp):
    xf = x.astype(np.float64)
    base = xf @ inp["L0_ffn_base_W"].T.astype(np.float64) + inp["L0_ffn_base_b"]
    shared = (xf @ inp["L0_ffn_shared_A"].T.astype(np.float64)) @ inp["L0_ffn_shared_B"].T.astype(np.float64)
    logits = xf @ inp["L0_ffn_router_W"].T.astype(np.float64) + inp["L0_ffn_router_b"]
    e = np.exp(logits - logits.max(-1, keepdims=True))
    g = e / e.sum(-1, keepdims=True)
    idx = np.argsort(-g, axis=-1, kind="stable")[..., :2]
    sw = np.zeros_like(g)
    np.put_along_axis(sw, idx, np.take_along_axis(g, idx, -1), -1)
    sw = sw / (sw.sum(-1, keepdims=True) + 1e-6)
    h = np.einsum("...i,eri->...er", xf, inp["L0_ffn_routed_A"].astype(np.float64))
    ro = np.einsum("...er,eor->...eo", h, inp["L0_ffn_routed_B"].astype(np.float64))
    routed = np.einsum("...eo,...e->...o", ro, sw)
    t = xf + base + shared + routed
    mu = t.mean(-1, keepdims=True)
    var = t.var(-1, keepdims=True)
    return ((t - mu) / np.sqrt(var + EPS)).astype(np.float32)


FUSED = True
N_CORES = 8
B_CORE = 2
S_FULL = 2048


def _declare_inputs(nc, wnp):
    aps, hin = {}, {}
    for k, v in wnp.items():
        dt = BF16 if v.dtype == ml_dtypes.bfloat16 else F32
        aps[k] = nc.dram_tensor("w_" + k, list(v.shape), dt, kind="ExternalInput").ap()
        hin["w_" + k] = np.ascontiguousarray(v)
    return aps, hin


def kernel(**inputs):
    x = np.ascontiguousarray(inputs["L0_attn_hidden_states"].astype(np.float32, copy=False))
    shards = [np.ascontiguousarray(x[2 * c:2 * c + 2]) for c in range(N_CORES)]
    cfg = {"B": B_CORE, "S": S_FULL, "HID": 2816}

    consts = host_consts()
    rope = host_rope_consts(S_FULL)
    w_all = [host_prep_p1(inputs), host_prep_p2(inputs),
             {**host_prep_p3(inputs), **rope}, host_prep_p4(inputs)]
    builders = [build_p1, build_p2, build_p3, build_p4]

    if not FUSED:
        cur = shards
        for pi, (bld, wnp) in enumerate(zip(builders, w_all)):
            wnp = {**wnp, **consts}
            nc = bass.Bass("TRN2", target_bir_lowering=False)
            w_aps, hin = _declare_inputs(nc, wnp)
            src = nc.dram_tensor("xsrc", [B_CORE, S_FULL, 1024], F32, kind="ExternalInput").ap()
            dst = nc.dram_tensor("xdst", [B_CORE, S_FULL, 1024], F32, kind="ExternalOutput").ap()
            with contextlib.ExitStack() as es:
                P = Prog(nc, es, "mk")
                bld(nc, P, cfg, src, dst, w_aps, pre=f"p{pi+1}_")
            in_maps = [{**hin, "xsrc": cur[c]} for c in range(N_CORES)]
            res = run_bass_kernel_spmd(nc, in_maps, core_ids=list(range(N_CORES)))
            cur = [np.asarray(res.results[c]["xdst"], dtype=np.float32) for c in range(N_CORES)]
        return np.concatenate(cur, axis=0)

    nc = bass.Bass("TRN2", target_bir_lowering=False)
    wnp_merged = {}
    for wnp in w_all:
        wnp_merged.update(wnp)
    wnp_merged.update(consts)
    w_aps, hin = _declare_inputs(nc, wnp_merged)
    src = nc.dram_tensor("xsrc", [B_CORE, S_FULL, 1024], F32, kind="ExternalInput").ap()
    dst = nc.dram_tensor("xdst", [B_CORE, S_FULL, 1024], F32, kind="ExternalOutput").ap()
    xh = [nc.dram_tensor(f"xh{i}", [B_CORE, S_FULL, 1024], F32, kind="Internal").ap()
          for i in range(3)]
    hops = [(src, xh[0]), (xh[0], xh[1]), (xh[1], xh[2]), (xh[2], dst)]
    with contextlib.ExitStack() as es:
        P = Prog(nc, es, "mk")
        for pi, (bld, (s_, d_)) in enumerate(zip(builders, hops)):
            bld(nc, P, cfg, s_, d_, w_aps, pre=f"p{pi+1}_")
    in_maps = [{**hin, "xsrc": shards[c]} for c in range(N_CORES)]
    res = run_bass_kernel_spmd(nc, in_maps, core_ids=list(range(N_CORES)))
    return np.concatenate([np.asarray(res.results[c]["xdst"], dtype=np.float32)
                           for c in range(N_CORES)], axis=0)
